# version 10
# baseline (speedup 1.0000x reference)
import contextlib
import numpy as np
import ml_dtypes
import concourse.bass as bass
import concourse.mybir as mybir
from concourse.bass_utils import run_bass_kernel_spmd

F32 = mybir.dt.float32
BF16 = mybir.dt.bfloat16
AF = mybir.ActivationFunctionType
ALU = mybir.AluOpType
AX = mybir.AxisListType
BF = ml_dtypes.bfloat16

N_CORES = 8
D = 2048
T = 8192
DC = D // 128
SEG = 512
NSEG = T // SEG
DK = 128
HPC = 2
CSHIFT = 20.0
ISQRT_DK = float(1.0 / np.sqrt(np.float32(DK)))
TPC = T // N_CORES


DRAIN_DVE = True
DRAIN_ACT = False


class Drained:
    _PASSTHRU = {"wait_ge", "drain", "nop", "wait_op"}

    def __init__(self, eng):
        self._e = eng

    def __getattr__(self, name):
        f = getattr(self._e, name)
        en = DRAIN_ACT if type(self._e).__name__.endswith("ScalarEngine") else DRAIN_DVE
        if name in Drained._PASSTHRU or not callable(f) or not en:
            return f
        def wrapped(*a, **k):
            inst = f(*a, **k)
            self._e.drain()
            return inst
        return wrapped


class Sem:

    def __init__(self, ctx, nc, name):
        self.h = ctx.enter_context(nc.semaphore(name))
        self.n = 0

    def inc(self, inst, k=1):
        inst.then_inc(self.h, k)
        self.n += k
        return self.n

    def wait(self, eng, n=None):
        eng.wait_ge(self.h, self.n if n is None else n)



class Env:

    def __init__(self, nc, block, ctx):
        self.nc, self.block, self.ctx = nc, block, ctx
        self.VEC = lambda f: block.vector(lambda v: f(Drained(v)))
        self.SCA = lambda f: block.scalar(lambda a: f(Drained(a)))

    def sem(self, name):
        return Sem(self.ctx, self.nc, name)


def _standalone(emit, io_builder):
    nc = bass.Bass("TRN2", target_bir_lowering=False, debug=False)
    ctx = contextlib.ExitStack()
    with ctx:
        io = io_builder(nc)
        block = ctx.enter_context(nc.Block())
        E = Env(nc, block, ctx)
        exits = emit(E, io, [])
        osem, oval = exits["out"]
        def fin_s(s):
            s.wait_ge(osem, oval)
        block.sync(fin_s)
    return nc


def build_p0(nseg=NSEG):
    def io_builder(nc):
        return dict(
            x0s=nc.dram_tensor("x0s", [128, NSEG * DC * SEG], F32, kind="ExternalInput").ap(),
            wq=nc.dram_tensor("wq", [128, DC * HPC * DK], F32, kind="ExternalInput").ap(),
            wk=nc.dram_tensor("wk", [128, DC * HPC * DK], F32, kind="ExternalInput").ap(),
            wv=nc.dram_tensor("wv", [128, DC * HPC * DK], F32, kind="ExternalInput").ap(),
            betas=nc.dram_tensor("betas", [1, HPC * DK], F32, kind="ExternalInput").ap(),
            ident=nc.dram_tensor("ident", [128, 128], BF16, kind="ExternalInput").ap(),
            binmask=nc.dram_tensor("binmask", [SEG, SEG], BF16, kind="ExternalInput").ap(),
            negc=nc.dram_tensor("negc", [128, 1], F32, kind="ExternalInput").ap(),
            att0=nc.dram_tensor("att0", [HPC * DK, T], BF16, kind="ExternalOutput").ap(),
        )
    return _standalone(lambda E, io, entry: emit_p0(E, io, entry, nseg=nseg), io_builder)


def emit_p0(E, io, entry, nseg=NSEG, tag="p0"):
    nc, block, ctx = E.nc, E.block, E.ctx
    x0s, wq, wk, wv = io["x0s"], io["wq"], io["wk"], io["wv"]
    betas, ident_in, binmask_in, negc_in = io["betas"], io["ident"], io["binmask"], io["negc"]
    att0 = io["att0"]
    with contextlib.ExitStack() as pctx:
        sb = lambda name, shape, dt: pctx.enter_context(nc.sbuf_tensor(tag + name, shape, dt))
        ps = lambda name: pctx.enter_context(nc.psum_tensor(tag + name, [128, 512], F32))

        wq_sb = sb("wq_sb", [128, DC * HPC * DK], BF16)
        wk_sb = sb("wk_sb", [128, DC * HPC * DK], BF16)
        wv_sb = sb("wv_sb", [128, DC * HPC * DK], BF16)
        xseg = [sb(f"xseg{i}", [128, DC * SEG], BF16) for i in range(2)]
        ident = sb("ident_sb", [128, 128], BF16)
        binmask = sb("binmask_sb", [128, 4 * SEG], BF16)
        negc = sb("negc_sb", [128, 1], F32)
        ones_f = sb("ones_f", [1, 128], F32)
        beta_f = sb("beta_f", [1, HPC * DK], F32)
        bcast = sb("bcast", [128, HPC * DK], F32)
        mem_f = sb("mem_f", [128, HPC * DK], F32)
        z_f = sb("z_f", [128, HPC], F32)
        memz = sb("memz", [128, HPC * 129], BF16)
        q_sb = sb("q_sb", [128, SEG], BF16)
        k_sb = sb("k_sb", [128, SEG], BF16)
        sq_sb = sb("sq_sb", [128, SEG], BF16)
        sk_sb = sb("sk_sb", [128, SEG], BF16)
        sk_tm = sb("sk_tm", [128, SEG], BF16)
        v_aug = sb("v_aug", [128, 4 * 129], BF16)
        pst = sb("pst", [128, 4 * SEG], BF16)
        ndelta = sb("ndelta", [128, SEG], BF16)
        tq1 = sb("tq1", [128, SEG], F32)
        tq2 = sb("tq2", [128, SEG], F32)
        tk1 = sb("tk1", [128, SEG], F32)
        tk2 = sb("tk2", [128, SEG], F32)
        att_d = sb("att_d", [128, 128], F32)
        att_m = sb("att_m", [128, 128], F32)
        cmb = sb("cmb", [128, 128], F32)
        att_bf = sb("att_bf", [128, 128], BF16)
        att_tile = sb("att_tile", [128, SEG], BF16)
        rc = sb("rc", [128, 4], F32)
        zadd = sb("zadd", [128, 1], F32)

        p_q = ps("p_q")
        p_k = ps("p_k")
        p_v = ps("p_v")
        p_st = ps("p_st")
        p_ad = ps("p_ad")
        p_am = ps("p_am")
        p_tr = ps("p_tr")
        p_mu = ps("p_mu")

        sW = E.sem(tag + "sW")
        sW2 = E.sem(tag + "sW2")
        sX = [E.sem(tag + "sX0"), E.sem(tag + "sX1")]
        sP = E.sem(tag + "sP")
        sA = E.sem(tag + "sA")
        sV = E.sem(tag + "sV")
        sO = E.sem(tag + "sO")

        x0s_v = x0s.rearrange("p (s q) -> p s q", q=DC * SEG)

        def load_xseg(g, seg, buf):
            sX[buf].inc(g.dma_start(out=xseg[buf][:, :], in_=x0s_v[:, seg, :]), 16)

        if True:
            VEC, SCA = E.VEC, E.SCA
            def init_g(g):
                for se, va in entry:
                    g.wait_ge(se, va)
                for w, tsb in ((wq, wq_sb), (wk, wk_sb), (wv, wv_sb)):
                    sW.inc(g.dma_start(out=tsb[:, :], in_=w), 16)
                sW.inc(g.dma_start(out=beta_f[:, :], in_=betas), 16)
                load_xseg(g, 0, 0)
            block.gpsimd(init_g)

            def init_s(s):
                for se, va in entry:
                    s.wait_ge(se, va)
                sW2.inc(s.dma_start(out=ident[:, :], in_=ident_in), 16)
                sW2.inc(s.dma_start(
                    out=binmask[:, :].rearrange("p (k t) -> p k t", t=SEG),
                    in_=binmask_in.rearrange("(k p) t -> p k t", p=128)), 16)
                sW2.inc(s.dma_start(out=negc[:, :], in_=negc_in), 16)
            block.sync(init_s)

            def init_v(v):
                for se, va in entry:
                    v.wait_ge(se, va)
                v.memset(ones_f[:, :], 1.0)
                v.memset(mem_f[:, :], 0.0)
                v.memset(z_f[:, :], 1.0 / DK)
                v.memset(memz[:, :], 0.0)
                v.memset(v_aug[:, :], 1.0)
                inst = None
                for h in range(HPC):
                    inst = v.tensor_copy(memz[:, h * 129 + 128:h * 129 + 129], z_f[:, h:h + 1])
                sV.inc(inst, 1)
            VEC(init_v)
            v_init = sV.n

            def init_a(a):
                for se, va in entry:
                    a.wait_ge(se, va)
                a.wait_ge(sW.h, sW.n)
                a.wait_ge(sW2.h, sW2.n)
                sA.inc(a.activation(out=beta_f[:, :], in_=beta_f[:, :], func=AF.Sigmoid), 1)
            SCA(init_a)
            a_sig = sA.n

            def init_t(t):
                t.wait_ge(sA.h, a_sig)
                t.wait_ge(sV.h, v_init)
                for h in range(HPC):
                    mmi = t.matmul(p_mu[:, h * 128:(h + 1) * 128], ones_f[:, :],
                             beta_f[:, h * 128:(h + 1) * 128], start=True, stop=True)
                sP.inc(mmi, 1)
            block.tensor(init_t)
            p_bc = sP.n

            def init_v2(v):
                v.wait_ge(sP.h, p_bc)
                sV.inc(v.tensor_copy(bcast[:, :], p_mu[:, 0:256]), 1)
            VEC(init_v2)

            last = dict(aq=0, ak=0, vsq=0, vsk=0, vvaug=0, vpst=0, vcomb=0,
                        vdl=0, vsktm=0, vatile=0, vmemz=sV.n, pq=0, pev=0,
                        pst3=0, pad=0, pdl=0, pmu=0, out=0)

            it = 0
            for seg in range(nseg):
                buf = seg % 2
                xv = xseg[buf][:, :]

                if seg + 1 < nseg:
                    def pre_g(g, seg=seg, guard=last["pev"]):
                        if seg >= 1:
                            g.wait_ge(sP.h, guard)
                        load_xseg(g, seg + 1, (seg + 1) % 2)
                    block.gpsimd(pre_g)

                for h in range(HPC):
                    it += 1
                    wcol = lambda dc: dc * (HPC * DK) + h * DK

                    def proj_t(t, seg=seg, h=h, xv=xv, g=dict(last)):
                        t.wait_ge(sX[seg % 2].h, 16 * (seg // 2 + 1))
                        t.wait_ge(sA.h, g["ak"])
                        t.wait_ge(sV.h, g["vvaug"])
                        for dc in range(DC):
                            mmi = t.matmul(p_q[:, :], wq_sb[:, wcol(dc):wcol(dc) + DK],
                                     xv[:, dc * SEG:(dc + 1) * SEG],
                                     start=(dc == 0), stop=(dc == DC - 1))
                        sP.inc(mmi, 1)
                        for dc in range(DC):
                            mmi = t.matmul(p_k[:, :], wk_sb[:, wcol(dc):wcol(dc) + DK],
                                     xv[:, dc * SEG:(dc + 1) * SEG],
                                     start=(dc == 0), stop=(dc == DC - 1))
                        sP.inc(mmi, 1)
                        for tt in range(4):
                            for dc in range(DC):
                                mmi = t.matmul(p_v[:, tt * 128:(tt + 1) * 128],
                                         xv[:, dc * SEG + tt * 128:dc * SEG + tt * 128 + 128],
                                         wv_sb[:, wcol(dc):wcol(dc) + DK],
                                         start=(dc == 0), stop=(dc == DC - 1))
                        sP.inc(mmi, 1)
                    block.tensor(proj_t)
                    pe_q, pe_k, pe_v = sP.n - 2, sP.n - 1, sP.n
                    last["pq"] = pe_q
                    last["pev"] = pe_v

                    def sig_a(a, pe_q=pe_q, pe_k=pe_k, g=dict(last)):
                        a.wait_ge(sP.h, pe_q)
                        a.wait_ge(sV.h, g["vsk"])
                        a.activation(out=q_sb[:, :], in_=p_q[:, :], func=AF.Copy)
                        a.activation(out=tq1[:, :], in_=p_q[:, :], func=AF.Relu)
                        a.activation(out=tq2[:, :], in_=p_q[:, :], func=AF.Relu, scale=-1.0)
                        sA.inc(a.activation(out=tq2[:, :], in_=tq2[:, :], func=AF.Exp, scale=-1.0), 1)
                        a.wait_ge(sP.h, pe_k)
                        a.activation(out=k_sb[:, :], in_=p_k[:, :], func=AF.Copy)
                        a.activation(out=tk1[:, :], in_=p_k[:, :], func=AF.Relu)
                        a.activation(out=tk2[:, :], in_=p_k[:, :], func=AF.Relu, scale=-1.0)
                        sA.inc(a.activation(out=tk2[:, :], in_=tk2[:, :], func=AF.Exp, scale=-1.0), 1)
                    SCA(sig_a)
                    a_q, a_k = sA.n - 1, sA.n
                    last["aq"], last["ak"] = a_q, a_k

                    def sig_v(v, a_q=a_q, a_k=a_k, pe_v=pe_v, g=dict(last)):
                        v.wait_ge(sA.h, a_q)
                        sV.inc(v.tensor_add(sq_sb[:, :], tq1[:, :], tq2[:, :]), 1)
                        v.wait_ge(sA.h, a_k)
                        v.tensor_add(sk_sb[:, :], tk1[:, :], tk2[:, :])
                        sV.inc(v.reduce_sum(zadd[:, :], sk_sb[:, :], axis=AX.X), 1)
                        v.wait_ge(sP.h, pe_v)
                        inst = None
                        for tt in range(4):
                            inst = v.tensor_copy(v_aug[:, tt * 129:tt * 129 + 128],
                                                 p_v[:, tt * 128:(tt + 1) * 128])
                        sV.inc(inst, 1)
                    VEC(sig_v)
                    v_sq, v_sk, v_vaug = sV.n - 2, sV.n - 1, sV.n
                    last["vsq"], last["vsk"], last["vvaug"] = v_sq, v_sk, v_vaug

                    aexp = []
                    vpst = []
                    for tt in range(4):
                        def st_t(t, tt=tt, g=dict(last), aexp=list(aexp)):
                            if tt == 0:
                                t.wait_ge(sA.h, last["ak"])
                            else:
                                t.wait_ge(sA.h, aexp[tt - 1])
                            mmi = t.matmul(p_st[:, :], k_sb[:, tt * 128:(tt + 1) * 128],
                                     q_sb[:, :], start=True, stop=True)
                            sP.inc(mmi, 1)
                        block.tensor(st_t)
                        pe_st = sP.n

                        def st_a(a, tt=tt, pe_st=pe_st):
                            a.wait_ge(sP.h, pe_st)
                            sA.inc(a.activation(out=pst[:, tt * SEG:(tt + 1) * SEG], in_=p_st[:, :],
                                                func=AF.Exp, scale=ISQRT_DK, bias=negc[:, :]), 1)
                        SCA(st_a)
                        aexp.append(sA.n)

                        def st_v(v, tt=tt, a_e=sA.n):
                            v.wait_ge(sA.h, a_e)
                            sV.inc(v.tensor_mul(pst[:, tt * SEG:(tt + 1) * SEG],
                                                pst[:, tt * SEG:(tt + 1) * SEG],
                                                binmask[:, tt * SEG:(tt + 1) * SEG]), 1)
                        VEC(st_v)
                        vpst.append(sV.n)
                    last["pst3"] = sP.n
                    last["vpst"] = vpst[-1]

                    for ss in range(4):
                        def ad_t(t, ss=ss, g=dict(last), vpst=list(vpst)):
                            if ss == 0:
                                t.wait_ge(sV.h, vpst[3])
                                t.wait_ge(sV.h, last["vvaug"])
                            else:
                                t.wait_ge(sV.h, last["vcomb"])
                            for tt in range(4):
                                mmi = t.matmul(p_ad[:, 0:129],
                                         pst[:, tt * SEG + ss * 128:tt * SEG + ss * 128 + 128],
                                         v_aug[:, tt * 129:(tt + 1) * 129],
                                         start=(tt == 0), stop=(tt == 3))
                            sP.inc(mmi, 1)
                            if ss == 0:
                                t.wait_ge(sV.h, last["vsq"])
                                t.wait_ge(sV.h, last["vmemz"])
                            mmi = t.matmul(p_am[:, 0:129], sq_sb[:, ss * 128:(ss + 1) * 128],
                                     memz[:, h * 129:(h + 1) * 129], start=True, stop=True)
                            sP.inc(mmi, 1)
                        block.tensor(ad_t)
                        pe_ad, pe_am = sP.n - 1, sP.n
                        last["pad"] = pe_am

                        def comb_v(v, ss=ss, pe_ad=pe_ad, pe_am=pe_am, h=h, g=dict(last)):
                            v.wait_ge(sP.h, pe_ad)
                            v.reciprocal(rc[:, 0:1], p_ad[:, 128:129])
                            v.tensor_scalar(att_d[:, :], p_ad[:, 0:128], rc[:, 0:1], None, ALU.mult)
                            v.wait_ge(sP.h, pe_am)
                            v.reciprocal(rc[:, 1:2], p_am[:, 128:129])
                            v.tensor_scalar(att_m[:, :], p_am[:, 0:128], rc[:, 1:2], None, ALU.mult)
                            v.tensor_sub(cmb[:, :], att_m[:, :], att_d[:, :])
                            v.tensor_mul(cmb[:, :], cmb[:, :], bcast[:, h * 128:(h + 1) * 128])
                            sV.inc(v.tensor_add(att_bf[:, :], att_d[:, :], cmb[:, :]), 1)
                        VEC(comb_v)
                        last["vcomb"] = sV.n
                        v_comb = sV.n

                        def tr_t(t, ss=ss, v_comb=v_comb, g=dict(last)):
                            t.wait_ge(sV.h, v_comb)
                            if ss == 0:
                                t.wait_ge(sV.h, g["vsktm"])
                            else:
                                t.wait_ge(sV.h, g["vatile"])
                            mmi = t.matmul(p_tr[:, ss * 128:(ss + 1) * 128], att_bf[:, :], ident[:, :],
                                     start=True, stop=True)
                            sP.inc(mmi, 1)
                        block.tensor(tr_t)
                        pe_tr = sP.n

                        def atile_v(v, ss=ss, pe_tr=pe_tr, g=dict(last)):
                            v.wait_ge(sP.h, pe_tr)
                            if ss == 0:
                                v.wait_ge(sO.h, g["out"])
                            sV.inc(v.tensor_copy(att_tile[:, ss * 128:(ss + 1) * 128],
                                                 p_tr[:, ss * 128:(ss + 1) * 128]), 1)
                        VEC(atile_v)
                        last["vatile"] = sV.n

                    def out_s(s, h=h, seg=seg, v_at=sV.n):
                        s.wait_ge(sV.h, v_at)
                        sO.inc(s.dma_start(out=att0[h * DK:(h + 1) * DK, seg * SEG:(seg + 1) * SEG],
                                           in_=att_tile[:, :]), 16)
                    block.sync(out_s)
                    last["out"] = sO.n

                    vdl = []
                    for tt in range(4):
                        def dl_t(t, tt=tt, h=h, g=dict(last), vdl=list(vdl)):
                            if tt == 0:
                                t.wait_ge(sV.h, last["vsk"])
                                t.wait_ge(sV.h, last["vcomb"])
                            else:
                                t.wait_ge(sV.h, vdl[tt - 1])
                            mmi = t.matmul(p_am[:, 0:129], sk_sb[:, tt * 128:(tt + 1) * 128],
                                     memz[:, h * 129:(h + 1) * 129], start=True, stop=True)
                            sP.inc(mmi, 1)
                        block.tensor(dl_t)
                        pe_dl = sP.n

                        def dl_v(v, tt=tt, pe_dl=pe_dl):
                            v.wait_ge(sP.h, pe_dl)
                            v.reciprocal(rc[:, 2:3], p_am[:, 128:129])
                            v.tensor_scalar(rc[:, 3:4], rc[:, 2:3], -1.0, None, ALU.mult)
                            sV.inc(v.scalar_tensor_tensor(
                                ndelta[:, tt * 128:(tt + 1) * 128], p_am[:, 0:128],
                                rc[:, 3:4], v_aug[:, tt * 129:tt * 129 + 128],
                                ALU.mult, ALU.add), 1)
                        VEC(dl_v)
                        vdl.append(sV.n)
                    last["vdl"] = sV.n

                    def kt_t(t, g=dict(last), vdl3=sV.n):
                        t.wait_ge(sV.h, g["vatile"])
                        t.wait_ge(sV.h, g["vsk"])
                        for tt in range(4):
                            mmi = t.matmul(p_tr[:, tt * 128:(tt + 1) * 128],
                                     sk_sb[:, tt * 128:(tt + 1) * 128], ident[:, :],
                                     start=True, stop=True)
                        sP.inc(mmi, 1)
                    block.tensor(kt_t)
                    pe_kt = sP.n

                    def kt_v(v, pe_kt=pe_kt):
                        v.wait_ge(sP.h, pe_kt)
                        sV.inc(v.tensor_copy(sk_tm[:, :], p_tr[:, :]), 1)
                    VEC(kt_v)
                    last["vsktm"] = sV.n
                    v_ktm = sV.n

                    def mu_t(t, v_ktm=v_ktm, vdl3=last["vdl"]):
                        t.wait_ge(sV.h, v_ktm)
                        t.wait_ge(sV.h, vdl3)
                        for tt in range(4):
                            mmi = t.matmul(p_mu[:, 0:128], sk_tm[:, tt * 128:(tt + 1) * 128],
                                     ndelta[:, tt * 128:(tt + 1) * 128],
                                     start=(tt == 0), stop=(tt == 3))
                        sP.inc(mmi, 1)
                    block.tensor(mu_t)
                    pe_mu = sP.n
                    last["pmu"] = pe_mu

                    def mu_v(v, pe_mu=pe_mu, h=h):
                        v.wait_ge(sP.h, pe_mu)
                        v.tensor_add(mem_f[:, h * 128:(h + 1) * 128],
                                     mem_f[:, h * 128:(h + 1) * 128], p_mu[:, 0:128])
                        v.tensor_add(z_f[:, h:h + 1], z_f[:, h:h + 1], zadd[:, :])
                        v.tensor_copy(memz[:, h * 129:h * 129 + 128],
                                      mem_f[:, h * 128:(h + 1) * 128])
                        sV.inc(v.tensor_copy(memz[:, h * 129 + 128:h * 129 + 129],
                                             z_f[:, h:h + 1]), 1)
                    VEC(mu_v)
                    last["vmemz"] = sV.n

    return {"pe": (sP.h, sP.n), "act": (sA.h, sA.n), "dve": (sV.h, sV.n),
            "out": (sO.h, sO.n)}



def _consts_p0():
    ident = np.eye(128, dtype=BF)
    t_idx = np.arange(SEG)[:, None]
    s_idx = np.arange(SEG)[None, :]
    binmask = (s_idx >= t_idx).astype(BF)
    negc = np.full((128, 1), -CSHIFT, np.float32)
    return ident, binmask, negc


def _p0_in_maps(x0T_np, Wq, Wk, Wv, betas):
    ident, binmask, negc = _consts_p0()
    maps = []
    for c in range(N_CORES):
        cols = slice(c * HPC * DK, (c + 1) * HPC * DK)
        maps.append({
            "x0T": x0T_np,
            "wq": np.ascontiguousarray(Wq[:, cols]),
            "wk": np.ascontiguousarray(Wk[:, cols]),
            "wv": np.ascontiguousarray(Wv[:, cols]),
            "betas": np.ascontiguousarray(
                betas.reshape(16, DK)[2 * c:2 * c + 2, :].reshape(1, HPC * DK)),
            "ident": ident, "binmask": binmask, "negc": negc,
        })
    return maps





FFN_I = 8192
GSZ = 512
NG = FFN_I // GSZ
GC = GSZ // 128
NT = TPC // 128
LN_EPS = 1e-5


def build_p1(sparsify, final_tm):
    def io_builder(nc):
        io = dict(
            attc_mode="direct",
            attc=nc.dram_tensor("attc", [D, TPC], BF16, kind="ExternalInput").ap(),
            wo=nc.dram_tensor("wo", [128, DC * D], F32, kind="ExternalInput").ap(),
            wg=nc.dram_tensor("wg", [128, NG * DC * GSZ], F32, kind="ExternalInput").ap(),
            wu=nc.dram_tensor("wu", [128, NG * DC * GSZ], F32, kind="ExternalInput").ap(),
            wd=nc.dram_tensor("wd", [128, NG * GC * D], F32, kind="ExternalInput").ap(),
            ident=nc.dram_tensor("ident", [128, 128], BF16, kind="ExternalInput").ap(),
            identf=nc.dram_tensor("identf", [128, 128], F32, kind="ExternalInput").ap(),
            eps=nc.dram_tensor("epsrow", [1, 1], F32, kind="ExternalInput").ap(),
            x1scr=nc.dram_tensor("x1scr", [D, TPC], F32).ap(),
            xsum_t=None, cc_wait=None,
        )
        xres = nc.dram_tensor("xres", [D, TPC], F32, kind="ExternalInput").ap()
        io["xres_chunk"] = lambda s, dc: xres[dc * 128:(dc + 1) * 128, :]
        if final_tm:
            io["out_mode"] = "yout"
            io["out"] = nc.dram_tensor("yout", [TPC, D], F32, kind="ExternalOutput").ap()
        else:
            io["out_mode"] = "x2T"
            io["out"] = nc.dram_tensor("x2T", [D, TPC], F32, kind="ExternalOutput").ap()
        return io
    return _standalone(lambda E, io, entry: emit_p1(E, io, entry, sparsify, final_tm),
                       io_builder)


def emit_p1(E, io, entry, sparsify, final_tm, tag="p1"):
    nc, block, ctx = E.nc, E.block, E.ctx
    wo, wg, wu, wd = io["wo"], io["wg"], io["wu"], io["wd"]
    ident_in, identf_in, eps_in = io["ident"], io["identf"], io["eps"]
    out = io["out"]
    x1scr = io["x1scr"]
    out_mode = io["out_mode"]
    resident_res = io["xres_chunk"] is None
    with contextlib.ExitStack() as pctx:
        sb = lambda name, shape, dt: pctx.enter_context(nc.sbuf_tensor(tag + name, shape, dt))

        if io["xsum_t"] is None:
            xsum = sb("xsum", [128, DC * TPC], F32)
        else:
            xsum = io["xsum_t"]
        xstr = [sb(f"xstr{i}", [128, TPC], F32) for i in range(2)]
        ident = sb("ident_sb", [128, 128], BF16)
        identf = sb("identf_sb", [128, 128], F32)
        ones_col = sb("ones_col", [128, 1], F32)
        ones_row = sb("ones_row", [1, 128], F32)
        eps_sb = sb("eps_sb", [1, 1], F32)
        strow = sb("strow", [1, 3 * TPC], F32)
        silu_s = sb("silu_s", [128, 512], F32)

        pA = pctx.enter_context(nc.psum_tensor(tag + "pA", [128, 1024], F32))
        pB = pctx.enter_context(nc.psum_tensor(tag + "pB", [128, 1024], F32))
        pC = pctx.enter_context(nc.psum_tensor(tag + "pC", [128, 512], F32))
        pD = pctx.enter_context(nc.psum_tensor(tag + "pD", [128, 512], F32))
        pE = pctx.enter_context(nc.psum_tensor(tag + "pE", [128, 512], F32))
        pF = pctx.enter_context(nc.psum_tensor(tag + "pF", [128, 512], F32))

        sLo = E.sem(tag + "sLo")
        sLgu = E.sem(tag + "sLgu")
        sLd = E.sem(tag + "sLd")
        sWi = E.sem(tag + "sWi")
        sAt = E.sem(tag + "sAt")
        sXr = [E.sem(tag + "sXr0"), E.sem(tag + "sXr1")]
        sP = E.sem(tag + "sP")
        sA = E.sem(tag + "sA")
        sV = E.sem(tag + "sV")
        sScr = E.sem(tag + "sScr")
        sO = E.sem(tag + "sO")

        if True:
            VEC, SCA = E.VEC, E.SCA

            def emit_ln(src, v_ready):
                pe_st = []
                for dc in range(DC):
                    def sq_a(a, dc=dc):
                        if dc == 0:
                            a.wait_ge(sV.h, v_ready)
                        if dc >= 2:
                            a.wait_ge(sP.h, pe_st[dc - 2])
                        sA.inc(a.activation(out=xstr[dc % 2][:, :],
                                            in_=src[:, dc * TPC:(dc + 1) * TPC],
                                            func=AF.Square), 1)
                    SCA(sq_a)
                    a_n = sA.n

                    def st_t(t, dc=dc, a_n=a_n):
                        if dc == 0:
                            t.wait_ge(sV.h, v_ready)
                        for tb, bank in ((0, pC), (1, pD)):
                            t.matmul(bank[0:1, 0:512], ones_col[:, :],
                                     src[:, dc * TPC + tb * 512:dc * TPC + tb * 512 + 512],
                                     start=(dc == 0), stop=(dc == DC - 1))
                        t.wait_ge(sA.h, a_n)
                        mmi = None
                        for tb, bank in ((0, pE), (1, pF)):
                            mmi = t.matmul(bank[0:1, 0:512], ones_col[:, :],
                                           xstr[dc % 2][:, tb * 512:tb * 512 + 512],
                                           start=(dc == 0), stop=(dc == DC - 1))
                        sP.inc(mmi, 1)
                    block.tensor(st_t)
                    pe_st.append(sP.n)
                pe_stats = sP.n

                def stat_v(v, pe_stats=pe_stats):
                    m = strow[0:1, 0:TPC]
                    w = strow[0:1, TPC:2 * TPC]
                    tmp = strow[0:1, 2 * TPC:3 * TPC]
                    v.wait_ge(sP.h, pe_stats)
                    v.tensor_scalar(m[0:1, 0:512], pC[0:1, 0:512], 1.0 / D, None, ALU.mult)
                    v.tensor_scalar(m[0:1, 512:1024], pD[0:1, 0:512], 1.0 / D, None, ALU.mult)
                    v.tensor_scalar(w[0:1, 0:512], pE[0:1, 0:512], 1.0 / D, None, ALU.mult)
                    v.tensor_scalar(w[0:1, 512:1024], pF[0:1, 0:512], 1.0 / D, None, ALU.mult)
                    v.tensor_mul(tmp, m, m)
                    sV.inc(v.tensor_sub(w, w, tmp), 1)
                    vv = sV.n

                    return vv
                v_var = VEC(stat_v) or sV.n

                def sd_a(a, v_var=v_var):
                    a.wait_ge(sV.h, v_var)
                    sA.inc(a.activation(out=strow[0:1, 2 * TPC:3 * TPC],
                                        in_=strow[0:1, TPC:2 * TPC],
                                        func=AF.Sqrt, bias=eps_sb[0:1, 0:1]), 1)
                SCA(sd_a)
                a_sd = sA.n

                def rstd_v(v, a_sd=a_sd):
                    v.wait_ge(sA.h, a_sd)
                    sd = strow[0:1, 2 * TPC:3 * TPC]
                    w = strow[0:1, TPC:2 * TPC]
                    m2 = strow[0:1, 0:TPC]
                    v.reciprocal(w, sd)
                    v.tensor_mul(sd, sd, w)
                    v.tensor_scalar(sd, sd, -1.0, 2.0, ALU.mult, ALU.add)
                    sV.inc(v.tensor_mul(w, w, sd), 1)
                VEC(rstd_v)
                v_rstd = sV.n

                def bc_t(t, v_rstd=v_rstd):
                    t.wait_ge(sV.h, v_rstd)
                    mmi = None
                    for tb in range(2):
                        t.matmul(pA[:, tb * 512:(tb + 1) * 512], ones_row[:, :],
                                 strow[0:1, tb * 512:(tb + 1) * 512], start=True, stop=True)
                        mmi = t.matmul(pB[:, tb * 512:(tb + 1) * 512], ones_row[:, :],
                                       strow[0:1, TPC + tb * 512:TPC + (tb + 1) * 512],
                                       start=True, stop=True)
                    sP.inc(mmi, 1)
                block.tensor(bc_t)
                return sP.n

            _scA = nc.named_scope(tag + "_compose"); _scA.__enter__()
            with nc.sbuf_tensor(tag + "attc_sb", [128, DC * TPC], BF16) as attc_sb, \
                 contextlib.ExitStack() as actx:

                v_scr = 0
                if io["attc_mode"] == "direct":
                    attc = io["attc"]
                    def attc_s(s):
                        for se, va in entry:
                            s.wait_ge(se, va)
                        sAt.inc(s.dma_start(
                            out=attc_sb[:, :].rearrange("p (k t) -> p k t", t=TPC),
                            in_=attc.rearrange("(k p) t -> p k t", p=128)), 16)
                    block.sync(attc_s)
                elif io["attc_mode"] == "gather":
                    attg = io["attc_g"]
                    ccw = io["cc_wait"]
                    def attc_s(s):
                        for se, va in entry:
                            s.wait_ge(se, va)
                        s.wait_ge(ccw[0], ccw[1])
                        pid = s.partition_id()
                        sAt.inc(s.dma_start(
                            out=attc_sb[:, :].rearrange("p (k t) -> p k t", t=TPC),
                            in_=attg.rearrange("(k p) t -> p k t", p=128)[:, :, bass.ds(pid * TPC, TPC)]), 16)
                    block.sync(attc_s)
                else:
                    attg = io["attc_g"]
                    ccw = io["cc_wait"]
                    with nc.sbuf_tensor(tag + "att_raw", [128, DC * TPC], BF16) as att_raw:
                        def attc_s(s):
                            for se, va in entry:
                                s.wait_ge(se, va)
                            s.wait_ge(ccw[0], ccw[1])
                            pid = s.partition_id()
                            sAt.inc(s.dma_start(
                                out=att_raw[:, :].rearrange("p (k t) -> p k t", t=TPC),
                                in_=attg.rearrange("(k p) t -> p k t", p=128)[:, :, bass.ds(pid * TPC, TPC)]), 16)
                        block.sync(attc_s)

                        def scr_v(v, n_at=None):
                            for se, va in entry:
                                v.wait_ge(se, va)
                            v.wait_ge(sAt.h, sAt.n)
                            dst = attc_sb[:, :].rearrange(
                                "p (j segl h w) -> p j segl h w", j=16, segl=2, h=16, w=32)
                            srcv = att_raw[:, :].rearrange(
                                "p (h segl w j) -> p h segl w j", h=16, segl=2, w=32, j=16
                            ).transpose([0, 4, 2, 1, 3])
                            sV.inc(v.tensor_copy(dst, srcv), 1)
                        VEC(scr_v)
                        v_scr = sV.n

                wo_sb = actx.enter_context(nc.sbuf_tensor(tag + "wo_sb", [128, DC * D], BF16))

                def init_g(g):
                    for se, va in entry:
                        g.wait_ge(se, va)
                    if v_scr:
                        g.wait_ge(sV.h, v_scr)
                    sLo.inc(g.dma_start(out=wo_sb[:, :], in_=wo), 16)
                block.gpsimd(init_g)

                xres_w = {}
                def init_s(s):
                    sWi.inc(s.dma_start(out=ident[:, :], in_=ident_in), 16)
                    sWi.inc(s.dma_start(out=identf[:, :], in_=identf_in), 16)
                    sWi.inc(s.dma_start(out=eps_sb[:, :], in_=eps_in), 16)
                    if not resident_res:
                        for i in range(2):
                            xres_w[i] = sXr[i].inc(s.dma_start(
                                out=xstr[i][:, :], in_=io["xres_chunk"](s, i)), 16)
                block.sync(init_s)

                def init_v(v):
                    for se, va in entry:
                        v.wait_ge(se, va)
                    v.memset(ones_col[:, :], 1.0)
                    sV.inc(v.memset(ones_row[:, :], 1.0), 1)
                VEC(init_v)

                vcomb = []
                for fc in range(DC):
                    for tb in range(2):
                        bank = pC if tb == 0 else pD
                        def comb_t(t, fc=fc, tb=tb, bank=bank):
                            if fc == 0 and tb == 0:
                                t.wait_ge(sLo.h, sLo.n)
                                t.wait_ge(sWi.h, sWi.n)
                                t.wait_ge(sAt.h, sAt.n)
                                if v_scr:
                                    t.wait_ge(sV.h, v_scr)
                            if len(vcomb) >= 2:
                                t.wait_ge(sV.h, vcomb[-2])
                            mmi = None
                            for ac in range(DC):
                                mmi = t.matmul(bank[:, 0:512],
                                               wo_sb[:, ac * D + fc * 128:ac * D + fc * 128 + 128],
                                               attc_sb[:, ac * TPC + tb * 512:ac * TPC + tb * 512 + 512],
                                               start=(ac == 0), stop=(ac == DC - 1))
                            sP.inc(mmi, 1)
                        block.tensor(comb_t)
                        pe_c = sP.n

                        def comb_v(v, fc=fc, tb=tb, bank=bank, pe_c=pe_c,
                                   w_n=(0 if resident_res else xres_w[fc])):
                            v.wait_ge(sP.h, pe_c)
                            dst = xsum[:, fc * TPC + tb * 512:fc * TPC + tb * 512 + 512]
                            if resident_res:
                                sV.inc(v.tensor_add(dst, bank[:, 0:512], dst), 1)
                            else:
                                v.wait_ge(sXr[fc % 2].h, w_n)
                                sV.inc(v.tensor_add(dst, bank[:, 0:512],
                                                    xstr[fc % 2][:, tb * 512:tb * 512 + 512]), 1)
                        VEC(comb_v)
                        vcomb.append(sV.n)
                    if fc + 2 < DC and not resident_res:
                        def pre_s(s, fc=fc, guard=vcomb[-1]):
                            s.wait_ge(sV.h, guard)
                            xres_w[fc + 2] = sXr[fc % 2].inc(
                                s.dma_start(out=xstr[fc % 2][:, :],
                                            in_=io["xres_chunk"](s, fc + 2)), 16)
                        block.sync(pre_s)
                v_combe = sV.n
                pe_combe = sP.n
            _scA.__exit__(None, None, None)

            with nc.sbuf_tensor(tag + "x1b", [128, DC * TPC], BF16) as x1b:
                _scL = nc.named_scope(tag + "_ln1"); _scL.__enter__()
                pe_bc1 = emit_ln(xsum, v_combe)
                scr_w = 0
                for dc in range(DC):
                    def ap_v(v, dc=dc, pe_bc1=pe_bc1):
                        if dc == 0:
                            v.wait_ge(sP.h, pe_bc1)
                        c = xsum[:, dc * TPC:(dc + 1) * TPC]
                        v.tensor_sub(c, c, pA[:, :])
                        sV.inc(v.tensor_mul(c, c, pB[:, :]), 1)
                    VEC(ap_v)
                    v_n = sV.n

                    def ap_a(a, dc=dc, v_n=v_n):
                        a.wait_ge(sV.h, v_n)
                        sA.inc(a.activation(out=x1b[:, dc * TPC:(dc + 1) * TPC],
                                            in_=xsum[:, dc * TPC:(dc + 1) * TPC],
                                            func=AF.Copy), 1)
                    SCA(ap_a)

                    def ap_s(s, dc=dc, v_n=v_n):
                        s.wait_ge(sV.h, v_n)
                        sScr.inc(s.dma_start(out=x1scr[dc * 128:(dc + 1) * 128, :],
                                             in_=xsum[:, dc * TPC:(dc + 1) * TPC]), 16)
                    block.sync(ap_s)
                a_x1b = sA.n
                v_ap1 = sV.n
                scr_w = sScr.n
                _scL.__exit__(None, None, None)

                _scF = nc.named_scope(tag + "_ffn"); _scF.__enter__()
                QSZ = NT * GSZ // 4
                with nc.sbuf_tensor(tag + "wg_sb", [128, DC * GSZ], BF16) as wg_sb, \
                     nc.sbuf_tensor(tag + "wu_sb", [128, DC * GSZ], BF16) as wu_sb, \
                     nc.sbuf_tensor(tag + "wd_sb", [128, GC * D], BF16) as wd_sb, \
                     nc.sbuf_tensor(tag + "h_tm0", [128, NT * GSZ], BF16) as h_tm0, \
                     nc.sbuf_tensor(tag + "h_tm1", [128, NT * GSZ], BF16) as h_tm1, \
                     nc.sbuf_tensor(tag + "h_fm0", [128, GC * TPC], BF16) as h_fm0, \
                     nc.sbuf_tensor(tag + "h_fm1", [128, GC * TPC], BF16) as h_fm1, \
                     nc.sbuf_tensor(tag + "habs", [128, QSZ], BF16) as habs, \
                     nc.sbuf_tensor(tag + "spm", [128, 5 * (QSZ // 4)], BF16) as spm:
                    h_tmb = [h_tm0, h_tm1]
                    h_fmb = [h_fm0, h_fm1]

                    wg_v = wg.rearrange("p (g q) -> p g q", q=DC * GSZ)
                    wu_v = wu.rearrange("p (g q) -> p g q", q=DC * GSZ)
                    wd_v = wd.rearrange("p (g q) -> p g q", q=GC * D)

                    wld_gu = {}
                    wld_d = {}

                    def load_gu(g_eng, g, guard):
                        g_eng.wait_ge(sP.h, guard)
                        sLgu.inc(g_eng.dma_start(out=wg_sb[:, :], in_=wg_v[:, g, :]), 16)
                        sLgu.inc(g_eng.dma_start(out=wu_sb[:, :], in_=wu_v[:, g, :]), 16)
                        wld_gu[g] = sLgu.n

                    def load_d(g_eng, g, guard):
                        if guard:
                            g_eng.wait_ge(sP.h, guard)
                        sLd.inc(g_eng.dma_start(out=wd_sb[:, :], in_=wd_v[:, g, :]), 16)
                        wld_d[g] = sLd.n

                    block.gpsimd(lambda e: (load_gu(e, 0, pe_combe), load_d(e, 0, 0)))

                    pe_g = {}; pe_u = {}; pe_gu_end = {}; pe_tr = {}; pe_dn = {}
                    pe_group_end = {}
                    a_sil = {}; a_abs = {}
                    v_h = {}; v_sp = {}; v_trv = {}; v_dnv = {}

                    NSP = 4 if sparsify else 0

                    def plan_slot(g):
                        n = sP.n
                        for tt in range(NT):
                            n += 1; pe_g[(g, tt)] = n
                            n += 1; pe_u[(g, tt)] = n
                        pe_gu_end[g] = n
                        if g >= 1:
                            for tt in range(NT):
                                n += 1; pe_tr[(g - 1, tt)] = n
                            for idx in range(2 * DC):
                                n += 1; pe_dn[(g - 1, idx)] = n
                            pe_group_end[g - 1] = n
                        sP.n = n
                        n = sA.n
                        for tt in range(NT):
                            n += 1; a_sil[(g, tt)] = n
                        for q in range(NSP):
                            n += 1; a_abs[(g, q)] = n
                        sA.n = n
                        n = sV.n
                        for tt in range(NT):
                            n += 1; v_h[(g, tt)] = n
                        if g >= 1:
                            for tt in range(NT):
                                n += 1; v_trv[(g - 1, tt)] = n
                            for idx in range(2 * DC):
                                n += 1; v_dnv[(g - 1, idx)] = n
                        for q in range(NSP):
                            n += 1; v_sp[(g, q)] = n
                        sV.n = n

                    def ten_slot(t, g):
                        htm = h_tmb[g % 2] if g < NG else None
                        if g < NG:
                            for tt in range(NT):
                                if tt == 0:
                                    t.wait_ge(sLgu.h, wld_gu[g])
                                    if g == 0:
                                        t.wait_ge(sA.h, a_x1b)
                                    else:
                                        t.wait_ge(sA.h, a_sil[(g - 1, NT - 1)])
                                        t.wait_ge(sV.h, v_h[(g - 1, NT - 1)])
                                else:
                                    t.wait_ge(sA.h, a_sil[(g, tt - 1)])
                                    t.wait_ge(sV.h, v_h[(g, tt - 1)])
                                mmi = None
                                for dc in range(DC):
                                    mmi = t.matmul(pC[:, :],
                                                   x1b[:, dc * TPC + tt * 128:dc * TPC + tt * 128 + 128],
                                                   wg_sb[:, dc * GSZ:(dc + 1) * GSZ],
                                                   start=(dc == 0), stop=(dc == DC - 1))
                                mmi.then_inc(sP.h, 1)
                                for dc in range(DC):
                                    mmi = t.matmul(pD[:, :],
                                                   x1b[:, dc * TPC + tt * 128:dc * TPC + tt * 128 + 128],
                                                   wu_sb[:, dc * GSZ:(dc + 1) * GSZ],
                                                   start=(dc == 0), stop=(dc == DC - 1))
                                mmi.then_inc(sP.h, 1)
                        if g >= 1:
                            gp = g - 1
                            hsrc = h_tmb[gp % 2]
                            hfm = h_fmb[gp % 2]
                            for tt in range(NT):
                                if tt == 0:
                                    if sparsify:
                                        t.wait_ge(sV.h, v_sp[(gp, NSP - 1)])
                                    else:
                                        t.wait_ge(sV.h, v_h[(gp, NT - 1)])
                                    if gp >= 2:
                                        t.wait_ge(sV.h, v_trv[(gp - 1, NT - 1)])
                                else:
                                    t.wait_ge(sV.h, v_trv[(gp, tt - 1)])
                                mmi = None
                                for ic in range(GC):
                                    mmi = t.matmul(pE[:, ic * 128:(ic + 1) * 128],
                                                   hsrc[:, tt * GSZ + ic * 128:tt * GSZ + ic * 128 + 128],
                                                   ident[:, :], start=True, stop=True)
                                mmi.then_inc(sP.h, 1)
                            for idx in range(2 * DC):
                                fc, tb = idx // 2, idx % 2
                                bank = pF if idx % 2 == 0 else pA
                                if idx == 0:
                                    t.wait_ge(sV.h, v_trv[(gp, NT - 1)])
                                    t.wait_ge(sLd.h, wld_d[gp])
                                    if gp == 0:
                                        t.wait_ge(sScr.h, scr_w)
                                if idx <= 1 and gp == 0:
                                    t.wait_ge(sV.h, v_ap1)
                                if idx >= 2:
                                    t.wait_ge(sV.h, v_dnv[(gp, idx - 2)])
                                mmi = None
                                for ic in range(GC):
                                    mmi = t.matmul(bank[:, 0:512],
                                                   wd_sb[:, ic * D + fc * 128:ic * D + fc * 128 + 128],
                                                   hfm[:, ic * TPC + tb * 512:ic * TPC + tb * 512 + 512],
                                                   start=(ic == 0), stop=(ic == GC - 1))
                                mmi.then_inc(sP.h, 1)

                    def sca_slot(a, g):
                        if g >= NG:
                            return
                        for tt in range(NT):
                            a.wait_ge(sP.h, pe_g[(g, tt)])
                            a.activation(out=silu_s[:, :], in_=pC[:, :],
                                         func=AF.Silu).then_inc(sA.h, 1)
                            DRAIN_ACT and a.drain()
                        htm = h_tmb[g % 2]
                        for q in range(NSP):
                            a.wait_ge(sV.h, v_h[(g, NT - 1)])
                            if q >= 1:
                                a.wait_ge(sV.h, v_sp[(g, q - 1)])
                            a.activation(out=habs[:, :], in_=htm[:, q * QSZ:(q + 1) * QSZ],
                                         func=AF.Abs).then_inc(sA.h, 1)
                            DRAIN_ACT and a.drain()

                    def vec_slot(v, g):
                        if g < NG:
                            htm = h_tmb[g % 2]
                            for tt in range(NT):
                                v.wait_ge(sA.h, a_sil[(g, tt)])
                                v.wait_ge(sP.h, pe_u[(g, tt)])
                                v.tensor_mul(htm[:, tt * GSZ:(tt + 1) * GSZ],
                                             silu_s[:, :], pD[:, :]).then_inc(sV.h, 1)
                                DRAIN_DVE and v.drain()
                        if g >= 1:
                            gp = g - 1
                            hfm = h_fmb[gp % 2]
                            for tt in range(NT):
                                v.wait_ge(sP.h, pe_tr[(gp, tt)])
                                dst = hfm[:, :].rearrange("p (ic t) -> p ic t", t=TPC)[:, :, tt * 128:(tt + 1) * 128]
                                srcp = pE[:, :].rearrange("p (ic r) -> p ic r", r=128)
                                v.tensor_copy(dst, srcp).then_inc(sV.h, 1)
                                DRAIN_DVE and v.drain()
                            for idx in range(2 * DC):
                                fc, tb = idx // 2, idx % 2
                                bank = pF if idx % 2 == 0 else pA
                                v.wait_ge(sP.h, pe_dn[(gp, idx)])
                                dst = xsum[:, fc * TPC + tb * 512:fc * TPC + tb * 512 + 512]
                                if gp == 0:
                                    v.tensor_copy(dst, bank[:, 0:512]).then_inc(sV.h, 1)
                                else:
                                    v.tensor_add(dst, dst, bank[:, 0:512]).then_inc(sV.h, 1)
                                DRAIN_DVE and v.drain()
                        if g < NG and sparsify:
                            htm = h_tmb[g % 2]
                            QQ = QSZ // 4
                            for q in range(NSP):
                                v.wait_ge(sA.h, a_abs[(g, q)])
                                hv = htm[:, q * QSZ:(q + 1) * QSZ].rearrange("p (b four) -> p b four", four=4)
                                av = habs[:, :].rearrange("p (b four) -> p b four", four=4)
                                s0 = spm[:, 0 * QQ:1 * QQ]; s1 = spm[:, 1 * QQ:2 * QQ]
                                s2 = spm[:, 2 * QQ:3 * QQ]; s3 = spm[:, 3 * QQ:4 * QQ]
                                tmp = spm[:, 4 * QQ:5 * QQ]
                                aa = [av[:, :, j] for j in range(4)]
                                hh = [hv[:, :, j] for j in range(4)]
                                D_ = Drained(v)
                                D_.tensor_tensor(tmp, aa[0], aa[1], ALU.is_ge)
                                D_.tensor_copy(s0, tmp)
                                D_.tensor_scalar(s1, tmp, -1.0, None, ALU.mult)
                                D_.tensor_tensor(tmp, aa[0], aa[2], ALU.is_ge)
                                D_.tensor_add(s0, s0, tmp)
                                D_.tensor_scalar(s2, tmp, -1.0, None, ALU.mult)
                                D_.tensor_tensor(tmp, aa[0], aa[3], ALU.is_ge)
                                D_.tensor_add(s0, s0, tmp)
                                D_.tensor_scalar(s3, tmp, -1.0, None, ALU.mult)
                                D_.tensor_tensor(tmp, aa[1], aa[2], ALU.is_ge)
                                D_.tensor_add(s1, s1, tmp)
                                D_.tensor_sub(s2, s2, tmp)
                                D_.tensor_tensor(tmp, aa[1], aa[3], ALU.is_ge)
                                D_.tensor_add(s1, s1, tmp)
                                D_.tensor_sub(s3, s3, tmp)
                                D_.tensor_tensor(tmp, aa[2], aa[3], ALU.is_ge)
                                D_.tensor_add(s2, s2, tmp)
                                D_.tensor_sub(s3, s3, tmp)
                                D_.tensor_single_scalar(s0, s0, 1.5, ALU.is_ge)
                                D_.tensor_single_scalar(s1, s1, 0.5, ALU.is_ge)
                                D_.tensor_single_scalar(s2, s2, -0.5, ALU.is_ge)
                                D_.tensor_single_scalar(s3, s3, -1.5, ALU.is_ge)
                                D_.tensor_tensor(hh[0], hh[0], s0, ALU.mult)
                                D_.tensor_tensor(hh[1], hh[1], s1, ALU.mult)
                                D_.tensor_tensor(hh[2], hh[2], s2, ALU.mult)
                                v.tensor_tensor(hh[3], hh[3], s3, ALU.mult).then_inc(sV.h, 1)
                                DRAIN_DVE and v.drain()

                    for g in range(NG + 1):
                        plan_slot(g) if g < NG else None
                        if g == NG:
                            n = sP.n
                            for tt in range(NT):
                                n += 1; pe_tr[(NG - 1, tt)] = n
                            for idx in range(2 * DC):
                                n += 1; pe_dn[(NG - 1, idx)] = n
                            pe_group_end[NG - 1] = n
                            sP.n = n
                            n = sV.n
                            for tt in range(NT):
                                n += 1; v_trv[(NG - 1, tt)] = n
                            for idx in range(2 * DC):
                                n += 1; v_dnv[(NG - 1, idx)] = n
                            sV.n = n
                        block.tensor(lambda t, g=g: ten_slot(t, g))
                        block.scalar(lambda a, g=g: sca_slot(a, g))
                        block.vector(lambda v, g=g: vec_slot(v, g))
                        if g + 1 < NG:
                            block.gpsimd(lambda e, g=g: load_gu(e, g + 1, pe_gu_end[g]))
                        if g < NG and g >= 1:
                            block.gpsimd(lambda e, g=g: load_d(
                                e, g, pe_group_end[g - 1]))
                    v_ffn = v_dnv[(NG - 1, 2 * DC - 1)]
                    _scF.__exit__(None, None, None)


                _scR = nc.named_scope(tag + "_ln2"); _scR.__enter__()
                vres = []
                for dc in range(DC):
                    def res_s(s, dc=dc, guard=(vres[dc - 2] if dc >= 2 else 0)):
                        if dc == 0:
                            s.wait_ge(sV.h, v_ffn)
                            s.wait_ge(sScr.h, scr_w)
                        if dc >= 2:
                            s.wait_ge(sV.h, guard)
                        sXr[dc % 2].inc(s.dma_start(out=xstr[dc % 2][:, :],
                                        in_=x1scr[dc * 128:(dc + 1) * 128, :]), 16)
                    block.sync(res_s)
                    n_ws = sXr[dc % 2].n

                    def res_v(v, dc=dc, n_ws=n_ws):
                        v.wait_ge(sXr[dc % 2].h, n_ws)
                        c = xsum[:, dc * TPC:(dc + 1) * TPC]
                        sV.inc(v.tensor_add(c, c, xstr[dc % 2][:, :]), 1)
                    VEC(res_v)
                    vres.append(sV.n)
                v_res = sV.n

                pe_bc2 = emit_ln(xsum, v_res)

                for dc in range(DC):
                    def ap2_v(v, dc=dc, pe_bc2=pe_bc2):
                        if dc == 0:
                            v.wait_ge(sP.h, pe_bc2)
                        c = xsum[:, dc * TPC:(dc + 1) * TPC]
                        v.tensor_sub(c, c, pA[:, :])
                        sV.inc(v.tensor_mul(c, c, pB[:, :]), 1)
                    VEC(ap2_v)
                    if out_mode == "x2T":
                        def o_s(s, dc=dc, v_n=sV.n):
                            s.wait_ge(sV.h, v_n)
                            sO.inc(s.dma_start(out=out[dc * 128:(dc + 1) * 128, :],
                                               in_=xsum[:, dc * TPC:(dc + 1) * TPC]), 16)
                        block.sync(o_s)
                    elif out_mode == "x2_cc":
                        def o_g(g, dc=dc, v_n=sV.n):
                            g.wait_ge(sV.h, v_n)
                            sO.inc(g.dma_start(out=out[dc * 128:(dc + 1) * 128, :],
                                               in_=xsum[:, dc * TPC:(dc + 1) * TPC]), 16)
                        block.gpsimd(o_g)
                v_x2 = sV.n

                if final_tm:
                    with nc.sbuf_tensor(tag + "ytile", [128, D], F32) as ytile:
                        v_yt = []
                        for tt in range(NT):
                            def ftr_t(t, tt=tt, v_x2=v_x2):
                                if tt == 0:
                                    t.wait_ge(sV.h, v_x2)
                                else:
                                    t.wait_ge(sV.h, v_yt[-1])
                                mmi = None
                                for dc in range(DC):
                                    bank = pA if dc < 8 else pB
                                    mmi = t.matmul(bank[:, (dc % 8) * 128:(dc % 8 + 1) * 128],
                                                   xsum[:, dc * TPC + tt * 128:dc * TPC + tt * 128 + 128],
                                                   identf[:, :], start=True, stop=True)
                                sP.inc(mmi, 1)
                            block.tensor(ftr_t)
                            pe_ftr = sP.n

                            def ftr_v(v, pe_ftr=pe_ftr, guard=sO.n):
                                v.wait_ge(sP.h, pe_ftr)
                                v.wait_ge(sO.h, guard)
                                v.tensor_copy(ytile[:, 0:1024], pA[:, :])
                                sV.inc(v.tensor_copy(ytile[:, 1024:2048], pB[:, :]), 1)
                            VEC(ftr_v)
                            v_yt.append(sV.n)

                            def ftr_s(s, tt=tt, v_n=sV.n):
                                s.wait_ge(sV.h, v_n)
                                sO.inc(s.dma_start(out=out[tt * 128:(tt + 1) * 128, :],
                                                   in_=ytile[:, :]), 16)
                            block.sync(ftr_s)
                _scR.__exit__(None, None, None)

    return {"pe": (sP.h, sP.n), "act": (sA.h, sA.n), "dve": (sV.h, sV.n),
            "out": (sO.h, sO.n)}



def build_p2(nblk=16):
    def io_builder(nc):
        x2T = nc.dram_tensor("x2T", [D, T], F32, kind="ExternalInput").ap()
        return dict(
            xblk_in=lambda grp, blk: x2T[512 * grp:512 * (grp + 1),
                                         blk * 512:(blk + 1) * 512].rearrange("(k p) t -> p k t", p=128),
            wq=nc.dram_tensor("wq", [128, DC * HPC * DK], F32, kind="ExternalInput").ap(),
            wk=nc.dram_tensor("wk", [128, DC * DK], F32, kind="ExternalInput").ap(),
            wv=nc.dram_tensor("wv", [128, DC * DK], F32, kind="ExternalInput").ap(),
            ropec=nc.dram_tensor("ropec", [128, T], F32, kind="ExternalInput").ap(),
            ropes=nc.dram_tensor("ropes", [128, T], F32, kind="ExternalInput").ap(),
            ident=nc.dram_tensor("ident", [128, 128], BF16, kind="ExternalInput").ap(),
            dmask=nc.dram_tensor("dmask", [128, 4 * 512], BF16, kind="ExternalInput").ap(),
            negc=nc.dram_tensor("negc", [128, 1], F32, kind="ExternalInput").ap(),
            att1=nc.dram_tensor("att1", [HPC * DK, T], BF16, kind="ExternalOutput").ap(),
            cc_wait=None,
        )
    return _standalone(lambda E, io, entry: emit_p2(E, io, entry, nblk=nblk), io_builder)


def emit_p2(E, io, entry, nblk=16, tag="p2"):
    nc, block, ctx = E.nc, E.block, E.ctx
    wq, wk, wv = io["wq"], io["wk"], io["wv"]
    cos_in, sin_in = io["ropec"], io["ropes"]
    ident_in, dmask_in, negc_in = io["ident"], io["dmask"], io["negc"]
    att1 = io["att1"]

    NQT = nblk * 4
    NQG = nblk

    with contextlib.ExitStack() as pctx:
        sb = lambda name, shape, dt: pctx.enter_context(nc.sbuf_tensor(tag + name, shape, dt))

        wq_sb = sb("wq_sb", [128, DC * HPC * DK], BF16)
        wk_sb = sb("wk_sb", [128, DC * DK], BF16)
        wv_sb = sb("wv_sb", [128, DC * DK], BF16)
        xblk = [sb(f"xblk{i}", [128, DC * 512], BF16) for i in range(2)]
        ropeb = [sb(f"ropeb{i}", [128, 1024], F32) for i in range(2)]
        ident = sb("ident_sb", [128, 128], BF16)
        dmask = sb("dmask_sb", [128, 4 * 512], BF16)
        negc = sb("negc_sb", [128, 1], F32)
        qsb = [sb(f"q{h}_sb", [128, T], BF16) for h in range(HPC)]
        ksb = sb("k_sb", [128, T], BF16)
        v_aug = sb("v_aug", [128, NQT * 129], BF16)
        vtmp = sb("vtmp", [128, 512], BF16)
        tr1 = sb("tr1", [128, 512], F32)
        tr2 = sb("tr2", [128, 512], F32)
        tr3 = sb("tr3", [128, 512], F32)
        ptile = [sb(f"ptile{i}", [128, 1024], BF16) for i in range(2)]
        ybf = sb("ybf", [128, 128], BF16)
        att_tile = sb("att_tile", [128, 512], BF16)
        rc = sb("rc", [128, 2], F32)

        pS = [pctx.enter_context(nc.psum_tensor(tag + f"pS{i}", [128, 1024], F32)) for i in range(2)]
        pY = [pctx.enter_context(nc.psum_tensor(tag + f"pY{i}", [128, 512], F32)) for i in range(4)]
        pQ, pV, pT = pY[0], pY[1], pY[2]

        sLw = E.sem(tag + "sLw")
        sXB = [E.sem(tag + "sXB0"), E.sem(tag + "sXB1")]
        sRC = [E.sem(tag + "sRC0"), E.sem(tag + "sRC1")]
        sWi = E.sem(tag + "sWi")
        sP = E.sem(tag + "sP")
        sA = E.sem(tag + "sA")
        sV = E.sem(tag + "sV")
        sO = E.sem(tag + "sO")

        def load_xblk(g, blk, buf):
            if io["cc_wait"] is not None and blk == 0:
                g.wait_ge(io["cc_wait"][0], io["cc_wait"][1])
            for grp in range(4):
                sXB[buf].inc(g.dma_start(
                    out=xblk[buf][:, :].rearrange("p (k t) -> p k t", t=512)[:, 4 * grp:4 * grp + 4, :],
                    in_=io["xblk_in"](grp, blk)), 16)

        if True:
            VEC, SCA = E.VEC, E.SCA

            XQ = block.sync if io.get("xblk_sync") else block.gpsimd

            def init_g(g):
                for se, va in entry:
                    g.wait_ge(se, va)
                sLw.inc(g.dma_start(out=wq_sb[:, :], in_=wq), 16)
                sLw.inc(g.dma_start(out=wk_sb[:, :], in_=wk), 16)
                sLw.inc(g.dma_start(out=wv_sb[:, :], in_=wv), 16)
            block.gpsimd(init_g)
            XQ(lambda e: load_xblk(e, 0, 0))

            def load_rope(s, blk):
                buf = blk % 2
                sRC[buf].inc(s.dma_start(out=ropeb[buf][:, 0:512],
                                         in_=cos_in[:, blk * 512:(blk + 1) * 512]), 16)
                sRC[buf].inc(s.dma_start(out=ropeb[buf][:, 512:1024],
                                         in_=sin_in[:, blk * 512:(blk + 1) * 512]), 16)

            def init_s(s):
                for se, va in entry:
                    s.wait_ge(se, va)
                load_rope(s, 0)
                load_rope(s, 1)
                sWi.inc(s.dma_start(out=ident[:, :], in_=ident_in), 16)
                sWi.inc(s.dma_start(out=dmask[:, :], in_=dmask_in), 16)
                sWi.inc(s.dma_start(out=negc[:, :], in_=negc_in), 16)
            block.sync(init_s)
            n_wi = sWi.n

            def init_v(v):
                for se, va in entry:
                    v.wait_ge(se, va)
                sV.inc(v.memset(v_aug[:, :], 1.0), 1)
            VEC(init_v)
            v_init = sV.n

            _sc1 = nc.named_scope(tag + "_proj"); _sc1.__enter__()
            last_pev = 0
            a_evac = 0
            v_rope = 0
            for blk in range(nblk):
                buf = blk % 2
                xv = xblk[buf][:, :]
                if blk + 1 < nblk:
                    def pre_g(g, blk=blk, guard=last_pev):
                        if blk >= 1:
                            g.wait_ge(sP.h, guard)
                        load_xblk(g, blk + 1, (blk + 1) % 2)
                    XQ(pre_g)

                for which in range(4):
                    def proj_t(t, blk=blk, which=which, xv=xv, g_a=a_evac, g_v=v_rope):
                        if which == 0:
                            t.wait_ge(sXB[blk % 2].h, 64 * (blk // 2 + 1))
                            if blk == 0:
                                t.wait_ge(sLw.h, sLw.n)
                                t.wait_ge(sWi.h, n_wi)
                                t.wait_ge(sV.h, v_init)
                        t.wait_ge(sA.h, g_a)
                        mmi = None
                        if which < 3:
                            wsb, col = ((wq_sb, which * DK) if which < 2 else (wk_sb, 0))
                            for dc in range(DC):
                                base = dc * (HPC * DK if which < 2 else DK) + col
                                mmi = t.matmul(pQ[:, :], wsb[:, base:base + DK],
                                               xv[:, dc * 512:(dc + 1) * 512],
                                               start=(dc == 0), stop=(dc == DC - 1))
                            sP.inc(mmi, 1)
                        else:
                            for dc in range(DC):
                                mmi = t.matmul(pV[:, :], wv_sb[:, dc * DK:dc * DK + DK],
                                               xv[:, dc * 512:(dc + 1) * 512],
                                               start=(dc == 0), stop=(dc == DC - 1))
                            sP.inc(mmi, 1)
                    block.tensor(proj_t)
                    pe_p = sP.n
                    if which == 3:
                        last_pev = pe_p

                    if which < 3:
                        def rope_a(a, pe_p=pe_p, g_v=v_rope):
                            a.wait_ge(sP.h, pe_p)
                            a.wait_ge(sV.h, g_v)
                            sA.inc(a.activation(out=tr1[:, :], in_=pQ[:, :], func=AF.Copy), 1)
                        SCA(rope_a)
                        a_evac = sA.n

                        def rope_v(v, blk=blk, which=which, a_n=sA.n):
                            v.wait_ge(sA.h, a_n)
                            if which == 0:
                                v.wait_ge(sRC[blk % 2].h, 32 * (blk // 2 + 1))
                            v.tensor_copy(tr2[0:64, :], tr1[64:128, :])
                            v.tensor_copy(tr2[64:128, :], tr1[0:64, :])
                            cs = ropeb[blk % 2][:, 0:512]
                            sn = ropeb[blk % 2][:, 512:1024]
                            v.tensor_mul(tr3[:, :], tr1[:, :], cs)
                            v.tensor_mul(tr2[:, :], tr2[:, :], sn)
                            dst = (qsb[which] if which < 2 else ksb)[:, blk * 512:(blk + 1) * 512]
                            sV.inc(v.tensor_add(dst, tr3[:, :], tr2[:, :]), 1)
                        VEC(rope_v)
                        v_rope = sV.n
                        if which == 2 and blk + 2 < nblk:
                            def rpre_s(s, blk=blk, guard=v_rope):
                                s.wait_ge(sV.h, guard)
                                load_rope(s, blk + 2)
                            block.sync(rpre_s)
                    if which == 3:
                        def v_a(a, pe_p=pe_p):
                            a.wait_ge(sP.h, pe_p)
                            sA.inc(a.activation(out=vtmp[:, :], in_=pV[:, :], func=AF.Copy), 1)
                        SCA(v_a)
                        a_evac = sA.n

                        def vt_t(t, a_n=sA.n, g_v=v_rope):
                            t.wait_ge(sA.h, a_n)
                            t.wait_ge(sV.h, g_v)
                            mmi = None
                            for j in range(4):
                                mmi = t.matmul(pT[:, j * 128:(j + 1) * 128],
                                               vtmp[:, j * 128:(j + 1) * 128], ident[:, :],
                                               start=True, stop=True)
                            sP.inc(mmi, 1)
                        block.tensor(vt_t)
                        pe_vt = sP.n

                        def vt_v(v, blk=blk, pe_vt=pe_vt):
                            v.wait_ge(sP.h, pe_vt)
                            dst = v_aug[:, :].rearrange("p (n c) -> p n c", c=129)[:, 4 * blk:4 * blk + 4, 0:128]
                            src = pT[:, :].rearrange("p (n c) -> p n c", c=128)
                            sV.inc(v.tensor_copy(dst, src), 1)
                        VEC(vt_v)
                        v_rope = sV.n
            v_stage1 = sV.n
            pe_stage1 = sP.n
            _sc1.__exit__(None, None, None)
            _sc2 = nc.named_scope(tag + "_attn"); _sc2.__enter__()

            ISQ = ISQRT_DK
            v_att = 0
            prev_av_last = 0
            for h in range(HPC):
                for qg in range(NQG):
                    nkb = 4 * qg + 4
                    npair = 2 * qg + 2

                    pe_ops = [("st", 0)]
                    for p in range(npair):
                        if p + 1 < npair:
                            pe_ops.append(("st", p + 1))
                        pe_ops.append(("av", p))
                    pe_st = {}
                    av_done = {}
                    n = sP.n
                    for kind, p in pe_ops:
                        n += 1
                        (pe_st if kind == "st" else av_done)[p] = n
                    pe_ytr = [n + 1 + j for j in range(4)]
                    sP.n = n + 4
                    aexp = {p: sA.n + p + 1 for p in range(npair)}
                    sA.n += npair
                    vmask = {npair - 2: sV.n + 1, npair - 1: sV.n + 2}
                    v_ye = [sV.n + 3 + j for j in range(4)]
                    v_att_new = sV.n + 7
                    sV.n += 7
                    o_prev = sO.n

                    def ten_body(t, h=h, qg=qg, nkb=nkb, npair=npair,
                                 pe_ops=list(pe_ops), aexp=dict(aexp),
                                 vmask=dict(vmask), v_ye=list(v_ye),
                                 v_att=v_att, first=((h, qg) == (0, 0))):
                        qslice = qsb[h][:, qg * 512:(qg + 1) * 512]
                        for kind, p in pe_ops:
                            if kind == "st":
                                if first and p < 2:
                                    t.wait_ge(sP.h, pe_stage1)
                                    t.wait_ge(sV.h, v_stage1)
                                if p >= 2:
                                    t.wait_ge(sA.h, aexp[p - 2])
                                elif v_att:
                                    t.wait_ge(sV.h, v_att)
                                mmi = None
                                for kb in (2 * p, 2 * p + 1):
                                    mmi = t.matmul(
                                        pS[p % 2][:, (kb % 2) * 512:(kb % 2) * 512 + 512],
                                        ksb[:, kb * 128:(kb + 1) * 128], qslice,
                                        start=True, stop=True)
                                mmi.then_inc(sP.h, 1)
                            else:
                                t.wait_ge(sA.h, aexp[p])
                                if p >= npair - 2:
                                    t.wait_ge(sV.h, vmask[p])
                                mmi = None
                                for kb in (2 * p, 2 * p + 1):
                                    for j in range(4):
                                        mmi = t.matmul(
                                            pY[j][:, 0:129],
                                            ptile[p % 2][:, (kb % 2) * 512 + j * 128:(kb % 2) * 512 + j * 128 + 128],
                                            v_aug[:, kb * 129:(kb + 1) * 129],
                                            start=(kb == 0), stop=(kb == nkb - 1))
                                mmi.then_inc(sP.h, 1)
                        for j in range(4):
                            t.wait_ge(sV.h, v_ye[j])
                            t.matmul(pS[1][:, j * 128:(j + 1) * 128], ybf[:, :],
                                     ident[:, :], start=True, stop=True).then_inc(sP.h, 1)
                    block.tensor(ten_body)

                    def sca_body(a, npair=npair, pe_st=dict(pe_st),
                                 av_done=dict(av_done), prev_av_last=prev_av_last):
                        for p in range(npair):
                            a.wait_ge(sP.h, pe_st[p])
                            guard = av_done[p - 2] if p >= 2 else prev_av_last
                            if guard:
                                a.wait_ge(sP.h, guard)
                            a.activation(out=ptile[p % 2][:, :], in_=pS[p % 2][:, :],
                                         func=AF.Exp, scale=ISQ,
                                         bias=negc[:, :]).then_inc(sA.h, 1)
                            DRAIN_ACT and a.drain()
                    block.scalar(sca_body)
                    sca_drains = npair

                    def vec_body(v, h=h, qg=qg, npair=npair, aexp=dict(aexp),
                                 av_last=av_done[npair - 1], pe_ytr=list(pe_ytr),
                                 o_prev=o_prev):
                        for p in (npair - 2, npair - 1):
                            v.wait_ge(sA.h, aexp[p])
                            inst = None
                            for kb in (2 * p, 2 * p + 1):
                                r = kb - 4 * qg
                                if 0 <= r < 4:
                                    inst = v.tensor_mul(
                                        ptile[p % 2][:, (kb % 2) * 512:(kb % 2) * 512 + 512],
                                        ptile[p % 2][:, (kb % 2) * 512:(kb % 2) * 512 + 512],
                                        dmask[:, r * 512:(r + 1) * 512])
                                    DRAIN_DVE and v.drain()
                            inst.then_inc(sV.h, 1)
                        for j in range(4):
                            v.wait_ge(sP.h, av_last)
                            if j > 0:
                                v.wait_ge(sP.h, pe_ytr[j - 1])
                            v.reciprocal(rc[:, 0:1], pY[j][:, 128:129])
                            DRAIN_DVE and v.drain()
                            v.tensor_scalar(ybf[:, :], pY[j][:, 0:128], rc[:, 0:1],
                                            None, ALU.mult).then_inc(sV.h, 1)
                            DRAIN_DVE and v.drain()
                        v.wait_ge(sP.h, pe_ytr[3])
                        v.wait_ge(sO.h, o_prev)
                        v.tensor_copy(att_tile[:, :], pS[1][:, 0:512]).then_inc(sV.h, 1)
                        DRAIN_DVE and v.drain()
                    block.vector(vec_body)

                    def out_s(s, h=h, qg=qg, v_n=v_att_new):
                        s.wait_ge(sV.h, v_n)
                        sO.inc(s.dma_start(
                            out=att1[h * DK:(h + 1) * DK, qg * 512:(qg + 1) * 512],
                            in_=att_tile[:, :]), 16)
                    block.sync(out_s)

                    v_att = v_att_new
                    prev_av_last = av_done[npair - 1]
            _sc2.__exit__(None, None, None)

    return {"pe": (sP.h, sP.n), "act": (sA.h, sA.n), "dve": (sV.h, sV.n),
            "out": (sO.h, sO.n)}



def build_fused():
    nc = bass.Bass("TRN2", target_bir_lowering=False, debug=False)
    I = lambda n, shp, dt=F32: nc.dram_tensor(n, shp, dt, kind="ExternalInput").ap()
    x0s = I("x0s", [128, NSEG * DC * SEG])
    wq0, wk0, wv0 = I("wq0", [128, DC * 256]), I("wk0", [128, DC * 256]), I("wv0", [128, DC * 256])
    betas = I("betas", [1, 256])
    wo0, wg0 = I("wo0", [128, DC * D]), I("wg0", [128, NG * DC * GSZ])
    wu0, wd0 = I("wu0", [128, NG * DC * GSZ]), I("wd0", [128, NG * GC * D])
    wq1, wk1, wv1 = I("wq1", [128, DC * 256]), I("wk1", [128, DC * DK]), I("wv1", [128, DC * DK])
    ropec, ropes = I("ropec", [128, T]), I("ropes", [128, T])
    wo1, wg1 = I("wo1", [128, DC * D]), I("wg1", [128, NG * DC * GSZ])
    wu1, wd1 = I("wu1", [128, NG * DC * GSZ]), I("wd1", [128, NG * GC * D])
    xres0 = I("xres", [D, TPC])
    ident = I("ident", [128, 128], BF16)
    identf = I("identf", [128, 128], F32)
    binmask = I("binmask", [SEG, SEG], BF16)
    dmask = I("dmask", [128, 4 * 512], BF16)
    negc = I("negc", [128, 1], F32)
    eps = I("epsrow", [1, 1], F32)
    yout = nc.dram_tensor("yout", [TPC, D], F32, kind="ExternalOutput").ap()

    cc_a0_in = nc.dram_tensor("cc_a0_in", [256, T], BF16).ap()
    cc_a0_out = nc.dram_tensor("cc_a0_out", [D, T], BF16, addr_space="Shared").ap()
    cc_x2_in = nc.dram_tensor("cc_x2_in", [D, TPC], BF16).ap()
    cc_x2_out = nc.dram_tensor("cc_x2_out", [N_CORES * D, TPC], BF16, addr_space="Shared").ap()
    cc_a1_in = nc.dram_tensor("cc_a1_in", [256, T], BF16).ap()
    cc_a1_out = nc.dram_tensor("cc_a1_out", [D, T], BF16, addr_space="Shared").ap()
    x1scr = nc.dram_tensor("x1scr", [D, TPC], F32).ap()
    x3scr = nc.dram_tensor("x3scr", [D, TPC], F32).ap()

    RG = [list(range(N_CORES))]
    ctx = contextlib.ExitStack()
    with ctx:
        xsum = ctx.enter_context(nc.sbuf_tensor("xsum_shared", [128, DC * TPC], F32))
        block = ctx.enter_context(nc.Block())
        E = Env(nc, block, ctx)
        sCC = E.sem("sCC")

        with nc.named_scope("P0"):
            ex0 = emit_p0(E, dict(x0s=x0s, wq=wq0, wk=wk0, wv=wv0, betas=betas,
                                  ident=ident, binmask=binmask, negc=negc,
                                  att0=cc_a0_in), [], tag="p0")

        def cc1_g(g):
            g.wait_ge(*ex0["out"])
            sCC.inc(g.collective_compute(
                "AllGather", ALU.bypass, replica_groups=RG,
                ins=[cc_a0_in.opt()], outs=[cc_a0_out.opt()]), 1)
        block.gpsimd(cc1_g)

        entry1 = [ex0["pe"], ex0["act"], ex0["dve"], ex0["out"]]
        io1 = dict(attc_mode="gather_scramble", attc_g=cc_a0_out,
                   cc_wait=(sCC.h, 1),
                   xres_chunk=lambda s, dc: xres0[dc * 128:(dc + 1) * 128, :],
                   wo=wo0, wg=wg0, wu=wu0, wd=wd0,
                   ident=ident, identf=identf, eps=eps, x1scr=x1scr,
                   xsum_t=xsum, out_mode="x2_cc", out=cc_x2_in)
        with nc.named_scope("P1"):
            ex1 = emit_p1(E, io1, entry1, sparsify=True, final_tm=False, tag="p1")

        def cc2_g(g):
            g.wait_ge(*ex1["out"])
            sCC.inc(g.collective_compute(
                "AllGather", ALU.bypass, replica_groups=RG,
                ins=[cc_x2_in.opt()], outs=[cc_x2_out.opt()]), 1)
        block.gpsimd(cc2_g)

        entry2 = [ex1["pe"], ex1["act"], ex1["dve"], ex1["out"]]
        x2g = cc_x2_out.rearrange("(r d) t -> r d t", d=D)
        io2 = dict(
            xblk_in=lambda grp, blk: x2g[blk // 2, 512 * grp:512 * (grp + 1),
                                         (blk % 2) * 512:(blk % 2) * 512 + 512
                                         ].rearrange("(k p) t -> p k t", p=128),
            wq=wq1, wk=wk1, wv=wv1, ropec=ropec, ropes=ropes,
            ident=ident, dmask=dmask, negc=negc, att1=cc_a1_in,
            cc_wait=(sCC.h, 2), xblk_sync=True)
        with nc.named_scope("P2"):
            ex2 = emit_p2(E, io2, entry2, tag="p2")

        def cc3_g(g):
            g.wait_ge(*ex2["out"])
            sCC.inc(g.collective_compute(
                "AllGather", ALU.bypass, replica_groups=RG,
                ins=[cc_a1_in.opt()], outs=[cc_a1_out.opt()]), 1)
        block.gpsimd(cc3_g)

        entry3 = [ex2["pe"], ex2["act"], ex2["dve"], ex2["out"]]
        io3 = dict(attc_mode="gather", attc_g=cc_a1_out, cc_wait=(sCC.h, 3),
                   xres_chunk=None,
                   wo=wo1, wg=wg1, wu=wu1, wd=wd1,
                   ident=ident, identf=identf, eps=eps, x1scr=x3scr,
                   xsum_t=xsum, out_mode="yout", out=yout)
        with nc.named_scope("P3"):
            ex3 = emit_p1(E, io3, entry3, sparsify=False, final_tm=True, tag="p3")

        osem, oval = ex3["out"]
        def fin_s(s):
            s.wait_ge(osem, oval)
        block.sync(fin_s)
    return nc




def _prearr_dmat(w):
    Dd, F = w.shape
    K = Dd // 128
    return np.ascontiguousarray(w.reshape(K, 128, F).transpose(1, 0, 2).reshape(128, K * F))


def _prearr_ffn(w):
    return np.ascontiguousarray(
        w.reshape(DC, 128, NG, GSZ).transpose(1, 2, 0, 3).reshape(128, NG * DC * GSZ))


def _prearr_wd(w):
    return np.ascontiguousarray(
        w.reshape(NG, GC, 128, D).transpose(2, 0, 1, 3).reshape(128, NG * GC * D))


def _prearr_x0(x0):
    return np.ascontiguousarray(
        x0.reshape(NSEG, SEG, DC, 128).transpose(3, 0, 2, 1).reshape(128, NSEG * DC * SEG))


def _scramble_fm(att_fm):
    A = att_fm.reshape(16, 128, NSEG, 32, 16)
    C = A.transpose(4, 1, 2, 0, 3)
    return np.ascontiguousarray(C.reshape(D, T))


def _p2_consts():
    inv = np.power(10000.0, -2.0 * np.arange(64) / 128.0)
    ang = np.arange(T)[:, None].astype(np.float64) * inv[None, :]
    cosT = np.ascontiguousarray(np.tile(np.cos(ang), (1, 2)).T.astype(np.float32))
    sinT = np.tile(np.sin(ang), (1, 2)).T.astype(np.float32)
    sinT[:64, :] *= -1.0
    sinT = np.ascontiguousarray(sinT)
    p = np.arange(128)[:, None]
    c = np.arange(512)[None, :]
    dmask = np.zeros((128, 4 * 512), BF)
    for r in range(4):
        dmask[:, r * 512:(r + 1) * 512] = (c >= r * 128 + p).astype(BF)
    negc = np.full((128, 1), -CSHIFT, np.float32)
    return cosT, sinT, dmask, negc


def kernel(L0_attn_x, L0_attn_Wq, L0_attn_Wk, L0_attn_Wv, L0_attn_Wo, L0_attn_betas,
           L0_ffn_w_gate, L0_ffn_w_up, L0_ffn_w_down,
           L1_attn_Wq, L1_attn_Wk, L1_attn_Wv, L1_attn_Wo,
           L1_ffn_w_gate, L1_ffn_w_up, L1_ffn_w_down):
    cores = list(range(N_CORES))
    x0T = np.ascontiguousarray(np.asarray(L0_attn_x)[0].T)
    ident = np.eye(128, dtype=BF)
    identf = np.eye(128, dtype=np.float32)
    eps = np.full((1, 1), LN_EPS, np.float32)
    _, binmask, negc = _consts_p0()
    cosT, sinT, dmask, _ = _p2_consts()
    wg1 = np.ascontiguousarray(L1_ffn_w_gate.T)
    wu1 = np.ascontiguousarray(L1_ffn_w_up.T)
    wd1 = np.ascontiguousarray(L1_ffn_w_down.T)
    betas16 = L0_attn_betas.reshape(16, DK)
    x0s = _prearr_x0(np.asarray(L0_attn_x)[0])
    wo0p, wo1p = _prearr_dmat(L0_attn_Wo), _prearr_dmat(L1_attn_Wo)
    wg0p, wu0p = _prearr_ffn(L0_ffn_w_gate), _prearr_ffn(L0_ffn_w_up)
    wd0p = _prearr_wd(L0_ffn_w_down)
    wg1p, wu1p, wd1p = _prearr_ffn(wg1), _prearr_ffn(wu1), _prearr_wd(wd1)
    maps = []
    for c in cores:
        sl = slice(c * TPC, (c + 1) * TPC)
        hs = slice(c * 256, (c + 1) * 256)
        kv = slice((c // 2) * 128, (c // 2 + 1) * 128)
        maps.append({
            "x0s": x0s,
            "xres": np.ascontiguousarray(x0T[:, sl]),
            "wq0": _prearr_dmat(L0_attn_Wq[:, hs]),
            "wk0": _prearr_dmat(L0_attn_Wk[:, hs]),
            "wv0": _prearr_dmat(L0_attn_Wv[:, hs]),
            "betas": np.ascontiguousarray(betas16[2 * c:2 * c + 2, :].reshape(1, 256)),
            "wo0": wo0p, "wg0": wg0p, "wu0": wu0p, "wd0": wd0p,
            "wq1": _prearr_dmat(L1_attn_Wq[:, hs]),
            "wk1": _prearr_dmat(L1_attn_Wk[:, kv]),
            "wv1": _prearr_dmat(L1_attn_Wv[:, kv]),
            "ropec": cosT, "ropes": sinT,
            "wo1": wo1p, "wg1": wg1p, "wu1": wu1p, "wd1": wd1p,
            "ident": ident, "identf": identf, "binmask": binmask,
            "dmask": dmask, "negc": negc, "epsrow": eps,
        })
    res = run_bass_kernel_spmd(build_fused(), maps, core_ids=cores)
    y = np.concatenate([r["yout"] for r in res.results], axis=0)
    return y.reshape(1, T, D)


def _kernel_unfused(L0_attn_x, L0_attn_Wq, L0_attn_Wk, L0_attn_Wv, L0_attn_Wo, L0_attn_betas,
           L0_ffn_w_gate, L0_ffn_w_up, L0_ffn_w_down,
           L1_attn_Wq, L1_attn_Wk, L1_attn_Wv, L1_attn_Wo,
           L1_ffn_w_gate, L1_ffn_w_up, L1_ffn_w_down):
    cores = list(range(N_CORES))
    x0T = np.ascontiguousarray(np.asarray(L0_attn_x)[0].T)
    ident = np.eye(128, dtype=BF)
    identf = np.eye(128, dtype=np.float32)
    eps = np.full((1, 1), LN_EPS, np.float32)

    res0 = run_bass_kernel_spmd(
        build_p0(), _p0_in_maps(x0T, L0_attn_Wq, L0_attn_Wk, L0_attn_Wv, L0_attn_betas),
        core_ids=cores)
    att0 = np.concatenate([r["att0"] for r in res0.results], axis=0)
    att0s = _scramble_fm(att0)

    maps1 = []
    for c in cores:
        sl = slice(c * TPC, (c + 1) * TPC)
        maps1.append({
            "attc": np.ascontiguousarray(att0s[:, sl]),
            "xres": np.ascontiguousarray(x0T[:, sl]),
            "wo": L0_attn_Wo, "wg": L0_ffn_w_gate, "wu": L0_ffn_w_up,
            "wd": L0_ffn_w_down,
            "ident": ident, "identf": identf, "epsrow": eps,
        })
    res1 = run_bass_kernel_spmd(build_p1(True, False), maps1, core_ids=cores)
    x2T = np.concatenate([r["x2T"] for r in res1.results], axis=1)

    cosT, sinT, dmask, negc = _p2_consts()
    maps2 = []
    for c in cores:
        kv = c // 2
        maps2.append({
            "x2T": x2T,
            "wq": np.ascontiguousarray(L1_attn_Wq[:, c * 256:(c + 1) * 256]),
            "wk": np.ascontiguousarray(L1_attn_Wk[:, kv * 128:(kv + 1) * 128]),
            "wv": np.ascontiguousarray(L1_attn_Wv[:, kv * 128:(kv + 1) * 128]),
            "ropec": cosT, "ropes": sinT,
            "ident": ident, "dmask": dmask, "negc": negc,
        })
    res2 = run_bass_kernel_spmd(build_p2(), maps2, core_ids=cores)
    att1 = np.concatenate([r["att1"] for r in res2.results], axis=0)

    wg1 = np.ascontiguousarray(L1_ffn_w_gate.T)
    wu1 = np.ascontiguousarray(L1_ffn_w_up.T)
    wd1 = np.ascontiguousarray(L1_ffn_w_down.T)
    maps3 = []
    for c in cores:
        sl = slice(c * TPC, (c + 1) * TPC)
        maps3.append({
            "attc": np.ascontiguousarray(att1[:, sl]),
            "xres": np.ascontiguousarray(x2T[:, sl]),
            "wo": L1_attn_Wo, "wg": wg1, "wu": wu1, "wd": wd1,
            "ident": ident, "identf": identf, "epsrow": eps,
        })
    res3 = run_bass_kernel_spmd(build_p1(False, True), maps3, core_ids=cores)
    y = np.concatenate([r["yout"] for r in res3.results], axis=0)
    return y.reshape(1, T, D)


# version 12
# speedup vs baseline: 1.1188x; 1.1188x over previous
import contextlib
import numpy as np
import ml_dtypes
import concourse.bass as bass
import concourse.mybir as mybir
from concourse.bass_utils import run_bass_kernel_spmd

F32 = mybir.dt.float32
BF16 = mybir.dt.bfloat16
AF = mybir.ActivationFunctionType
ALU = mybir.AluOpType
AX = mybir.AxisListType
BF = ml_dtypes.bfloat16

N_CORES = 8
D = 2048
T = 8192
DC = D // 128
SEG = 512
NSEG = T // SEG
DK = 128
HPC = 2
CSHIFT = 20.0
ISQRT_DK = float(1.0 / np.sqrt(np.float32(DK)))
TPC = T // N_CORES


DRAIN_DVE = True
DRAIN_ACT = False


class Drained:
    _PASSTHRU = {"wait_ge", "drain", "nop", "wait_op"}

    def __init__(self, eng):
        self._e = eng

    def __getattr__(self, name):
        f = getattr(self._e, name)
        en = DRAIN_ACT if type(self._e).__name__.endswith("ScalarEngine") else DRAIN_DVE
        if name in Drained._PASSTHRU or not callable(f) or not en:
            return f
        def wrapped(*a, **k):
            inst = f(*a, **k)
            self._e.drain()
            return inst
        return wrapped


class Sem:

    def __init__(self, ctx, nc, name):
        self.h = ctx.enter_context(nc.semaphore(name))
        self.n = 0

    def inc(self, inst, k=1):
        inst.then_inc(self.h, k)
        self.n += k
        return self.n

    def wait(self, eng, n=None):
        eng.wait_ge(self.h, self.n if n is None else n)



class Env:

    def __init__(self, nc, block, ctx):
        self.nc, self.block, self.ctx = nc, block, ctx
        self.VEC = lambda f: block.vector(lambda v: f(Drained(v)))
        self.SCA = lambda f: block.scalar(lambda a: f(Drained(a)))

    def sem(self, name):
        return Sem(self.ctx, self.nc, name)


def _standalone(emit, io_builder):
    nc = bass.Bass("TRN2", target_bir_lowering=False, debug=False)
    ctx = contextlib.ExitStack()
    with ctx:
        io = io_builder(nc)
        block = ctx.enter_context(nc.Block())
        E = Env(nc, block, ctx)
        exits = emit(E, io, [])
        osem, oval = exits["out"]
        def fin_s(s):
            s.wait_ge(osem, oval)
        block.sync(fin_s)
    return nc


def build_p0(nseg=NSEG):
    def io_builder(nc):
        return dict(
            x0s=nc.dram_tensor("x0s", [128, NSEG * DC * SEG], F32, kind="ExternalInput").ap(),
            wq=nc.dram_tensor("wq", [128, DC * HPC * DK], F32, kind="ExternalInput").ap(),
            wk=nc.dram_tensor("wk", [128, DC * HPC * DK], F32, kind="ExternalInput").ap(),
            wv=nc.dram_tensor("wv", [128, DC * HPC * DK], F32, kind="ExternalInput").ap(),
            betas=nc.dram_tensor("betas", [1, HPC * DK], F32, kind="ExternalInput").ap(),
            ident=nc.dram_tensor("ident", [128, 128], BF16, kind="ExternalInput").ap(),
            binmask=nc.dram_tensor("binmask", [SEG, SEG], BF16, kind="ExternalInput").ap(),
            negc=nc.dram_tensor("negc", [128, 1], F32, kind="ExternalInput").ap(),
            att0=nc.dram_tensor("att0", [HPC * DK, T], BF16, kind="ExternalOutput").ap(),
        )
    return _standalone(lambda E, io, entry: emit_p0(E, io, entry, nseg=nseg), io_builder)


def emit_p0(E, io, entry, nseg=NSEG, tag="p0"):
    nc, block, ctx = E.nc, E.block, E.ctx
    x0s, wq, wk, wv = io["x0s"], io["wq"], io["wk"], io["wv"]
    betas, ident_in, binmask_in, negc_in = io["betas"], io["ident"], io["binmask"], io["negc"]
    att0 = io["att0"]
    with contextlib.ExitStack() as pctx:
        sb = lambda name, shape, dt: pctx.enter_context(nc.sbuf_tensor(tag + name, shape, dt))
        ps = lambda name: pctx.enter_context(nc.psum_tensor(tag + name, [128, 512], F32))

        wq_sb = sb("wq_sb", [128, DC * HPC * DK], BF16)
        wk_sb = sb("wk_sb", [128, DC * HPC * DK], BF16)
        wv_sb = sb("wv_sb", [128, DC * HPC * DK], BF16)
        xseg = [sb(f"xseg{i}", [128, DC * SEG], BF16) for i in range(2)]
        ident = sb("ident_sb", [128, 128], BF16)
        binmask = sb("binmask_sb", [128, 4 * SEG], BF16)
        negc = sb("negc_sb", [128, 1], F32)
        ones_f = sb("ones_f", [1, 128], F32)
        beta_f = sb("beta_f", [1, HPC * DK], F32)
        bcast = sb("bcast", [128, HPC * DK], F32)
        mem_f = sb("mem_f", [128, HPC * DK], F32)
        z_f = sb("z_f", [128, HPC], F32)
        memz = sb("memz", [128, HPC * 129], BF16)
        q_sb = sb("q_sb", [128, SEG], BF16)
        k_sb = sb("k_sb", [128, SEG], BF16)
        sq_sb = sb("sq_sb", [128, SEG], BF16)
        sk_sb = sb("sk_sb", [128, SEG], BF16)
        sk_tm = sb("sk_tm", [128, SEG], BF16)
        v_aug = sb("v_aug", [128, 4 * 129], BF16)
        pst = sb("pst", [128, 4 * SEG], BF16)
        ndelta = sb("ndelta", [128, SEG], BF16)
        tq1 = sb("tq1", [128, SEG], F32)
        tq2 = sb("tq2", [128, SEG], F32)
        tk1 = sb("tk1", [128, SEG], F32)
        tk2 = sb("tk2", [128, SEG], F32)
        att_d = sb("att_d", [128, 128], F32)
        att_m = sb("att_m", [128, 128], F32)
        cmb = sb("cmb", [128, 128], F32)
        att_bf = sb("att_bf", [128, 128], BF16)
        att_tile = sb("att_tile", [128, SEG], BF16)
        rc = sb("rc", [128, 4], F32)
        zadd = sb("zadd", [128, 1], F32)

        p_q = ps("p_q")
        p_k = ps("p_k")
        p_v = ps("p_v")
        p_st = ps("p_st")
        p_ad = ps("p_ad")
        p_am = ps("p_am")
        p_tr = ps("p_tr")
        p_mu = ps("p_mu")

        sW = E.sem(tag + "sW")
        sW2 = E.sem(tag + "sW2")
        sX = [E.sem(tag + "sX0"), E.sem(tag + "sX1")]
        sP = E.sem(tag + "sP")
        sA = E.sem(tag + "sA")
        sV = E.sem(tag + "sV")
        sO = E.sem(tag + "sO")

        x0s_v = x0s.rearrange("p (s q) -> p s q", q=DC * SEG)

        def load_xseg(g, seg, buf):
            sX[buf].inc(g.dma_start(out=xseg[buf][:, :], in_=x0s_v[:, seg, :]), 16)

        if True:
            VEC, SCA = E.VEC, E.SCA
            def init_g(g):
                for se, va in entry:
                    g.wait_ge(se, va)
                for w, tsb in ((wq, wq_sb), (wk, wk_sb), (wv, wv_sb)):
                    sW.inc(g.dma_start(out=tsb[:, :], in_=w), 16)
                sW.inc(g.dma_start(out=beta_f[:, :], in_=betas), 16)
                load_xseg(g, 0, 0)
            block.gpsimd(init_g)

            def init_s(s):
                for se, va in entry:
                    s.wait_ge(se, va)
                sW2.inc(s.dma_start(out=ident[:, :], in_=ident_in), 16)
                sW2.inc(s.dma_start(
                    out=binmask[:, :].rearrange("p (k t) -> p k t", t=SEG),
                    in_=binmask_in.rearrange("(k p) t -> p k t", p=128)), 16)
                sW2.inc(s.dma_start(out=negc[:, :], in_=negc_in), 16)
            block.sync(init_s)

            def init_v(v):
                for se, va in entry:
                    v.wait_ge(se, va)
                v.memset(ones_f[:, :], 1.0)
                v.memset(mem_f[:, :], 0.0)
                v.memset(z_f[:, :], 1.0 / DK)
                v.memset(memz[:, :], 0.0)
                v.memset(v_aug[:, :], 1.0)
                inst = None
                for h in range(HPC):
                    inst = v.tensor_copy(memz[:, h * 129 + 128:h * 129 + 129], z_f[:, h:h + 1])
                sV.inc(inst, 1)
            VEC(init_v)
            v_init = sV.n

            def init_a(a):
                for se, va in entry:
                    a.wait_ge(se, va)
                a.wait_ge(sW.h, sW.n)
                a.wait_ge(sW2.h, sW2.n)
                sA.inc(a.activation(out=beta_f[:, :], in_=beta_f[:, :], func=AF.Sigmoid), 1)
            SCA(init_a)
            a_sig = sA.n

            def init_t(t):
                t.wait_ge(sA.h, a_sig)
                t.wait_ge(sV.h, v_init)
                for h in range(HPC):
                    mmi = t.matmul(p_mu[:, h * 128:(h + 1) * 128], ones_f[:, :],
                             beta_f[:, h * 128:(h + 1) * 128], start=True, stop=True)
                sP.inc(mmi, 1)
            block.tensor(init_t)
            p_bc = sP.n

            def init_v2(v):
                v.wait_ge(sP.h, p_bc)
                sV.inc(v.tensor_copy(bcast[:, :], p_mu[:, 0:256]), 1)
            VEC(init_v2)

            last = dict(aq=0, ak=0, vsq=0, vsk=0, vvaug=0, vpst=0, vcomb=0,
                        vdl=0, vsktm=0, vatile=0, vmemz=sV.n, pq=0, pev=0,
                        pst3=0, pad=0, pdl=0, pmu=0, out=0)

            it = 0
            for seg in range(nseg):
                buf = seg % 2
                xv = xseg[buf][:, :]

                if seg + 1 < nseg:
                    def pre_g(g, seg=seg, guard=last["pev"]):
                        if seg >= 1:
                            g.wait_ge(sP.h, guard)
                        load_xseg(g, seg + 1, (seg + 1) % 2)
                    block.gpsimd(pre_g)

                for h in range(HPC):
                    it += 1
                    wcol = lambda dc: dc * (HPC * DK) + h * DK

                    def proj_t(t, seg=seg, h=h, xv=xv, g=dict(last)):
                        t.wait_ge(sX[seg % 2].h, 16 * (seg // 2 + 1))
                        t.wait_ge(sA.h, g["ak"])
                        t.wait_ge(sV.h, g["vvaug"])
                        for dc in range(DC):
                            mmi = t.matmul(p_q[:, :], wq_sb[:, wcol(dc):wcol(dc) + DK],
                                     xv[:, dc * SEG:(dc + 1) * SEG],
                                     start=(dc == 0), stop=(dc == DC - 1))
                        sP.inc(mmi, 1)
                        for dc in range(DC):
                            mmi = t.matmul(p_k[:, :], wk_sb[:, wcol(dc):wcol(dc) + DK],
                                     xv[:, dc * SEG:(dc + 1) * SEG],
                                     start=(dc == 0), stop=(dc == DC - 1))
                        sP.inc(mmi, 1)
                        for tt in range(4):
                            for dc in range(DC):
                                mmi = t.matmul(p_v[:, tt * 128:(tt + 1) * 128],
                                         xv[:, dc * SEG + tt * 128:dc * SEG + tt * 128 + 128],
                                         wv_sb[:, wcol(dc):wcol(dc) + DK],
                                         start=(dc == 0), stop=(dc == DC - 1))
                        sP.inc(mmi, 1)
                    block.tensor(proj_t)
                    pe_q, pe_k, pe_v = sP.n - 2, sP.n - 1, sP.n
                    last["pq"] = pe_q
                    last["pev"] = pe_v

                    def sig_a(a, pe_q=pe_q, pe_k=pe_k, g=dict(last)):
                        a.wait_ge(sP.h, pe_q)
                        a.wait_ge(sV.h, g["vsk"])
                        a.activation(out=q_sb[:, :], in_=p_q[:, :], func=AF.Copy)
                        a.activation(out=tq1[:, :], in_=p_q[:, :], func=AF.Relu)
                        a.activation(out=tq2[:, :], in_=p_q[:, :], func=AF.Relu, scale=-1.0)
                        sA.inc(a.activation(out=tq2[:, :], in_=tq2[:, :], func=AF.Exp, scale=-1.0), 1)
                        a.wait_ge(sP.h, pe_k)
                        a.activation(out=k_sb[:, :], in_=p_k[:, :], func=AF.Copy)
                        a.activation(out=tk1[:, :], in_=p_k[:, :], func=AF.Relu)
                        a.activation(out=tk2[:, :], in_=p_k[:, :], func=AF.Relu, scale=-1.0)
                        sA.inc(a.activation(out=tk2[:, :], in_=tk2[:, :], func=AF.Exp, scale=-1.0), 1)
                    SCA(sig_a)
                    a_q, a_k = sA.n - 1, sA.n
                    last["aq"], last["ak"] = a_q, a_k

                    def sig_v(v, a_q=a_q, a_k=a_k, pe_v=pe_v, g=dict(last)):
                        v.wait_ge(sA.h, a_q)
                        sV.inc(v.tensor_add(sq_sb[:, :], tq1[:, :], tq2[:, :]), 1)
                        v.wait_ge(sA.h, a_k)
                        v.tensor_add(sk_sb[:, :], tk1[:, :], tk2[:, :])
                        sV.inc(v.reduce_sum(zadd[:, :], sk_sb[:, :], axis=AX.X), 1)
                        v.wait_ge(sP.h, pe_v)
                        inst = None
                        for tt in range(4):
                            inst = v.tensor_copy(v_aug[:, tt * 129:tt * 129 + 128],
                                                 p_v[:, tt * 128:(tt + 1) * 128])
                        sV.inc(inst, 1)
                    VEC(sig_v)
                    v_sq, v_sk, v_vaug = sV.n - 2, sV.n - 1, sV.n
                    last["vsq"], last["vsk"], last["vvaug"] = v_sq, v_sk, v_vaug

                    aexp = []
                    vpst = []
                    for tt in range(4):
                        def st_t(t, tt=tt, g=dict(last), aexp=list(aexp)):
                            if tt == 0:
                                t.wait_ge(sA.h, last["ak"])
                            else:
                                t.wait_ge(sA.h, aexp[tt - 1])
                            mmi = t.matmul(p_st[:, :], k_sb[:, tt * 128:(tt + 1) * 128],
                                     q_sb[:, :], start=True, stop=True)
                            sP.inc(mmi, 1)
                        block.tensor(st_t)
                        pe_st = sP.n

                        def st_a(a, tt=tt, pe_st=pe_st):
                            a.wait_ge(sP.h, pe_st)
                            sA.inc(a.activation(out=pst[:, tt * SEG:(tt + 1) * SEG], in_=p_st[:, :],
                                                func=AF.Exp, scale=ISQRT_DK, bias=negc[:, :]), 1)
                        SCA(st_a)
                        aexp.append(sA.n)

                        def st_v(v, tt=tt, a_e=sA.n):
                            v.wait_ge(sA.h, a_e)
                            sV.inc(v.tensor_mul(pst[:, tt * SEG:(tt + 1) * SEG],
                                                pst[:, tt * SEG:(tt + 1) * SEG],
                                                binmask[:, tt * SEG:(tt + 1) * SEG]), 1)
                        VEC(st_v)
                        vpst.append(sV.n)
                    last["pst3"] = sP.n
                    last["vpst"] = vpst[-1]

                    for ss in range(4):
                        def ad_t(t, ss=ss, g=dict(last), vpst=list(vpst)):
                            if ss == 0:
                                t.wait_ge(sV.h, vpst[3])
                                t.wait_ge(sV.h, last["vvaug"])
                            else:
                                t.wait_ge(sV.h, last["vcomb"])
                            for tt in range(4):
                                mmi = t.matmul(p_ad[:, 0:129],
                                         pst[:, tt * SEG + ss * 128:tt * SEG + ss * 128 + 128],
                                         v_aug[:, tt * 129:(tt + 1) * 129],
                                         start=(tt == 0), stop=(tt == 3))
                            sP.inc(mmi, 1)
                            if ss == 0:
                                t.wait_ge(sV.h, last["vsq"])
                                t.wait_ge(sV.h, last["vmemz"])
                            mmi = t.matmul(p_am[:, 0:129], sq_sb[:, ss * 128:(ss + 1) * 128],
                                     memz[:, h * 129:(h + 1) * 129], start=True, stop=True)
                            sP.inc(mmi, 1)
                        block.tensor(ad_t)
                        pe_ad, pe_am = sP.n - 1, sP.n
                        last["pad"] = pe_am

                        def comb_v(v, ss=ss, pe_ad=pe_ad, pe_am=pe_am, h=h, g=dict(last)):
                            v.wait_ge(sP.h, pe_ad)
                            v.reciprocal(rc[:, 0:1], p_ad[:, 128:129])
                            v.tensor_scalar(att_d[:, :], p_ad[:, 0:128], rc[:, 0:1], None, ALU.mult)
                            v.wait_ge(sP.h, pe_am)
                            v.reciprocal(rc[:, 1:2], p_am[:, 128:129])
                            v.tensor_scalar(att_m[:, :], p_am[:, 0:128], rc[:, 1:2], None, ALU.mult)
                            v.tensor_sub(cmb[:, :], att_m[:, :], att_d[:, :])
                            v.tensor_mul(cmb[:, :], cmb[:, :], bcast[:, h * 128:(h + 1) * 128])
                            sV.inc(v.tensor_add(att_bf[:, :], att_d[:, :], cmb[:, :]), 1)
                        VEC(comb_v)
                        last["vcomb"] = sV.n
                        v_comb = sV.n

                        def tr_t(t, ss=ss, v_comb=v_comb, g=dict(last)):
                            t.wait_ge(sV.h, v_comb)
                            if ss == 0:
                                t.wait_ge(sV.h, g["vsktm"])
                            else:
                                t.wait_ge(sV.h, g["vatile"])
                            mmi = t.matmul(p_tr[:, ss * 128:(ss + 1) * 128], att_bf[:, :], ident[:, :],
                                     start=True, stop=True)
                            sP.inc(mmi, 1)
                        block.tensor(tr_t)
                        pe_tr = sP.n

                        def atile_v(v, ss=ss, pe_tr=pe_tr, g=dict(last)):
                            v.wait_ge(sP.h, pe_tr)
                            if ss == 0:
                                v.wait_ge(sO.h, g["out"])
                            sV.inc(v.tensor_copy(att_tile[:, ss * 128:(ss + 1) * 128],
                                                 p_tr[:, ss * 128:(ss + 1) * 128]), 1)
                        VEC(atile_v)
                        last["vatile"] = sV.n

                    def out_s(s, h=h, seg=seg, v_at=sV.n):
                        s.wait_ge(sV.h, v_at)
                        sO.inc(s.dma_start(out=att0[h * DK:(h + 1) * DK, seg * SEG:(seg + 1) * SEG],
                                           in_=att_tile[:, :]), 16)
                    block.sync(out_s)
                    last["out"] = sO.n

                    vdl = []
                    for tt in range(4):
                        def dl_t(t, tt=tt, h=h, g=dict(last), vdl=list(vdl)):
                            if tt == 0:
                                t.wait_ge(sV.h, last["vsk"])
                                t.wait_ge(sV.h, last["vcomb"])
                            else:
                                t.wait_ge(sV.h, vdl[tt - 1])
                            mmi = t.matmul(p_am[:, 0:129], sk_sb[:, tt * 128:(tt + 1) * 128],
                                     memz[:, h * 129:(h + 1) * 129], start=True, stop=True)
                            sP.inc(mmi, 1)
                        block.tensor(dl_t)
                        pe_dl = sP.n

                        def dl_v(v, tt=tt, pe_dl=pe_dl):
                            v.wait_ge(sP.h, pe_dl)
                            v.reciprocal(rc[:, 2:3], p_am[:, 128:129])
                            v.tensor_scalar(rc[:, 3:4], rc[:, 2:3], -1.0, None, ALU.mult)
                            sV.inc(v.scalar_tensor_tensor(
                                ndelta[:, tt * 128:(tt + 1) * 128], p_am[:, 0:128],
                                rc[:, 3:4], v_aug[:, tt * 129:tt * 129 + 128],
                                ALU.mult, ALU.add), 1)
                        VEC(dl_v)
                        vdl.append(sV.n)
                    last["vdl"] = sV.n

                    def kt_t(t, g=dict(last), vdl3=sV.n):
                        t.wait_ge(sV.h, g["vatile"])
                        t.wait_ge(sV.h, g["vsk"])
                        for tt in range(4):
                            mmi = t.matmul(p_tr[:, tt * 128:(tt + 1) * 128],
                                     sk_sb[:, tt * 128:(tt + 1) * 128], ident[:, :],
                                     start=True, stop=True)
                        sP.inc(mmi, 1)
                    block.tensor(kt_t)
                    pe_kt = sP.n

                    def kt_v(v, pe_kt=pe_kt):
                        v.wait_ge(sP.h, pe_kt)
                        sV.inc(v.tensor_copy(sk_tm[:, :], p_tr[:, :]), 1)
                    VEC(kt_v)
                    last["vsktm"] = sV.n
                    v_ktm = sV.n

                    def mu_t(t, v_ktm=v_ktm, vdl3=last["vdl"]):
                        t.wait_ge(sV.h, v_ktm)
                        t.wait_ge(sV.h, vdl3)
                        for tt in range(4):
                            mmi = t.matmul(p_mu[:, 0:128], sk_tm[:, tt * 128:(tt + 1) * 128],
                                     ndelta[:, tt * 128:(tt + 1) * 128],
                                     start=(tt == 0), stop=(tt == 3))
                        sP.inc(mmi, 1)
                    block.tensor(mu_t)
                    pe_mu = sP.n
                    last["pmu"] = pe_mu

                    def mu_v(v, pe_mu=pe_mu, h=h):
                        v.wait_ge(sP.h, pe_mu)
                        v.tensor_add(mem_f[:, h * 128:(h + 1) * 128],
                                     mem_f[:, h * 128:(h + 1) * 128], p_mu[:, 0:128])
                        v.tensor_add(z_f[:, h:h + 1], z_f[:, h:h + 1], zadd[:, :])
                        v.tensor_copy(memz[:, h * 129:h * 129 + 128],
                                      mem_f[:, h * 128:(h + 1) * 128])
                        sV.inc(v.tensor_copy(memz[:, h * 129 + 128:h * 129 + 129],
                                             z_f[:, h:h + 1]), 1)
                    VEC(mu_v)
                    last["vmemz"] = sV.n

    return {"pe": (sP.h, sP.n), "act": (sA.h, sA.n), "dve": (sV.h, sV.n),
            "out": (sO.h, sO.n)}



def _consts_p0():
    ident = np.eye(128, dtype=BF)
    t_idx = np.arange(SEG)[:, None]
    s_idx = np.arange(SEG)[None, :]
    binmask = (s_idx >= t_idx).astype(BF)
    negc = np.full((128, 1), -CSHIFT, np.float32)
    return ident, binmask, negc


def _p0_in_maps(x0T_np, Wq, Wk, Wv, betas):
    ident, binmask, negc = _consts_p0()
    maps = []
    for c in range(N_CORES):
        cols = slice(c * HPC * DK, (c + 1) * HPC * DK)
        maps.append({
            "x0T": x0T_np,
            "wq": np.ascontiguousarray(Wq[:, cols]),
            "wk": np.ascontiguousarray(Wk[:, cols]),
            "wv": np.ascontiguousarray(Wv[:, cols]),
            "betas": np.ascontiguousarray(
                betas.reshape(16, DK)[2 * c:2 * c + 2, :].reshape(1, HPC * DK)),
            "ident": ident, "binmask": binmask, "negc": negc,
        })
    return maps





FFN_I = 8192
GSZ = 512
NG = FFN_I // GSZ
GC = GSZ // 128
NT = TPC // 128
LN_EPS = 1e-5


def build_p1(sparsify, final_tm):
    def io_builder(nc):
        io = dict(
            attc_mode="direct",
            attc=nc.dram_tensor("attc", [D, TPC], BF16, kind="ExternalInput").ap(),
            wo=nc.dram_tensor("wo", [128, DC * D], F32, kind="ExternalInput").ap(),
            wg=nc.dram_tensor("wg", [128, NG * DC * GSZ], F32, kind="ExternalInput").ap(),
            wu=nc.dram_tensor("wu", [128, NG * DC * GSZ], F32, kind="ExternalInput").ap(),
            wd=nc.dram_tensor("wd", [128, NG * GC * D], F32, kind="ExternalInput").ap(),
            ident=nc.dram_tensor("ident", [128, 128], BF16, kind="ExternalInput").ap(),
            identf=nc.dram_tensor("identf", [128, 128], F32, kind="ExternalInput").ap(),
            eps=nc.dram_tensor("epsrow", [1, 1], F32, kind="ExternalInput").ap(),
            x1scr=nc.dram_tensor("x1scr", [D, TPC], F32).ap(),
            xsum_t=None, cc_wait=None,
        )
        xres = nc.dram_tensor("xres", [D, TPC], F32, kind="ExternalInput").ap()
        io["xres_chunk"] = lambda s, dc: xres[dc * 128:(dc + 1) * 128, :]
        if final_tm:
            io["out_mode"] = "yout"
            io["out"] = nc.dram_tensor("yout", [TPC, D], F32, kind="ExternalOutput").ap()
        else:
            io["out_mode"] = "x2T"
            io["out"] = nc.dram_tensor("x2T", [D, TPC], F32, kind="ExternalOutput").ap()
        return io
    return _standalone(lambda E, io, entry: emit_p1(E, io, entry, sparsify, final_tm),
                       io_builder)


def emit_p1(E, io, entry, sparsify, final_tm, tag="p1"):
    nc, block, ctx = E.nc, E.block, E.ctx
    wo, wg, wu, wd = io["wo"], io["wg"], io["wu"], io["wd"]
    ident_in, identf_in, eps_in = io["ident"], io["identf"], io["eps"]
    out = io["out"]
    x1scr = io["x1scr"]
    out_mode = io["out_mode"]
    resident_res = io["xres_chunk"] is None
    with contextlib.ExitStack() as pctx:
        sb = lambda name, shape, dt: pctx.enter_context(nc.sbuf_tensor(tag + name, shape, dt))

        if io["xsum_t"] is None:
            xsum = sb("xsum", [128, DC * TPC], F32)
        else:
            xsum = io["xsum_t"]
        xstr = [sb(f"xstr{i}", [128, TPC], F32) for i in range(2)]
        ident = sb("ident_sb", [128, 128], BF16)
        identf = sb("identf_sb", [128, 128], F32)
        ones_col = sb("ones_col", [128, 1], F32)
        ones_row = sb("ones_row", [1, 128], F32)
        eps_sb = sb("eps_sb", [1, 1], F32)
        strow = sb("strow", [1, 3 * TPC], F32)
        silu_s = sb("silu_s", [128, 512], F32)

        pA = pctx.enter_context(nc.psum_tensor(tag + "pA", [128, 1024], F32))
        pB = pctx.enter_context(nc.psum_tensor(tag + "pB", [128, 1024], F32))
        pC = pctx.enter_context(nc.psum_tensor(tag + "pC", [128, 512], F32))
        pD = pctx.enter_context(nc.psum_tensor(tag + "pD", [128, 512], F32))
        pE = pctx.enter_context(nc.psum_tensor(tag + "pE", [128, 512], F32))
        pF = pctx.enter_context(nc.psum_tensor(tag + "pF", [128, 512], F32))

        sLo = E.sem(tag + "sLo")
        sLgu = E.sem(tag + "sLgu")
        sLd = E.sem(tag + "sLd")
        sWi = E.sem(tag + "sWi")
        sAt = E.sem(tag + "sAt")
        sXr = [E.sem(tag + "sXr0"), E.sem(tag + "sXr1")]
        sP = E.sem(tag + "sP")
        sA = E.sem(tag + "sA")
        sV = E.sem(tag + "sV")
        sScr = E.sem(tag + "sScr")
        sO = E.sem(tag + "sO")

        if True:
            VEC, SCA = E.VEC, E.SCA

            def emit_ln(src, v_ready):
                pe_st = []
                for dc in range(DC):
                    def sq_a(a, dc=dc):
                        if dc == 0:
                            a.wait_ge(sV.h, v_ready)
                        if dc >= 2:
                            a.wait_ge(sP.h, pe_st[dc - 2])
                        sA.inc(a.activation(out=xstr[dc % 2][:, :],
                                            in_=src[:, dc * TPC:(dc + 1) * TPC],
                                            func=AF.Square), 1)
                    SCA(sq_a)
                    a_n = sA.n

                    def st_t(t, dc=dc, a_n=a_n):
                        if dc == 0:
                            t.wait_ge(sV.h, v_ready)
                        for tb, bank in ((0, pC), (1, pD)):
                            t.matmul(bank[0:1, 0:512], ones_col[:, :],
                                     src[:, dc * TPC + tb * 512:dc * TPC + tb * 512 + 512],
                                     start=(dc == 0), stop=(dc == DC - 1))
                        t.wait_ge(sA.h, a_n)
                        mmi = None
                        for tb, bank in ((0, pE), (1, pF)):
                            mmi = t.matmul(bank[0:1, 0:512], ones_col[:, :],
                                           xstr[dc % 2][:, tb * 512:tb * 512 + 512],
                                           start=(dc == 0), stop=(dc == DC - 1))
                        sP.inc(mmi, 1)
                    block.tensor(st_t)
                    pe_st.append(sP.n)
                pe_stats = sP.n

                def stat_v(v, pe_stats=pe_stats):
                    m = strow[0:1, 0:TPC]
                    w = strow[0:1, TPC:2 * TPC]
                    tmp = strow[0:1, 2 * TPC:3 * TPC]
                    v.wait_ge(sP.h, pe_stats)
                    v.tensor_scalar(m[0:1, 0:512], pC[0:1, 0:512], 1.0 / D, None, ALU.mult)
                    v.tensor_scalar(m[0:1, 512:1024], pD[0:1, 0:512], 1.0 / D, None, ALU.mult)
                    v.tensor_scalar(w[0:1, 0:512], pE[0:1, 0:512], 1.0 / D, None, ALU.mult)
                    v.tensor_scalar(w[0:1, 512:1024], pF[0:1, 0:512], 1.0 / D, None, ALU.mult)
                    v.tensor_mul(tmp, m, m)
                    sV.inc(v.tensor_sub(w, w, tmp), 1)
                    vv = sV.n

                    return vv
                v_var = VEC(stat_v) or sV.n

                def sd_a(a, v_var=v_var):
                    a.wait_ge(sV.h, v_var)
                    sA.inc(a.activation(out=strow[0:1, 2 * TPC:3 * TPC],
                                        in_=strow[0:1, TPC:2 * TPC],
                                        func=AF.Sqrt, bias=eps_sb[0:1, 0:1]), 1)
                SCA(sd_a)
                a_sd = sA.n

                def rstd_v(v, a_sd=a_sd):
                    v.wait_ge(sA.h, a_sd)
                    sd = strow[0:1, 2 * TPC:3 * TPC]
                    w = strow[0:1, TPC:2 * TPC]
                    m2 = strow[0:1, 0:TPC]
                    v.reciprocal(w, sd)
                    v.tensor_mul(sd, sd, w)
                    v.tensor_scalar(sd, sd, -1.0, 2.0, ALU.mult, ALU.add)
                    sV.inc(v.tensor_mul(w, w, sd), 1)
                VEC(rstd_v)
                v_rstd = sV.n

                def bc_t(t, v_rstd=v_rstd):
                    t.wait_ge(sV.h, v_rstd)
                    mmi = None
                    for tb in range(2):
                        t.matmul(pA[:, tb * 512:(tb + 1) * 512], ones_row[:, :],
                                 strow[0:1, tb * 512:(tb + 1) * 512], start=True, stop=True)
                        mmi = t.matmul(pB[:, tb * 512:(tb + 1) * 512], ones_row[:, :],
                                       strow[0:1, TPC + tb * 512:TPC + (tb + 1) * 512],
                                       start=True, stop=True)
                    sP.inc(mmi, 1)
                block.tensor(bc_t)
                return sP.n

            _scA = nc.named_scope(tag + "_compose"); _scA.__enter__()
            with nc.sbuf_tensor(tag + "attc_sb", [128, DC * TPC], BF16) as attc_sb, \
                 contextlib.ExitStack() as actx:

                v_scr = 0
                if io["attc_mode"] == "direct":
                    attc = io["attc"]
                    def attc_s(s):
                        for se, va in entry:
                            s.wait_ge(se, va)
                        sAt.inc(s.dma_start(
                            out=attc_sb[:, :].rearrange("p (k t) -> p k t", t=TPC),
                            in_=attc.rearrange("(k p) t -> p k t", p=128)), 16)
                    block.sync(attc_s)
                elif io["attc_mode"] == "gather":
                    attg = io["attc_g"]
                    ccw = io["cc_wait"]
                    def attc_s(s):
                        for se, va in entry:
                            s.wait_ge(se, va)
                        s.wait_ge(ccw[0], ccw[1])
                        pid = s.partition_id()
                        sAt.inc(s.dma_start(
                            out=attc_sb[:, :].rearrange("p (k t) -> p k t", t=TPC),
                            in_=attg.rearrange("(k p) t -> p k t", p=128)[:, :, bass.ds(pid * TPC, TPC)]), 16)
                    block.sync(attc_s)
                else:
                    attg = io["attc_g"]
                    ccw = io["cc_wait"]
                    with nc.sbuf_tensor(tag + "att_raw", [128, DC * TPC], BF16) as att_raw:
                        def attc_s(s):
                            for se, va in entry:
                                s.wait_ge(se, va)
                            s.wait_ge(ccw[0], ccw[1])
                            pid = s.partition_id()
                            sAt.inc(s.dma_start(
                                out=att_raw[:, :].rearrange("p (k t) -> p k t", t=TPC),
                                in_=attg.rearrange("(k p) t -> p k t", p=128)[:, :, bass.ds(pid * TPC, TPC)]), 16)
                        block.sync(attc_s)

                        def scr_v(v, n_at=None):
                            for se, va in entry:
                                v.wait_ge(se, va)
                            v.wait_ge(sAt.h, sAt.n)
                            dst = attc_sb[:, :].rearrange(
                                "p (j segl h w) -> p j segl h w", j=16, segl=2, h=16, w=32)
                            srcv = att_raw[:, :].rearrange(
                                "p (h segl w j) -> p h segl w j", h=16, segl=2, w=32, j=16
                            ).transpose([0, 4, 2, 1, 3])
                            sV.inc(v.tensor_copy(dst, srcv), 1)
                        VEC(scr_v)
                        v_scr = sV.n

                wo_sb = actx.enter_context(nc.sbuf_tensor(tag + "wo_sb", [128, DC * D], BF16))

                def init_g(g):
                    for se, va in entry:
                        g.wait_ge(se, va)
                    if v_scr:
                        g.wait_ge(sV.h, v_scr)
                    sLo.inc(g.dma_start(out=wo_sb[:, :], in_=wo), 16)
                block.gpsimd(init_g)

                xres_w = {}
                def init_s(s):
                    sWi.inc(s.dma_start(out=ident[:, :], in_=ident_in), 16)
                    sWi.inc(s.dma_start(out=identf[:, :], in_=identf_in), 16)
                    sWi.inc(s.dma_start(out=eps_sb[:, :], in_=eps_in), 16)
                    if not resident_res:
                        for i in range(2):
                            xres_w[i] = sXr[i].inc(s.dma_start(
                                out=xstr[i][:, :], in_=io["xres_chunk"](s, i)), 16)
                block.sync(init_s)

                def init_v(v):
                    for se, va in entry:
                        v.wait_ge(se, va)
                    v.memset(ones_col[:, :], 1.0)
                    sV.inc(v.memset(ones_row[:, :], 1.0), 1)
                VEC(init_v)

                vcomb = []
                for fc in range(DC):
                    for tb in range(2):
                        bank = pC if tb == 0 else pD
                        def comb_t(t, fc=fc, tb=tb, bank=bank):
                            if fc == 0 and tb == 0:
                                t.wait_ge(sLo.h, sLo.n)
                                t.wait_ge(sWi.h, sWi.n)
                                t.wait_ge(sAt.h, sAt.n)
                                if v_scr:
                                    t.wait_ge(sV.h, v_scr)
                            if len(vcomb) >= 2:
                                t.wait_ge(sV.h, vcomb[-2])
                            mmi = None
                            for ac in range(DC):
                                mmi = t.matmul(bank[:, 0:512],
                                               wo_sb[:, ac * D + fc * 128:ac * D + fc * 128 + 128],
                                               attc_sb[:, ac * TPC + tb * 512:ac * TPC + tb * 512 + 512],
                                               start=(ac == 0), stop=(ac == DC - 1))
                            sP.inc(mmi, 1)
                        block.tensor(comb_t)
                        pe_c = sP.n

                        def comb_v(v, fc=fc, tb=tb, bank=bank, pe_c=pe_c,
                                   w_n=(0 if resident_res else xres_w[fc])):
                            v.wait_ge(sP.h, pe_c)
                            dst = xsum[:, fc * TPC + tb * 512:fc * TPC + tb * 512 + 512]
                            if resident_res:
                                sV.inc(v.tensor_add(dst, bank[:, 0:512], dst), 1)
                            else:
                                v.wait_ge(sXr[fc % 2].h, w_n)
                                sV.inc(v.tensor_add(dst, bank[:, 0:512],
                                                    xstr[fc % 2][:, tb * 512:tb * 512 + 512]), 1)
                        VEC(comb_v)
                        vcomb.append(sV.n)
                    if fc + 2 < DC and not resident_res:
                        def pre_s(s, fc=fc, guard=vcomb[-1]):
                            s.wait_ge(sV.h, guard)
                            xres_w[fc + 2] = sXr[fc % 2].inc(
                                s.dma_start(out=xstr[fc % 2][:, :],
                                            in_=io["xres_chunk"](s, fc + 2)), 16)
                        block.sync(pre_s)
                v_combe = sV.n
                pe_combe = sP.n
            _scA.__exit__(None, None, None)

            with nc.sbuf_tensor(tag + "x1b", [128, DC * TPC], BF16) as x1b:
                _scL = nc.named_scope(tag + "_ln1"); _scL.__enter__()
                pe_bc1 = emit_ln(xsum, v_combe)
                scr_w = 0
                for dc in range(DC):
                    def ap_v(v, dc=dc, pe_bc1=pe_bc1):
                        if dc == 0:
                            v.wait_ge(sP.h, pe_bc1)
                        c = xsum[:, dc * TPC:(dc + 1) * TPC]
                        v.tensor_sub(c, c, pA[:, :])
                        sV.inc(v.tensor_mul(c, c, pB[:, :]), 1)
                    VEC(ap_v)
                    v_n = sV.n

                    def ap_a(a, dc=dc, v_n=v_n):
                        a.wait_ge(sV.h, v_n)
                        sA.inc(a.activation(out=x1b[:, dc * TPC:(dc + 1) * TPC],
                                            in_=xsum[:, dc * TPC:(dc + 1) * TPC],
                                            func=AF.Copy), 1)
                    SCA(ap_a)

                    def ap_s(s, dc=dc, v_n=v_n):
                        s.wait_ge(sV.h, v_n)
                        sScr.inc(s.dma_start(out=x1scr[dc * 128:(dc + 1) * 128, :],
                                             in_=xsum[:, dc * TPC:(dc + 1) * TPC]), 16)
                    block.sync(ap_s)
                a_x1b = sA.n
                v_ap1 = sV.n
                scr_w = sScr.n
                _scL.__exit__(None, None, None)

                _scF = nc.named_scope(tag + "_ffn"); _scF.__enter__()
                QSZ = NT * GSZ // 4
                with nc.sbuf_tensor(tag + "wg_sb", [128, DC * GSZ], BF16) as wg_sb, \
                     nc.sbuf_tensor(tag + "wu_sb", [128, DC * GSZ], BF16) as wu_sb, \
                     nc.sbuf_tensor(tag + "wd_sb", [128, GC * D], BF16) as wd_sb, \
                     nc.sbuf_tensor(tag + "h_tm0", [128, NT * GSZ], BF16) as h_tm0, \
                     nc.sbuf_tensor(tag + "h_tm1", [128, NT * GSZ], BF16) as h_tm1, \
                     nc.sbuf_tensor(tag + "h_fm0", [128, GC * TPC], BF16) as h_fm0, \
                     nc.sbuf_tensor(tag + "h_fm1", [128, GC * TPC], BF16) as h_fm1, \
                     nc.sbuf_tensor(tag + "habs", [128, QSZ], BF16) as habs, \
                     nc.sbuf_tensor(tag + "spm", [128, 5 * (QSZ // 4)], BF16) as spm:
                    h_tmb = [h_tm0, h_tm1]
                    h_fmb = [h_fm0, h_fm1]

                    wg_v = wg.rearrange("p (g q) -> p g q", q=DC * GSZ)
                    wu_v = wu.rearrange("p (g q) -> p g q", q=DC * GSZ)
                    wd_v = wd.rearrange("p (g q) -> p g q", q=GC * D)

                    wld_gu = {}
                    wld_d = {}

                    def load_gu(g_eng, g, guard):
                        g_eng.wait_ge(sP.h, guard)
                        sLgu.inc(g_eng.dma_start(out=wg_sb[:, :], in_=wg_v[:, g, :]), 16)
                        sLgu.inc(g_eng.dma_start(out=wu_sb[:, :], in_=wu_v[:, g, :]), 16)
                        wld_gu[g] = sLgu.n

                    def load_d(g_eng, g, guard):
                        if guard:
                            g_eng.wait_ge(sP.h, guard)
                        sLd.inc(g_eng.dma_start(out=wd_sb[:, :], in_=wd_v[:, g, :]), 16)
                        wld_d[g] = sLd.n

                    block.gpsimd(lambda e: (load_gu(e, 0, pe_combe), load_d(e, 0, 0)))

                    pe_g = {}; pe_u = {}; pe_gu_end = {}; pe_tr = {}; pe_dn = {}
                    pe_group_end = {}
                    a_sil = {}; a_abs = {}
                    v_h = {}; v_sp = {}; v_trv = {}; v_dnv = {}

                    NSP = 4 if sparsify else 0

                    def plan_slot(g):
                        n = sP.n
                        for tt in range(NT):
                            n += 1; pe_g[(g, tt)] = n
                            n += 1; pe_u[(g, tt)] = n
                        pe_gu_end[g] = n
                        if g >= 1:
                            for tt in range(NT):
                                n += 1; pe_tr[(g - 1, tt)] = n
                            for idx in range(2 * DC):
                                n += 1; pe_dn[(g - 1, idx)] = n
                            pe_group_end[g - 1] = n
                        sP.n = n
                        n = sA.n
                        for tt in range(NT):
                            n += 1; a_sil[(g, tt)] = n
                        for q in range(NSP):
                            n += 1; a_abs[(g, q)] = n
                        sA.n = n
                        n = sV.n
                        for tt in range(NT):
                            n += 1; v_h[(g, tt)] = n
                        if g >= 1:
                            for tt in range(NT):
                                n += 1; v_trv[(g - 1, tt)] = n
                            for q in range(DC):
                                n += 1; v_dnv[(g - 1, q)] = n
                        for q in range(NSP):
                            n += 1; v_sp[(g, q)] = n
                        sV.n = n

                    def ten_slot(t, g):
                        htm = h_tmb[g % 2] if g < NG else None
                        if g < NG:
                            for tt in range(NT):
                                if tt == 0:
                                    t.wait_ge(sLgu.h, wld_gu[g])
                                    if g == 0:
                                        t.wait_ge(sA.h, a_x1b)
                                    else:
                                        t.wait_ge(sA.h, a_sil[(g - 1, NT - 1)])
                                        t.wait_ge(sV.h, v_h[(g - 1, NT - 1)])
                                else:
                                    t.wait_ge(sA.h, a_sil[(g, tt - 1)])
                                    t.wait_ge(sV.h, v_h[(g, tt - 1)])
                                mmi = None
                                for dc in range(DC):
                                    mmi = t.matmul(pC[:, :],
                                                   x1b[:, dc * TPC + tt * 128:dc * TPC + tt * 128 + 128],
                                                   wg_sb[:, dc * GSZ:(dc + 1) * GSZ],
                                                   start=(dc == 0), stop=(dc == DC - 1))
                                mmi.then_inc(sP.h, 1)
                                for dc in range(DC):
                                    mmi = t.matmul(pD[:, :],
                                                   x1b[:, dc * TPC + tt * 128:dc * TPC + tt * 128 + 128],
                                                   wu_sb[:, dc * GSZ:(dc + 1) * GSZ],
                                                   start=(dc == 0), stop=(dc == DC - 1))
                                mmi.then_inc(sP.h, 1)
                        if g >= 1:
                            gp = g - 1
                            hsrc = h_tmb[gp % 2]
                            hfm = h_fmb[gp % 2]
                            for tt in range(NT):
                                if tt == 0:
                                    if sparsify:
                                        t.wait_ge(sV.h, v_sp[(gp, NSP - 1)])
                                    else:
                                        t.wait_ge(sV.h, v_h[(gp, NT - 1)])
                                    if gp >= 2:
                                        t.wait_ge(sV.h, v_trv[(gp - 1, NT - 1)])
                                else:
                                    t.wait_ge(sV.h, v_trv[(gp, tt - 1)])
                                mmi = None
                                for ic in range(GC):
                                    mmi = t.matmul(pE[:, ic * 128:(ic + 1) * 128],
                                                   hsrc[:, tt * GSZ + ic * 128:tt * GSZ + ic * 128 + 128],
                                                   ident[:, :], start=True, stop=True)
                                mmi.then_inc(sP.h, 1)
                            for idx in range(2 * DC):
                                fc, tb = idx // 2, idx % 2
                                q = idx // 2
                                bank = pA if q % 2 == 0 else pB
                                if idx == 0:
                                    t.wait_ge(sV.h, v_trv[(gp, NT - 1)])
                                    t.wait_ge(sLd.h, wld_d[gp])
                                    if gp == 0:
                                        t.wait_ge(sScr.h, scr_w)
                                if q <= 1 and tb == 0 and gp == 0:
                                    t.wait_ge(sV.h, v_ap1)
                                if q >= 2 and tb == 0:
                                    t.wait_ge(sV.h, v_dnv[(gp, q - 2)])
                                mmi = None
                                for ic in range(GC):
                                    mmi = t.matmul(bank[:, tb * 512:tb * 512 + 512],
                                                   wd_sb[:, ic * D + fc * 128:ic * D + fc * 128 + 128],
                                                   hfm[:, ic * TPC + tb * 512:ic * TPC + tb * 512 + 512],
                                                   start=(ic == 0), stop=(ic == GC - 1))
                                mmi.then_inc(sP.h, 1)

                    def sca_slot(a, g):
                        if g >= NG:
                            return
                        for tt in range(NT):
                            a.wait_ge(sP.h, pe_g[(g, tt)])
                            a.activation(out=silu_s[:, :], in_=pC[:, :],
                                         func=AF.Silu).then_inc(sA.h, 1)
                            DRAIN_ACT and a.drain()
                        htm = h_tmb[g % 2]
                        for q in range(NSP):
                            a.wait_ge(sV.h, v_h[(g, NT - 1)])
                            if q >= 1:
                                a.wait_ge(sV.h, v_sp[(g, q - 1)])
                            a.activation(out=habs[:, :], in_=htm[:, q * QSZ:(q + 1) * QSZ],
                                         func=AF.Abs).then_inc(sA.h, 1)
                            DRAIN_ACT and a.drain()

                    def vec_slot(v, g):
                        if g < NG:
                            htm = h_tmb[g % 2]
                            for tt in range(NT):
                                v.wait_ge(sA.h, a_sil[(g, tt)])
                                v.wait_ge(sP.h, pe_u[(g, tt)])
                                v.tensor_mul(htm[:, tt * GSZ:(tt + 1) * GSZ],
                                             silu_s[:, :], pD[:, :]).then_inc(sV.h, 1)
                                DRAIN_DVE and v.drain()
                        if g >= 1:
                            gp = g - 1
                            hfm = h_fmb[gp % 2]
                            for tt in range(NT):
                                v.wait_ge(sP.h, pe_tr[(gp, tt)])
                                dst = hfm[:, :].rearrange("p (ic t) -> p ic t", t=TPC)[:, :, tt * 128:(tt + 1) * 128]
                                srcp = pE[:, :].rearrange("p (ic r) -> p ic r", r=128)
                                v.tensor_copy(dst, srcp).then_inc(sV.h, 1)
                                DRAIN_DVE and v.drain()
                            for q in range(DC):
                                bank = pA if q % 2 == 0 else pB
                                v.wait_ge(sP.h, pe_dn[(gp, 2 * q + 1)])
                                dst = xsum[:, q * TPC:(q + 1) * TPC]
                                if gp == 0:
                                    v.tensor_copy(dst, bank[:, :]).then_inc(sV.h, 1)
                                else:
                                    v.tensor_add(dst, dst, bank[:, :]).then_inc(sV.h, 1)
                                DRAIN_DVE and v.drain()
                        if g < NG and sparsify:
                            htm = h_tmb[g % 2]
                            QQ = QSZ // 4
                            for q in range(NSP):
                                v.wait_ge(sA.h, a_abs[(g, q)])
                                hv = htm[:, q * QSZ:(q + 1) * QSZ].rearrange("p (b four) -> p b four", four=4)
                                av = habs[:, :].rearrange("p (b four) -> p b four", four=4)
                                s0 = spm[:, 0 * QQ:1 * QQ]; s1 = spm[:, 1 * QQ:2 * QQ]
                                s2 = spm[:, 2 * QQ:3 * QQ]; s3 = spm[:, 3 * QQ:4 * QQ]
                                tmp = spm[:, 4 * QQ:5 * QQ]
                                aa = [av[:, :, j] for j in range(4)]
                                hh = [hv[:, :, j] for j in range(4)]
                                D_ = Drained(v)
                                D_.tensor_tensor(tmp, aa[0], aa[1], ALU.is_ge)
                                D_.tensor_copy(s0, tmp)
                                D_.tensor_scalar(s1, tmp, -1.0, None, ALU.mult)
                                D_.tensor_tensor(tmp, aa[0], aa[2], ALU.is_ge)
                                D_.tensor_add(s0, s0, tmp)
                                D_.tensor_scalar(s2, tmp, -1.0, None, ALU.mult)
                                D_.tensor_tensor(tmp, aa[0], aa[3], ALU.is_ge)
                                D_.tensor_add(s0, s0, tmp)
                                D_.tensor_scalar(s3, tmp, -1.0, None, ALU.mult)
                                D_.tensor_tensor(tmp, aa[1], aa[2], ALU.is_ge)
                                D_.tensor_add(s1, s1, tmp)
                                D_.tensor_sub(s2, s2, tmp)
                                D_.tensor_tensor(tmp, aa[1], aa[3], ALU.is_ge)
                                D_.tensor_add(s1, s1, tmp)
                                D_.tensor_sub(s3, s3, tmp)
                                D_.tensor_tensor(tmp, aa[2], aa[3], ALU.is_ge)
                                D_.tensor_add(s2, s2, tmp)
                                D_.tensor_sub(s3, s3, tmp)
                                D_.tensor_single_scalar(s0, s0, 1.5, ALU.is_ge)
                                D_.tensor_single_scalar(s1, s1, 0.5, ALU.is_ge)
                                D_.tensor_single_scalar(s2, s2, -0.5, ALU.is_ge)
                                D_.tensor_single_scalar(s3, s3, -1.5, ALU.is_ge)
                                D_.tensor_tensor(hh[0], hh[0], s0, ALU.mult)
                                D_.tensor_tensor(hh[1], hh[1], s1, ALU.mult)
                                D_.tensor_tensor(hh[2], hh[2], s2, ALU.mult)
                                v.tensor_tensor(hh[3], hh[3], s3, ALU.mult).then_inc(sV.h, 1)
                                DRAIN_DVE and v.drain()

                    for g in range(NG + 1):
                        plan_slot(g) if g < NG else None
                        if g == NG:
                            n = sP.n
                            for tt in range(NT):
                                n += 1; pe_tr[(NG - 1, tt)] = n
                            for idx in range(2 * DC):
                                n += 1; pe_dn[(NG - 1, idx)] = n
                            pe_group_end[NG - 1] = n
                            sP.n = n
                            n = sV.n
                            for tt in range(NT):
                                n += 1; v_trv[(NG - 1, tt)] = n
                            for q in range(DC):
                                n += 1; v_dnv[(NG - 1, q)] = n
                            sV.n = n
                        block.tensor(lambda t, g=g: ten_slot(t, g))
                        block.scalar(lambda a, g=g: sca_slot(a, g))
                        block.vector(lambda v, g=g: vec_slot(v, g))
                        if g + 1 < NG:
                            block.gpsimd(lambda e, g=g: load_gu(e, g + 1, pe_gu_end[g]))
                        if g < NG and g >= 1:
                            block.gpsimd(lambda e, g=g: load_d(
                                e, g, pe_group_end[g - 1]))
                    v_ffn = v_dnv[(NG - 1, DC - 1)]
                    _scF.__exit__(None, None, None)


                _scR = nc.named_scope(tag + "_ln2"); _scR.__enter__()
                vres = []
                for dc in range(DC):
                    def res_s(s, dc=dc, guard=(vres[dc - 2] if dc >= 2 else 0)):
                        if dc == 0:
                            s.wait_ge(sV.h, v_ffn)
                            s.wait_ge(sScr.h, scr_w)
                        if dc >= 2:
                            s.wait_ge(sV.h, guard)
                        sXr[dc % 2].inc(s.dma_start(out=xstr[dc % 2][:, :],
                                        in_=x1scr[dc * 128:(dc + 1) * 128, :]), 16)
                    block.sync(res_s)
                    n_ws = sXr[dc % 2].n

                    def res_v(v, dc=dc, n_ws=n_ws):
                        v.wait_ge(sXr[dc % 2].h, n_ws)
                        c = xsum[:, dc * TPC:(dc + 1) * TPC]
                        sV.inc(v.tensor_add(c, c, xstr[dc % 2][:, :]), 1)
                    VEC(res_v)
                    vres.append(sV.n)
                v_res = sV.n

                pe_bc2 = emit_ln(xsum, v_res)

                for dc in range(DC):
                    def ap2_v(v, dc=dc, pe_bc2=pe_bc2):
                        if dc == 0:
                            v.wait_ge(sP.h, pe_bc2)
                        c = xsum[:, dc * TPC:(dc + 1) * TPC]
                        v.tensor_sub(c, c, pA[:, :])
                        sV.inc(v.tensor_mul(c, c, pB[:, :]), 1)
                    VEC(ap2_v)
                    if out_mode == "x2T":
                        def o_s(s, dc=dc, v_n=sV.n):
                            s.wait_ge(sV.h, v_n)
                            sO.inc(s.dma_start(out=out[dc * 128:(dc + 1) * 128, :],
                                               in_=xsum[:, dc * TPC:(dc + 1) * TPC]), 16)
                        block.sync(o_s)
                    elif out_mode == "x2_cc":
                        def o_g(g, dc=dc, v_n=sV.n):
                            g.wait_ge(sV.h, v_n)
                            sO.inc(g.dma_start(out=out[dc * 128:(dc + 1) * 128, :],
                                               in_=xsum[:, dc * TPC:(dc + 1) * TPC]), 16)
                        block.gpsimd(o_g)
                v_x2 = sV.n

                if final_tm:
                    with nc.sbuf_tensor(tag + "ytile", [128, D], F32) as ytile:
                        v_yt = []
                        for tt in range(NT):
                            def ftr_t(t, tt=tt, v_x2=v_x2):
                                if tt == 0:
                                    t.wait_ge(sV.h, v_x2)
                                else:
                                    t.wait_ge(sV.h, v_yt[-1])
                                mmi = None
                                for dc in range(DC):
                                    bank = pA if dc < 8 else pB
                                    mmi = t.matmul(bank[:, (dc % 8) * 128:(dc % 8 + 1) * 128],
                                                   xsum[:, dc * TPC + tt * 128:dc * TPC + tt * 128 + 128],
                                                   identf[:, :], start=True, stop=True)
                                sP.inc(mmi, 1)
                            block.tensor(ftr_t)
                            pe_ftr = sP.n

                            def ftr_v(v, pe_ftr=pe_ftr, guard=sO.n):
                                v.wait_ge(sP.h, pe_ftr)
                                v.wait_ge(sO.h, guard)
                                v.tensor_copy(ytile[:, 0:1024], pA[:, :])
                                sV.inc(v.tensor_copy(ytile[:, 1024:2048], pB[:, :]), 1)
                            VEC(ftr_v)
                            v_yt.append(sV.n)

                            def ftr_s(s, tt=tt, v_n=sV.n):
                                s.wait_ge(sV.h, v_n)
                                sO.inc(s.dma_start(out=out[tt * 128:(tt + 1) * 128, :],
                                                   in_=ytile[:, :]), 16)
                            block.sync(ftr_s)
                _scR.__exit__(None, None, None)

    return {"pe": (sP.h, sP.n), "act": (sA.h, sA.n), "dve": (sV.h, sV.n),
            "out": (sO.h, sO.n)}



def build_p2(nblk=16):
    def io_builder(nc):
        x2T = nc.dram_tensor("x2T", [D, T], F32, kind="ExternalInput").ap()
        return dict(
            xblk_in=lambda grp, blk: x2T[512 * grp:512 * (grp + 1),
                                         blk * 512:(blk + 1) * 512].rearrange("(k p) t -> p k t", p=128),
            wq=nc.dram_tensor("wq", [128, DC * HPC * DK], F32, kind="ExternalInput").ap(),
            wk=nc.dram_tensor("wk", [128, DC * DK], F32, kind="ExternalInput").ap(),
            wv=nc.dram_tensor("wv", [128, DC * DK], F32, kind="ExternalInput").ap(),
            ropec=nc.dram_tensor("ropec", [128, T], F32, kind="ExternalInput").ap(),
            ropes=nc.dram_tensor("ropes", [128, T], F32, kind="ExternalInput").ap(),
            ident=nc.dram_tensor("ident", [128, 128], BF16, kind="ExternalInput").ap(),
            dmask=nc.dram_tensor("dmask", [128, 4 * 512], BF16, kind="ExternalInput").ap(),
            negc=nc.dram_tensor("negc", [128, 1], F32, kind="ExternalInput").ap(),
            att1=nc.dram_tensor("att1", [HPC * DK, T], BF16, kind="ExternalOutput").ap(),
            cc_wait=None,
        )
    return _standalone(lambda E, io, entry: emit_p2(E, io, entry, nblk=nblk), io_builder)


def emit_p2(E, io, entry, nblk=16, tag="p2"):
    nc, block, ctx = E.nc, E.block, E.ctx
    wq, wk, wv = io["wq"], io["wk"], io["wv"]
    cos_in, sin_in = io["ropec"], io["ropes"]
    ident_in, dmask_in, negc_in = io["ident"], io["dmask"], io["negc"]
    att1 = io["att1"]

    NQT = nblk * 4
    NQG = nblk

    with contextlib.ExitStack() as pctx:
        sb = lambda name, shape, dt: pctx.enter_context(nc.sbuf_tensor(tag + name, shape, dt))

        wq_sb = sb("wq_sb", [128, DC * HPC * DK], BF16)
        wk_sb = sb("wk_sb", [128, DC * DK], BF16)
        wv_sb = sb("wv_sb", [128, DC * DK], BF16)
        xblk = [sb(f"xblk{i}", [128, DC * 512], BF16) for i in range(2)]
        ropeb = [sb(f"ropeb{i}", [128, 1024], F32) for i in range(2)]
        ident = sb("ident_sb", [128, 128], BF16)
        dmask = sb("dmask_sb", [128, 4 * 512], BF16)
        negc = sb("negc_sb", [128, 1], F32)
        qsb = [sb(f"q{h}_sb", [128, T], BF16) for h in range(HPC)]
        ksb = sb("k_sb", [128, T], BF16)
        v_aug = sb("v_aug", [128, NQT * 129], BF16)
        vtmp = sb("vtmp", [128, 512], BF16)
        tr1 = sb("tr1", [128, 512], F32)
        tr2 = sb("tr2", [128, 512], F32)
        tr3 = sb("tr3", [128, 512], F32)
        ptile = [sb(f"ptile{i}", [128, 1024], BF16) for i in range(2)]
        ybf = sb("ybf", [128, 128], BF16)
        att_tile = sb("att_tile", [128, 512], BF16)
        rc = sb("rc", [128, 2], F32)

        pS = [pctx.enter_context(nc.psum_tensor(tag + f"pS{i}", [128, 1024], F32)) for i in range(2)]
        pY = [pctx.enter_context(nc.psum_tensor(tag + f"pY{i}", [128, 512], F32)) for i in range(4)]
        pQ, pV, pT = pY[0], pY[1], pY[2]

        sLw = E.sem(tag + "sLw")
        sXB = [E.sem(tag + "sXB0"), E.sem(tag + "sXB1")]
        sRC = [E.sem(tag + "sRC0"), E.sem(tag + "sRC1")]
        sWi = E.sem(tag + "sWi")
        sP = E.sem(tag + "sP")
        sA = E.sem(tag + "sA")
        sV = E.sem(tag + "sV")
        sO = E.sem(tag + "sO")

        def load_xblk(g, blk, buf):
            if io["cc_wait"] is not None and blk == 0:
                g.wait_ge(io["cc_wait"][0], io["cc_wait"][1])
            for grp in range(4):
                sXB[buf].inc(g.dma_start(
                    out=xblk[buf][:, :].rearrange("p (k t) -> p k t", t=512)[:, 4 * grp:4 * grp + 4, :],
                    in_=io["xblk_in"](grp, blk)), 16)

        if True:
            VEC, SCA = E.VEC, E.SCA

            XQ = block.sync if io.get("xblk_sync") else block.gpsimd

            def init_g(g):
                for se, va in entry:
                    g.wait_ge(se, va)
                sLw.inc(g.dma_start(out=wq_sb[:, :], in_=wq), 16)
                sLw.inc(g.dma_start(out=wk_sb[:, :], in_=wk), 16)
                sLw.inc(g.dma_start(out=wv_sb[:, :], in_=wv), 16)
            block.gpsimd(init_g)
            XQ(lambda e: load_xblk(e, 0, 0))

            def load_rope(s, blk):
                buf = blk % 2
                sRC[buf].inc(s.dma_start(out=ropeb[buf][:, 0:512],
                                         in_=cos_in[:, blk * 512:(blk + 1) * 512]), 16)
                sRC[buf].inc(s.dma_start(out=ropeb[buf][:, 512:1024],
                                         in_=sin_in[:, blk * 512:(blk + 1) * 512]), 16)

            def init_s(s):
                for se, va in entry:
                    s.wait_ge(se, va)
                load_rope(s, 0)
                load_rope(s, 1)
                sWi.inc(s.dma_start(out=ident[:, :], in_=ident_in), 16)
                sWi.inc(s.dma_start(out=dmask[:, :], in_=dmask_in), 16)
                sWi.inc(s.dma_start(out=negc[:, :], in_=negc_in), 16)
            block.sync(init_s)
            n_wi = sWi.n

            def init_v(v):
                for se, va in entry:
                    v.wait_ge(se, va)
                sV.inc(v.memset(v_aug[:, :], 1.0), 1)
            VEC(init_v)
            v_init = sV.n

            _sc1 = nc.named_scope(tag + "_proj"); _sc1.__enter__()
            last_pev = 0
            a_evac = 0
            v_rope = 0
            for blk in range(nblk):
                buf = blk % 2
                xv = xblk[buf][:, :]
                if blk + 1 < nblk:
                    def pre_g(g, blk=blk, guard=last_pev):
                        if blk >= 1:
                            g.wait_ge(sP.h, guard)
                        load_xblk(g, blk + 1, (blk + 1) % 2)
                    XQ(pre_g)

                for which in range(4):
                    def proj_t(t, blk=blk, which=which, xv=xv, g_a=a_evac, g_v=v_rope):
                        if which == 0:
                            t.wait_ge(sXB[blk % 2].h, 64 * (blk // 2 + 1))
                            if blk == 0:
                                t.wait_ge(sLw.h, sLw.n)
                                t.wait_ge(sWi.h, n_wi)
                                t.wait_ge(sV.h, v_init)
                        t.wait_ge(sA.h, g_a)
                        mmi = None
                        if which < 3:
                            wsb, col = ((wq_sb, which * DK) if which < 2 else (wk_sb, 0))
                            for dc in range(DC):
                                base = dc * (HPC * DK if which < 2 else DK) + col
                                mmi = t.matmul(pQ[:, :], wsb[:, base:base + DK],
                                               xv[:, dc * 512:(dc + 1) * 512],
                                               start=(dc == 0), stop=(dc == DC - 1))
                            sP.inc(mmi, 1)
                        else:
                            for dc in range(DC):
                                mmi = t.matmul(pV[:, :], wv_sb[:, dc * DK:dc * DK + DK],
                                               xv[:, dc * 512:(dc + 1) * 512],
                                               start=(dc == 0), stop=(dc == DC - 1))
                            sP.inc(mmi, 1)
                    block.tensor(proj_t)
                    pe_p = sP.n
                    if which == 3:
                        last_pev = pe_p

                    if which < 3:
                        def rope_a(a, pe_p=pe_p, g_v=v_rope):
                            a.wait_ge(sP.h, pe_p)
                            a.wait_ge(sV.h, g_v)
                            sA.inc(a.activation(out=tr1[:, :], in_=pQ[:, :], func=AF.Copy), 1)
                        SCA(rope_a)
                        a_evac = sA.n

                        def rope_v(v, blk=blk, which=which, a_n=sA.n):
                            v.wait_ge(sA.h, a_n)
                            if which == 0:
                                v.wait_ge(sRC[blk % 2].h, 32 * (blk // 2 + 1))
                            v.tensor_copy(tr2[0:64, :], tr1[64:128, :])
                            v.tensor_copy(tr2[64:128, :], tr1[0:64, :])
                            cs = ropeb[blk % 2][:, 0:512]
                            sn = ropeb[blk % 2][:, 512:1024]
                            v.tensor_mul(tr3[:, :], tr1[:, :], cs)
                            v.tensor_mul(tr2[:, :], tr2[:, :], sn)
                            dst = (qsb[which] if which < 2 else ksb)[:, blk * 512:(blk + 1) * 512]
                            sV.inc(v.tensor_add(dst, tr3[:, :], tr2[:, :]), 1)
                        VEC(rope_v)
                        v_rope = sV.n
                        if which == 2 and blk + 2 < nblk:
                            def rpre_s(s, blk=blk, guard=v_rope):
                                s.wait_ge(sV.h, guard)
                                load_rope(s, blk + 2)
                            block.sync(rpre_s)
                    if which == 3:
                        def v_a(a, pe_p=pe_p):
                            a.wait_ge(sP.h, pe_p)
                            sA.inc(a.activation(out=vtmp[:, :], in_=pV[:, :], func=AF.Copy), 1)
                        SCA(v_a)
                        a_evac = sA.n

                        def vt_t(t, a_n=sA.n, g_v=v_rope):
                            t.wait_ge(sA.h, a_n)
                            t.wait_ge(sV.h, g_v)
                            mmi = None
                            for j in range(4):
                                mmi = t.matmul(pT[:, j * 128:(j + 1) * 128],
                                               vtmp[:, j * 128:(j + 1) * 128], ident[:, :],
                                               start=True, stop=True)
                            sP.inc(mmi, 1)
                        block.tensor(vt_t)
                        pe_vt = sP.n

                        def vt_v(v, blk=blk, pe_vt=pe_vt):
                            v.wait_ge(sP.h, pe_vt)
                            dst = v_aug[:, :].rearrange("p (n c) -> p n c", c=129)[:, 4 * blk:4 * blk + 4, 0:128]
                            src = pT[:, :].rearrange("p (n c) -> p n c", c=128)
                            sV.inc(v.tensor_copy(dst, src), 1)
                        VEC(vt_v)
                        v_rope = sV.n
            v_stage1 = sV.n
            pe_stage1 = sP.n
            _sc1.__exit__(None, None, None)
            _sc2 = nc.named_scope(tag + "_attn"); _sc2.__enter__()

            ISQ = ISQRT_DK
            v_att = 0
            prev_av_last = 0
            for h in range(HPC):
                for qg in range(NQG):
                    nkb = 4 * qg + 4
                    npair = 2 * qg + 2

                    pe_ops = [("st", 0)]
                    for p in range(npair):
                        if p + 1 < npair:
                            pe_ops.append(("st", p + 1))
                        pe_ops.append(("av", p))
                    pe_st = {}
                    av_done = {}
                    n = sP.n
                    for kind, p in pe_ops:
                        n += 1
                        (pe_st if kind == "st" else av_done)[p] = n
                    pe_ytr = [n + 1 + j for j in range(4)]
                    sP.n = n + 4
                    aexp = {p: sA.n + p + 1 for p in range(npair)}
                    sA.n += npair
                    vmask = {npair - 2: sV.n + 1, npair - 1: sV.n + 2}
                    v_ye = [sV.n + 3 + j for j in range(4)]
                    v_att_new = sV.n + 7
                    sV.n += 7
                    o_prev = sO.n

                    def ten_body(t, h=h, qg=qg, nkb=nkb, npair=npair,
                                 pe_ops=list(pe_ops), aexp=dict(aexp),
                                 vmask=dict(vmask), v_ye=list(v_ye),
                                 v_att=v_att, first=((h, qg) == (0, 0))):
                        qslice = qsb[h][:, qg * 512:(qg + 1) * 512]
                        for kind, p in pe_ops:
                            if kind == "st":
                                if first and p < 2:
                                    t.wait_ge(sP.h, pe_stage1)
                                    t.wait_ge(sV.h, v_stage1)
                                if p >= 2:
                                    t.wait_ge(sA.h, aexp[p - 2])
                                elif v_att:
                                    t.wait_ge(sV.h, v_att)
                                mmi = None
                                for kb in (2 * p, 2 * p + 1):
                                    mmi = t.matmul(
                                        pS[p % 2][:, (kb % 2) * 512:(kb % 2) * 512 + 512],
                                        ksb[:, kb * 128:(kb + 1) * 128], qslice,
                                        start=True, stop=True)
                                mmi.then_inc(sP.h, 1)
                            else:
                                t.wait_ge(sA.h, aexp[p])
                                if p >= npair - 2:
                                    t.wait_ge(sV.h, vmask[p])
                                mmi = None
                                for kb in (2 * p, 2 * p + 1):
                                    for j in range(4):
                                        mmi = t.matmul(
                                            pY[j][:, 0:129],
                                            ptile[p % 2][:, (kb % 2) * 512 + j * 128:(kb % 2) * 512 + j * 128 + 128],
                                            v_aug[:, kb * 129:(kb + 1) * 129],
                                            start=(kb == 0), stop=(kb == nkb - 1))
                                mmi.then_inc(sP.h, 1)
                        for j in range(4):
                            t.wait_ge(sV.h, v_ye[j])
                            t.matmul(pS[1][:, j * 128:(j + 1) * 128], ybf[:, :],
                                     ident[:, :], start=True, stop=True).then_inc(sP.h, 1)
                    block.tensor(ten_body)

                    def sca_body(a, npair=npair, pe_st=dict(pe_st),
                                 av_done=dict(av_done), prev_av_last=prev_av_last):
                        for p in range(npair):
                            a.wait_ge(sP.h, pe_st[p])
                            guard = av_done[p - 2] if p >= 2 else prev_av_last
                            if guard:
                                a.wait_ge(sP.h, guard)
                            a.activation(out=ptile[p % 2][:, :], in_=pS[p % 2][:, :],
                                         func=AF.Exp, scale=ISQ,
                                         bias=negc[:, :]).then_inc(sA.h, 1)
                            DRAIN_ACT and a.drain()
                    block.scalar(sca_body)
                    sca_drains = npair

                    def vec_body(v, h=h, qg=qg, npair=npair, aexp=dict(aexp),
                                 av_last=av_done[npair - 1], pe_ytr=list(pe_ytr),
                                 o_prev=o_prev):
                        for p in (npair - 2, npair - 1):
                            v.wait_ge(sA.h, aexp[p])
                            inst = None
                            for kb in (2 * p, 2 * p + 1):
                                r = kb - 4 * qg
                                if 0 <= r < 4:
                                    inst = v.tensor_mul(
                                        ptile[p % 2][:, (kb % 2) * 512:(kb % 2) * 512 + 512],
                                        ptile[p % 2][:, (kb % 2) * 512:(kb % 2) * 512 + 512],
                                        dmask[:, r * 512:(r + 1) * 512])
                                    DRAIN_DVE and v.drain()
                            inst.then_inc(sV.h, 1)
                        for j in range(4):
                            v.wait_ge(sP.h, av_last)
                            if j > 0:
                                v.wait_ge(sP.h, pe_ytr[j - 1])
                            v.reciprocal(rc[:, 0:1], pY[j][:, 128:129])
                            DRAIN_DVE and v.drain()
                            v.tensor_scalar(ybf[:, :], pY[j][:, 0:128], rc[:, 0:1],
                                            None, ALU.mult).then_inc(sV.h, 1)
                            DRAIN_DVE and v.drain()
                        v.wait_ge(sP.h, pe_ytr[3])
                        v.wait_ge(sO.h, o_prev)
                        v.tensor_copy(att_tile[:, :], pS[1][:, 0:512]).then_inc(sV.h, 1)
                        DRAIN_DVE and v.drain()
                    block.vector(vec_body)

                    def out_s(s, h=h, qg=qg, v_n=v_att_new):
                        s.wait_ge(sV.h, v_n)
                        sO.inc(s.dma_start(
                            out=att1[h * DK:(h + 1) * DK, qg * 512:(qg + 1) * 512],
                            in_=att_tile[:, :]), 16)
                    block.sync(out_s)

                    v_att = v_att_new
                    prev_av_last = av_done[npair - 1]
            _sc2.__exit__(None, None, None)

    return {"pe": (sP.h, sP.n), "act": (sA.h, sA.n), "dve": (sV.h, sV.n),
            "out": (sO.h, sO.n)}



def build_fused():
    nc = bass.Bass("TRN2", target_bir_lowering=False, debug=False)
    I = lambda n, shp, dt=F32: nc.dram_tensor(n, shp, dt, kind="ExternalInput").ap()
    x0s = I("x0s", [128, NSEG * DC * SEG])
    wq0, wk0, wv0 = I("wq0", [128, DC * 256]), I("wk0", [128, DC * 256]), I("wv0", [128, DC * 256])
    betas = I("betas", [1, 256])
    wo0, wg0 = I("wo0", [128, DC * D]), I("wg0", [128, NG * DC * GSZ])
    wu0, wd0 = I("wu0", [128, NG * DC * GSZ]), I("wd0", [128, NG * GC * D])
    wq1, wk1, wv1 = I("wq1", [128, DC * 256]), I("wk1", [128, DC * DK]), I("wv1", [128, DC * DK])
    ropec, ropes = I("ropec", [128, T]), I("ropes", [128, T])
    wo1, wg1 = I("wo1", [128, DC * D]), I("wg1", [128, NG * DC * GSZ])
    wu1, wd1 = I("wu1", [128, NG * DC * GSZ]), I("wd1", [128, NG * GC * D])
    xres0 = I("xres", [D, TPC])
    ident = I("ident", [128, 128], BF16)
    identf = I("identf", [128, 128], F32)
    binmask = I("binmask", [SEG, SEG], BF16)
    dmask = I("dmask", [128, 4 * 512], BF16)
    negc = I("negc", [128, 1], F32)
    eps = I("epsrow", [1, 1], F32)
    yout = nc.dram_tensor("yout", [TPC, D], F32, kind="ExternalOutput").ap()

    cc_a0_in = nc.dram_tensor("cc_a0_in", [256, T], BF16).ap()
    cc_a0_out = nc.dram_tensor("cc_a0_out", [D, T], BF16, addr_space="Shared").ap()
    cc_x2_in = nc.dram_tensor("cc_x2_in", [D, TPC], BF16).ap()
    cc_x2_out = nc.dram_tensor("cc_x2_out", [N_CORES * D, TPC], BF16, addr_space="Shared").ap()
    cc_a1_in = nc.dram_tensor("cc_a1_in", [256, T], BF16).ap()
    cc_a1_out = nc.dram_tensor("cc_a1_out", [D, T], BF16, addr_space="Shared").ap()
    x1scr = nc.dram_tensor("x1scr", [D, TPC], F32).ap()
    x3scr = nc.dram_tensor("x3scr", [D, TPC], F32).ap()

    RG = [list(range(N_CORES))]
    ctx = contextlib.ExitStack()
    with ctx:
        xsum = ctx.enter_context(nc.sbuf_tensor("xsum_shared", [128, DC * TPC], F32))
        block = ctx.enter_context(nc.Block())
        E = Env(nc, block, ctx)
        sCC = E.sem("sCC")

        with nc.named_scope("P0"):
            ex0 = emit_p0(E, dict(x0s=x0s, wq=wq0, wk=wk0, wv=wv0, betas=betas,
                                  ident=ident, binmask=binmask, negc=negc,
                                  att0=cc_a0_in), [], tag="p0")

        def cc1_g(g):
            g.wait_ge(*ex0["out"])
            sCC.inc(g.collective_compute(
                "AllGather", ALU.bypass, replica_groups=RG,
                ins=[cc_a0_in.opt()], outs=[cc_a0_out.opt()]), 1)
        block.gpsimd(cc1_g)

        entry1 = [ex0["pe"], ex0["act"], ex0["dve"], ex0["out"]]
        io1 = dict(attc_mode="gather_scramble", attc_g=cc_a0_out,
                   cc_wait=(sCC.h, 1),
                   xres_chunk=lambda s, dc: xres0[dc * 128:(dc + 1) * 128, :],
                   wo=wo0, wg=wg0, wu=wu0, wd=wd0,
                   ident=ident, identf=identf, eps=eps, x1scr=x1scr,
                   xsum_t=xsum, out_mode="x2_cc", out=cc_x2_in)
        with nc.named_scope("P1"):
            ex1 = emit_p1(E, io1, entry1, sparsify=True, final_tm=False, tag="p1")

        def cc2_g(g):
            g.wait_ge(*ex1["out"])
            sCC.inc(g.collective_compute(
                "AllGather", ALU.bypass, replica_groups=RG,
                ins=[cc_x2_in.opt()], outs=[cc_x2_out.opt()]), 1)
        block.gpsimd(cc2_g)

        entry2 = [ex1["pe"], ex1["act"], ex1["dve"], ex1["out"]]
        x2g = cc_x2_out.rearrange("(r d) t -> r d t", d=D)
        io2 = dict(
            xblk_in=lambda grp, blk: x2g[blk // 2, 512 * grp:512 * (grp + 1),
                                         (blk % 2) * 512:(blk % 2) * 512 + 512
                                         ].rearrange("(k p) t -> p k t", p=128),
            wq=wq1, wk=wk1, wv=wv1, ropec=ropec, ropes=ropes,
            ident=ident, dmask=dmask, negc=negc, att1=cc_a1_in,
            cc_wait=(sCC.h, 2), xblk_sync=True)
        with nc.named_scope("P2"):
            ex2 = emit_p2(E, io2, entry2, tag="p2")

        def cc3_g(g):
            g.wait_ge(*ex2["out"])
            sCC.inc(g.collective_compute(
                "AllGather", ALU.bypass, replica_groups=RG,
                ins=[cc_a1_in.opt()], outs=[cc_a1_out.opt()]), 1)
        block.gpsimd(cc3_g)

        entry3 = [ex2["pe"], ex2["act"], ex2["dve"], ex2["out"]]
        io3 = dict(attc_mode="gather", attc_g=cc_a1_out, cc_wait=(sCC.h, 3),
                   xres_chunk=None,
                   wo=wo1, wg=wg1, wu=wu1, wd=wd1,
                   ident=ident, identf=identf, eps=eps, x1scr=x3scr,
                   xsum_t=xsum, out_mode="yout", out=yout)
        with nc.named_scope("P3"):
            ex3 = emit_p1(E, io3, entry3, sparsify=False, final_tm=True, tag="p3")

        osem, oval = ex3["out"]
        def fin_s(s):
            s.wait_ge(osem, oval)
        block.sync(fin_s)
    return nc




def _prearr_dmat(w):
    Dd, F = w.shape
    K = Dd // 128
    return np.ascontiguousarray(w.reshape(K, 128, F).transpose(1, 0, 2).reshape(128, K * F))


def _prearr_ffn(w):
    return np.ascontiguousarray(
        w.reshape(DC, 128, NG, GSZ).transpose(1, 2, 0, 3).reshape(128, NG * DC * GSZ))


def _prearr_wd(w):
    return np.ascontiguousarray(
        w.reshape(NG, GC, 128, D).transpose(2, 0, 1, 3).reshape(128, NG * GC * D))


def _prearr_x0(x0):
    return np.ascontiguousarray(
        x0.reshape(NSEG, SEG, DC, 128).transpose(3, 0, 2, 1).reshape(128, NSEG * DC * SEG))


def _scramble_fm(att_fm):
    A = att_fm.reshape(16, 128, NSEG, 32, 16)
    C = A.transpose(4, 1, 2, 0, 3)
    return np.ascontiguousarray(C.reshape(D, T))


def _p2_consts():
    inv = np.power(10000.0, -2.0 * np.arange(64) / 128.0)
    ang = np.arange(T)[:, None].astype(np.float64) * inv[None, :]
    cosT = np.ascontiguousarray(np.tile(np.cos(ang), (1, 2)).T.astype(np.float32))
    sinT = np.tile(np.sin(ang), (1, 2)).T.astype(np.float32)
    sinT[:64, :] *= -1.0
    sinT = np.ascontiguousarray(sinT)
    p = np.arange(128)[:, None]
    c = np.arange(512)[None, :]
    dmask = np.zeros((128, 4 * 512), BF)
    for r in range(4):
        dmask[:, r * 512:(r + 1) * 512] = (c >= r * 128 + p).astype(BF)
    negc = np.full((128, 1), -CSHIFT, np.float32)
    return cosT, sinT, dmask, negc


def kernel(L0_attn_x, L0_attn_Wq, L0_attn_Wk, L0_attn_Wv, L0_attn_Wo, L0_attn_betas,
           L0_ffn_w_gate, L0_ffn_w_up, L0_ffn_w_down,
           L1_attn_Wq, L1_attn_Wk, L1_attn_Wv, L1_attn_Wo,
           L1_ffn_w_gate, L1_ffn_w_up, L1_ffn_w_down):
    cores = list(range(N_CORES))
    x0T = np.ascontiguousarray(np.asarray(L0_attn_x)[0].T)
    ident = np.eye(128, dtype=BF)
    identf = np.eye(128, dtype=np.float32)
    eps = np.full((1, 1), LN_EPS, np.float32)
    _, binmask, negc = _consts_p0()
    cosT, sinT, dmask, _ = _p2_consts()
    wg1 = np.ascontiguousarray(L1_ffn_w_gate.T)
    wu1 = np.ascontiguousarray(L1_ffn_w_up.T)
    wd1 = np.ascontiguousarray(L1_ffn_w_down.T)
    betas16 = L0_attn_betas.reshape(16, DK)
    x0s = _prearr_x0(np.asarray(L0_attn_x)[0])
    wo0p, wo1p = _prearr_dmat(L0_attn_Wo), _prearr_dmat(L1_attn_Wo)
    wg0p, wu0p = _prearr_ffn(L0_ffn_w_gate), _prearr_ffn(L0_ffn_w_up)
    wd0p = _prearr_wd(L0_ffn_w_down)
    wg1p, wu1p, wd1p = _prearr_ffn(wg1), _prearr_ffn(wu1), _prearr_wd(wd1)
    maps = []
    for c in cores:
        sl = slice(c * TPC, (c + 1) * TPC)
        hs = slice(c * 256, (c + 1) * 256)
        kv = slice((c // 2) * 128, (c // 2 + 1) * 128)
        maps.append({
            "x0s": x0s,
            "xres": np.ascontiguousarray(x0T[:, sl]),
            "wq0": _prearr_dmat(L0_attn_Wq[:, hs]),
            "wk0": _prearr_dmat(L0_attn_Wk[:, hs]),
            "wv0": _prearr_dmat(L0_attn_Wv[:, hs]),
            "betas": np.ascontiguousarray(betas16[2 * c:2 * c + 2, :].reshape(1, 256)),
            "wo0": wo0p, "wg0": wg0p, "wu0": wu0p, "wd0": wd0p,
            "wq1": _prearr_dmat(L1_attn_Wq[:, hs]),
            "wk1": _prearr_dmat(L1_attn_Wk[:, kv]),
            "wv1": _prearr_dmat(L1_attn_Wv[:, kv]),
            "ropec": cosT, "ropes": sinT,
            "wo1": wo1p, "wg1": wg1p, "wu1": wu1p, "wd1": wd1p,
            "ident": ident, "identf": identf, "binmask": binmask,
            "dmask": dmask, "negc": negc, "epsrow": eps,
        })
    res = run_bass_kernel_spmd(build_fused(), maps, core_ids=cores)
    y = np.concatenate([r["yout"] for r in res.results], axis=0)
    return y.reshape(1, T, D)


def _kernel_unfused(L0_attn_x, L0_attn_Wq, L0_attn_Wk, L0_attn_Wv, L0_attn_Wo, L0_attn_betas,
           L0_ffn_w_gate, L0_ffn_w_up, L0_ffn_w_down,
           L1_attn_Wq, L1_attn_Wk, L1_attn_Wv, L1_attn_Wo,
           L1_ffn_w_gate, L1_ffn_w_up, L1_ffn_w_down):
    cores = list(range(N_CORES))
    x0T = np.ascontiguousarray(np.asarray(L0_attn_x)[0].T)
    ident = np.eye(128, dtype=BF)
    identf = np.eye(128, dtype=np.float32)
    eps = np.full((1, 1), LN_EPS, np.float32)

    res0 = run_bass_kernel_spmd(
        build_p0(), _p0_in_maps(x0T, L0_attn_Wq, L0_attn_Wk, L0_attn_Wv, L0_attn_betas),
        core_ids=cores)
    att0 = np.concatenate([r["att0"] for r in res0.results], axis=0)
    att0s = _scramble_fm(att0)

    maps1 = []
    for c in cores:
        sl = slice(c * TPC, (c + 1) * TPC)
        maps1.append({
            "attc": np.ascontiguousarray(att0s[:, sl]),
            "xres": np.ascontiguousarray(x0T[:, sl]),
            "wo": L0_attn_Wo, "wg": L0_ffn_w_gate, "wu": L0_ffn_w_up,
            "wd": L0_ffn_w_down,
            "ident": ident, "identf": identf, "epsrow": eps,
        })
    res1 = run_bass_kernel_spmd(build_p1(True, False), maps1, core_ids=cores)
    x2T = np.concatenate([r["x2T"] for r in res1.results], axis=1)

    cosT, sinT, dmask, negc = _p2_consts()
    maps2 = []
    for c in cores:
        kv = c // 2
        maps2.append({
            "x2T": x2T,
            "wq": np.ascontiguousarray(L1_attn_Wq[:, c * 256:(c + 1) * 256]),
            "wk": np.ascontiguousarray(L1_attn_Wk[:, kv * 128:(kv + 1) * 128]),
            "wv": np.ascontiguousarray(L1_attn_Wv[:, kv * 128:(kv + 1) * 128]),
            "ropec": cosT, "ropes": sinT,
            "ident": ident, "dmask": dmask, "negc": negc,
        })
    res2 = run_bass_kernel_spmd(build_p2(), maps2, core_ids=cores)
    att1 = np.concatenate([r["att1"] for r in res2.results], axis=0)

    wg1 = np.ascontiguousarray(L1_ffn_w_gate.T)
    wu1 = np.ascontiguousarray(L1_ffn_w_up.T)
    wd1 = np.ascontiguousarray(L1_ffn_w_down.T)
    maps3 = []
    for c in cores:
        sl = slice(c * TPC, (c + 1) * TPC)
        maps3.append({
            "attc": np.ascontiguousarray(att1[:, sl]),
            "xres": np.ascontiguousarray(x2T[:, sl]),
            "wo": L1_attn_Wo, "wg": wg1, "wu": wu1, "wd": wd1,
            "ident": ident, "identf": identf, "epsrow": eps,
        })
    res3 = run_bass_kernel_spmd(build_p1(False, True), maps3, core_ids=cores)
    y = np.concatenate([r["yout"] for r in res3.results], axis=0)
    return y.reshape(1, T, D)


# version 13
# speedup vs baseline: 1.1323x; 1.0120x over previous
import contextlib
import numpy as np
import ml_dtypes
import concourse.bass as bass
import concourse.mybir as mybir
from concourse.bass_utils import run_bass_kernel_spmd

F32 = mybir.dt.float32
BF16 = mybir.dt.bfloat16
AF = mybir.ActivationFunctionType
ALU = mybir.AluOpType
AX = mybir.AxisListType
BF = ml_dtypes.bfloat16

N_CORES = 8
D = 2048
T = 8192
DC = D // 128
SEG = 512
NSEG = T // SEG
DK = 128
HPC = 2
CSHIFT = 20.0
ISQRT_DK = float(1.0 / np.sqrt(np.float32(DK)))
TPC = T // N_CORES


DRAIN_DVE = True
DRAIN_ACT = False


class Drained:
    _PASSTHRU = {"wait_ge", "drain", "nop", "wait_op"}

    def __init__(self, eng):
        self._e = eng

    def __getattr__(self, name):
        f = getattr(self._e, name)
        en = DRAIN_ACT if type(self._e).__name__.endswith("ScalarEngine") else DRAIN_DVE
        if name in Drained._PASSTHRU or not callable(f) or not en:
            return f
        def wrapped(*a, **k):
            inst = f(*a, **k)
            self._e.drain()
            return inst
        return wrapped

    def drain_if(self):
        pass


class Sem:

    def __init__(self, ctx, nc, name):
        self.h = ctx.enter_context(nc.semaphore(name))
        self.n = 0

    def inc(self, inst, k=1):
        inst.then_inc(self.h, k)
        self.n += k
        return self.n

    def wait(self, eng, n=None):
        eng.wait_ge(self.h, self.n if n is None else n)



class Env:

    def __init__(self, nc, block, ctx):
        self.nc, self.block, self.ctx = nc, block, ctx
        self.VEC = lambda f: block.vector(lambda v: f(Drained(v)))
        self.SCA = lambda f: block.scalar(lambda a: f(Drained(a)))

    def sem(self, name):
        return Sem(self.ctx, self.nc, name)


def _standalone(emit, io_builder):
    nc = bass.Bass("TRN2", target_bir_lowering=False, debug=False)
    ctx = contextlib.ExitStack()
    with ctx:
        io = io_builder(nc)
        block = ctx.enter_context(nc.Block())
        E = Env(nc, block, ctx)
        exits = emit(E, io, [])
        osem, oval = exits["out"]
        def fin_s(s):
            s.wait_ge(osem, oval)
        block.sync(fin_s)
    return nc


def build_p0(nseg=NSEG):
    def io_builder(nc):
        return dict(
            x0s=nc.dram_tensor("x0s", [128, NSEG * DC * SEG], F32, kind="ExternalInput").ap(),
            wq=nc.dram_tensor("wq", [128, DC * HPC * DK], F32, kind="ExternalInput").ap(),
            wk=nc.dram_tensor("wk", [128, DC * HPC * DK], F32, kind="ExternalInput").ap(),
            wv=nc.dram_tensor("wv", [128, DC * HPC * DK], F32, kind="ExternalInput").ap(),
            betas=nc.dram_tensor("betas", [1, HPC * DK], F32, kind="ExternalInput").ap(),
            ident=nc.dram_tensor("ident", [128, 128], BF16, kind="ExternalInput").ap(),
            binmask=nc.dram_tensor("binmask", [SEG, SEG], BF16, kind="ExternalInput").ap(),
            negc=nc.dram_tensor("negc", [128, 1], F32, kind="ExternalInput").ap(),
            att0=nc.dram_tensor("att0", [HPC * DK, T], BF16, kind="ExternalOutput").ap(),
        )
    return _standalone(lambda E, io, entry: emit_p0(E, io, entry, nseg=nseg), io_builder)


def emit_p0(E, io, entry, nseg=NSEG, tag="p0"):
    nc, block, ctx = E.nc, E.block, E.ctx
    x0s, wq, wk, wv = io["x0s"], io["wq"], io["wk"], io["wv"]
    betas, ident_in, binmask_in, negc_in = io["betas"], io["ident"], io["binmask"], io["negc"]
    att0 = io["att0"]
    with contextlib.ExitStack() as pctx:
        sb = lambda name, shape, dt: pctx.enter_context(nc.sbuf_tensor(tag + name, shape, dt))
        ps = lambda name: pctx.enter_context(nc.psum_tensor(tag + name, [128, 512], F32))

        wq_sb = sb("wq_sb", [128, DC * HPC * DK], BF16)
        wk_sb = sb("wk_sb", [128, DC * HPC * DK], BF16)
        wv_sb = sb("wv_sb", [128, DC * HPC * DK], BF16)
        xseg = [sb(f"xseg{i}", [128, DC * SEG], BF16) for i in range(2)]
        ident = sb("ident_sb", [128, 128], BF16)
        binmask = sb("binmask_sb", [128, 4 * SEG], BF16)
        negc = sb("negc_sb", [128, 1], F32)
        ones_f = sb("ones_f", [1, 128], F32)
        beta_f = sb("beta_f", [1, HPC * DK], F32)
        bcast = sb("bcast", [128, HPC * DK], F32)
        mem_f = sb("mem_f", [128, HPC * DK], F32)
        z_f = sb("z_f", [128, HPC], F32)
        memz = sb("memz", [128, HPC * 129], BF16)
        q_sb = sb("q_sb", [128, SEG], BF16)
        k_sb = sb("k_sb", [128, SEG], BF16)
        sq_sb = sb("sq_sb", [128, SEG], BF16)
        sk_sb = sb("sk_sb", [128, SEG], BF16)
        sk_tm = sb("sk_tm", [128, SEG], BF16)
        v_aug = sb("v_aug", [128, 4 * 129], BF16)
        pst = sb("pst", [128, 4 * SEG], BF16)
        ndelta = sb("ndelta", [128, SEG], BF16)
        tq1 = sb("tq1", [128, SEG], F32)
        tq2 = sb("tq2", [128, SEG], F32)
        tk1 = sb("tk1", [128, SEG], F32)
        tk2 = sb("tk2", [128, SEG], F32)
        att_d = sb("att_d", [128, 128], F32)
        att_m = sb("att_m", [128, 128], F32)
        cmb = sb("cmb", [128, 128], F32)
        att_bf = sb("att_bf", [128, 512], BF16)
        att_tile = sb("att_tile", [128, SEG], BF16)
        rc = sb("rc", [128, 4], F32)
        zadd = sb("zadd", [128, 1], F32)

        p_q = ps("p_q")
        p_k = ps("p_k")
        p_v = ps("p_v")
        p_st = ps("p_st")
        p_ad = ps("p_ad")
        p_am = ps("p_am")
        p_tr = ps("p_tr")
        p_mu = ps("p_mu")

        sW = E.sem(tag + "sW")
        sW2 = E.sem(tag + "sW2")
        sX = [E.sem(tag + "sX0"), E.sem(tag + "sX1")]
        sP = E.sem(tag + "sP")
        sA = E.sem(tag + "sA")
        sV = E.sem(tag + "sV")
        sO = E.sem(tag + "sO")

        x0s_v = x0s.rearrange("p (s q) -> p s q", q=DC * SEG)

        def load_xseg(g, seg, buf):
            sX[buf].inc(g.dma_start(out=xseg[buf][:, :], in_=x0s_v[:, seg, :]), 16)

        if True:
            VEC, SCA = E.VEC, E.SCA
            def init_g(g):
                for se, va in entry:
                    g.wait_ge(se, va)
                for w, tsb in ((wq, wq_sb), (wk, wk_sb), (wv, wv_sb)):
                    sW.inc(g.dma_start(out=tsb[:, :], in_=w), 16)
                sW.inc(g.dma_start(out=beta_f[:, :], in_=betas), 16)
                load_xseg(g, 0, 0)
            block.gpsimd(init_g)

            def init_s(s):
                for se, va in entry:
                    s.wait_ge(se, va)
                sW2.inc(s.dma_start(out=ident[:, :], in_=ident_in), 16)
                sW2.inc(s.dma_start(
                    out=binmask[:, :].rearrange("p (k t) -> p k t", t=SEG),
                    in_=binmask_in.rearrange("(k p) t -> p k t", p=128)), 16)
                sW2.inc(s.dma_start(out=negc[:, :], in_=negc_in), 16)
            block.sync(init_s)

            def init_v(v):
                for se, va in entry:
                    v.wait_ge(se, va)
                v.memset(ones_f[:, :], 1.0)
                v.memset(mem_f[:, :], 0.0)
                v.memset(z_f[:, :], 1.0 / DK)
                v.memset(memz[:, :], 0.0)
                v.memset(v_aug[:, :], 1.0)
                inst = None
                for h in range(HPC):
                    inst = v.tensor_copy(memz[:, h * 129 + 128:h * 129 + 129], z_f[:, h:h + 1])
                sV.inc(inst, 1)
            VEC(init_v)
            v_init = sV.n

            def init_a(a):
                for se, va in entry:
                    a.wait_ge(se, va)
                a.wait_ge(sW.h, sW.n)
                a.wait_ge(sW2.h, sW2.n)
                sA.inc(a.activation(out=beta_f[:, :], in_=beta_f[:, :], func=AF.Sigmoid), 1)
            SCA(init_a)
            a_sig = sA.n

            def init_t(t):
                t.wait_ge(sA.h, a_sig)
                t.wait_ge(sV.h, v_init)
                for h in range(HPC):
                    mmi = t.matmul(p_mu[:, h * 128:(h + 1) * 128], ones_f[:, :],
                             beta_f[:, h * 128:(h + 1) * 128], start=True, stop=True)
                sP.inc(mmi, 1)
            block.tensor(init_t)
            p_bc = sP.n

            def init_v2(v):
                v.wait_ge(sP.h, p_bc)
                sV.inc(v.tensor_copy(bcast[:, :], p_mu[:, 0:256]), 1)
            VEC(init_v2)

            prev = dict(a_sigk=0, v_sk=0, v_vaug=0, v_comb3=0, v_dl3=0,
                        v_kt=0, v_memz=sV.n, v_atile=[0, 0, 0, 0], pe_v=0,
                        aexp=[0, 0, 0, 0], v_mu=0, o_n=0)

            for seg in range(nseg):
                buf = seg % 2
                xv = xseg[buf][:, :]

                if seg + 1 < nseg:
                    def pre_g(g, seg=seg, guard=prev["pe_v"]):
                        if seg >= 1:
                            g.wait_ge(sP.h, guard)
                        load_xseg(g, seg + 1, (seg + 1) % 2)
                    block.gpsimd(pre_g)

                for h in range(HPC):
                    n = sP.n
                    pe_q = n + 1; pe_k = n + 2; pe_v = n + 3
                    pe_st = [n + 4 + i for i in range(4)]
                    pe_ad = [0] * 4; pe_am = [0] * 4
                    m = n + 7
                    for ss in range(4):
                        m += 1; pe_ad[ss] = m
                        m += 1; pe_am[ss] = m
                    pe_tr = [m + 1 + i for i in range(4)]
                    pe_dl = [m + 5 + i for i in range(4)]
                    pe_kt = m + 9
                    pe_mu = m + 10
                    sP.n = pe_mu

                    a_qb = sA.n + 1
                    a_qs = sA.n + 2
                    a_kb = sA.n + 3
                    a_ks = sA.n + 4
                    aexp = [sA.n + 5 + i for i in range(4)]
                    sA.n += 8

                    v_sq = sV.n + 1
                    v_sk = sV.n + 2
                    v_vaug = sV.n + 3
                    v_pst = [sV.n + 4 + i for i in range(4)]
                    v_comb = [sV.n + 8 + i for i in range(4)]
                    v_atile = [sV.n + 12 + i for i in range(4)]
                    v_dl = [sV.n + 16 + i for i in range(4)]
                    v_kt = sV.n + 20
                    v_memz = sV.n + 21
                    sV.n += 21

                    stb = [p_st, p_mu, p_st, p_mu]
                    adb = [p_ad, p_q, p_ad, p_q]
                    amb = [p_am, p_k, p_am, p_k]
                    trb = [p_tr, p_st, p_tr, p_st]
                    dlb = [p_am, p_ad, p_am, p_ad]

                    def ten_body(t, seg=seg, h=h, xv=xv, p=dict(prev),
                                 aexp=list(aexp), v_pst=list(v_pst),
                                 v_comb=list(v_comb), v_atile=list(v_atile),
                                 v_sq=v_sq, v_sk=v_sk, v_vaug=v_vaug,
                                 v_dl=list(v_dl), v_kt=v_kt):
                        wc = lambda dc: dc * (HPC * DK) + h * DK
                        t.wait_ge(sX[seg % 2].h, 16 * (seg // 2 + 1))
                        if p["a_sigk"]:
                            t.wait_ge(sA.h, p["a_sigk"])
                        if p["v_comb3"]:
                            t.wait_ge(sV.h, p["v_comb3"])
                        if p["v_vaug"]:
                            t.wait_ge(sV.h, p["v_vaug"])
                        mmi = None
                        for dc in range(DC):
                            mmi = t.matmul(p_q[:, :], wq_sb[:, wc(dc):wc(dc) + DK],
                                           xv[:, dc * SEG:(dc + 1) * SEG],
                                           start=(dc == 0), stop=(dc == DC - 1))
                        mmi.then_inc(sP.h, 1)
                        for dc in range(DC):
                            mmi = t.matmul(p_k[:, :], wk_sb[:, wc(dc):wc(dc) + DK],
                                           xv[:, dc * SEG:(dc + 1) * SEG],
                                           start=(dc == 0), stop=(dc == DC - 1))
                        mmi.then_inc(sP.h, 1)
                        for tt in range(4):
                            for dc in range(DC):
                                mmi = t.matmul(p_v[:, tt * 128:(tt + 1) * 128],
                                               xv[:, dc * SEG + tt * 128:dc * SEG + tt * 128 + 128],
                                               wv_sb[:, wc(dc):wc(dc) + DK],
                                               start=(dc == 0), stop=(dc == DC - 1))
                        mmi.then_inc(sP.h, 1)
                        for tt in range(4):
                            if tt == 0:
                                t.wait_ge(sA.h, a_ks)
                                if p["aexp"][2]:
                                    t.wait_ge(sA.h, p["aexp"][2])
                                if p["v_atile"][3]:
                                    t.wait_ge(sV.h, p["v_atile"][3])
                            if tt == 1:
                                if p["aexp"][3]:
                                    t.wait_ge(sA.h, p["aexp"][3])
                                if p["v_memz"]:
                                    t.wait_ge(sV.h, p["v_memz"])
                            if tt >= 2:
                                t.wait_ge(sA.h, aexp[tt - 2])
                            t.matmul(stb[tt][:, :], k_sb[:, tt * 128:(tt + 1) * 128],
                                     q_sb[:, :], start=True,
                                     stop=True).then_inc(sP.h, 1)
                        for ss in range(4):
                            if ss == 0:
                                t.wait_ge(sV.h, v_pst[3])
                                t.wait_ge(sV.h, v_vaug)
                                t.wait_ge(sV.h, v_sq)
                                if p["v_memz"]:
                                    t.wait_ge(sV.h, p["v_memz"])
                            if ss == 1:
                                t.wait_ge(sA.h, a_qb)
                                t.wait_ge(sA.h, a_kb)
                            if ss >= 2:
                                t.wait_ge(sV.h, v_comb[ss - 2])
                            mmi = None
                            for tt in range(4):
                                mmi = t.matmul(adb[ss][:, 0:129],
                                               pst[:, tt * SEG + ss * 128:tt * SEG + ss * 128 + 128],
                                               v_aug[:, tt * 129:(tt + 1) * 129],
                                               start=(tt == 0), stop=(tt == 3))
                            mmi.then_inc(sP.h, 1)
                            t.matmul(amb[ss][:, 0:129], sq_sb[:, ss * 128:(ss + 1) * 128],
                                     memz[:, h * 129:(h + 1) * 129], start=True,
                                     stop=True).then_inc(sP.h, 1)
                        for ss in range(4):
                            t.wait_ge(sV.h, v_comb[ss])
                            if ss == 0 and p["v_kt"]:
                                t.wait_ge(sV.h, p["v_kt"])
                            if ss == 1:
                                t.wait_ge(sA.h, aexp[2])
                            if ss >= 2:
                                t.wait_ge(sV.h, v_atile[ss - 2])
                            t.matmul(trb[ss][:, 0:128], att_bf[:, ss * 128:(ss + 1) * 128],
                                     ident[:, :], start=True, stop=True).then_inc(sP.h, 1)
                        for tt in range(4):
                            if tt == 0:
                                t.wait_ge(sV.h, v_sk)
                                t.wait_ge(sV.h, v_comb[3])
                            if tt >= 2:
                                t.wait_ge(sV.h, v_dl[tt - 2])
                            t.matmul(dlb[tt][:, 0:129], sk_sb[:, tt * 128:(tt + 1) * 128],
                                     memz[:, h * 129:(h + 1) * 129], start=True,
                                     stop=True).then_inc(sP.h, 1)
                        t.wait_ge(sV.h, v_atile[2])
                        mmi = None
                        for tt in range(4):
                            mmi = t.matmul(p_tr[:, tt * 128:(tt + 1) * 128],
                                           sk_sb[:, tt * 128:(tt + 1) * 128], ident[:, :],
                                           start=True, stop=True)
                        mmi.then_inc(sP.h, 1)
                        t.wait_ge(sV.h, v_kt)
                        t.wait_ge(sV.h, v_dl[3])
                        t.wait_ge(sA.h, aexp[3])
                        mmi = None
                        for tt in range(4):
                            mmi = t.matmul(p_mu[:, 0:128], sk_tm[:, tt * 128:(tt + 1) * 128],
                                           ndelta[:, tt * 128:(tt + 1) * 128],
                                           start=(tt == 0), stop=(tt == 3))
                        mmi.then_inc(sP.h, 1)
                    block.tensor(ten_body)

                    def sca_body(a, p=dict(prev), pe_q=pe_q, pe_k=pe_k,
                                 pe_st=list(pe_st)):
                        a.wait_ge(sP.h, pe_q)
                        if p["v_sk"]:
                            a.wait_ge(sV.h, p["v_sk"])
                        a.activation(out=q_sb[:, :], in_=p_q[:, :], func=AF.Copy)
                        a.activation(out=tq1[:, :], in_=p_q[:, :], func=AF.Relu)
                        a.activation(out=tq2[:, :], in_=p_q[:, :], func=AF.Relu,
                                     scale=-1.0).then_inc(sA.h, 1)
                        a.activation(out=tq2[:, :], in_=tq2[:, :], func=AF.Exp,
                                     scale=-1.0).then_inc(sA.h, 1)
                        a.wait_ge(sP.h, pe_k)
                        a.activation(out=k_sb[:, :], in_=p_k[:, :], func=AF.Copy)
                        a.activation(out=tk1[:, :], in_=p_k[:, :], func=AF.Relu)
                        a.activation(out=tk2[:, :], in_=p_k[:, :], func=AF.Relu,
                                     scale=-1.0).then_inc(sA.h, 1)
                        a.activation(out=tk2[:, :], in_=tk2[:, :], func=AF.Exp,
                                     scale=-1.0).then_inc(sA.h, 1)
                        for tt in range(4):
                            a.wait_ge(sP.h, pe_st[tt])
                            a.activation(out=pst[:, tt * SEG:(tt + 1) * SEG],
                                         in_=stb[tt][:, :], func=AF.Exp,
                                         scale=ISQRT_DK,
                                         bias=negc[:, :]).then_inc(sA.h, 1)
                    block.scalar(lambda a: sca_body(Drained(a)))

                    def vec_body(vr, h=h, p=dict(prev), pe_v=pe_v,
                                 pe_ad=list(pe_ad), pe_am=list(pe_am),
                                 pe_tr=list(pe_tr), pe_dl=list(pe_dl),
                                 pe_kt=pe_kt, pe_mu=pe_mu,
                                 a_qs=a_qs, a_ks=a_ks, aexp=list(aexp)):
                        v = Drained(vr)
                        v.wait_ge(sA.h, a_qs)
                        v.tensor_add(sq_sb[:, :], tq1[:, :], tq2[:, :]).then_inc(sV.h, 1)
                        v.wait_ge(sA.h, a_ks)
                        v.tensor_add(sk_sb[:, :], tk1[:, :], tk2[:, :])
                        v.reduce_sum(zadd[:, :], sk_sb[:, :], axis=AX.X).then_inc(sV.h, 1)
                        v.wait_ge(sP.h, pe_v)
                        inst = None
                        for tt in range(4):
                            inst = v.tensor_copy(v_aug[:, tt * 129:tt * 129 + 128],
                                                 p_v[:, tt * 128:(tt + 1) * 128])
                        inst.then_inc(sV.h, 1)
                        for tt in range(4):
                            v.wait_ge(sA.h, aexp[tt])
                            v.tensor_mul(pst[:, tt * SEG:(tt + 1) * SEG],
                                         pst[:, tt * SEG:(tt + 1) * SEG],
                                         binmask[:, tt * SEG:(tt + 1) * SEG]).then_inc(sV.h, 1)
                        for ss in range(4):
                            v.wait_ge(sP.h, pe_ad[ss])
                            v.reciprocal(rc[:, 0:1], adb[ss][:, 128:129])
                            v.tensor_scalar(att_d[:, :], adb[ss][:, 0:128], rc[:, 0:1],
                                            None, ALU.mult)
                            v.wait_ge(sP.h, pe_am[ss])
                            v.reciprocal(rc[:, 1:2], amb[ss][:, 128:129])
                            v.tensor_scalar(att_m[:, :], amb[ss][:, 0:128], rc[:, 1:2],
                                            None, ALU.mult)
                            v.tensor_sub(cmb[:, :], att_m[:, :], att_d[:, :])
                            v.tensor_mul(cmb[:, :], cmb[:, :], bcast[:, h * 128:(h + 1) * 128])
                            v.tensor_add(att_bf[:, ss * 128:(ss + 1) * 128], att_d[:, :],
                                         cmb[:, :]).then_inc(sV.h, 1)
                        for ss in range(4):
                            v.wait_ge(sP.h, pe_tr[ss])
                            if ss == 0 and p["o_n"]:
                                v.wait_ge(sO.h, p["o_n"])
                            v.tensor_copy(att_tile[:, ss * 128:(ss + 1) * 128],
                                          trb[ss][:, 0:128]).then_inc(sV.h, 1)
                        for tt in range(4):
                            v.wait_ge(sP.h, pe_dl[tt])
                            v.reciprocal(rc[:, 2:3], dlb[tt][:, 128:129])
                            v.tensor_scalar(rc[:, 3:4], rc[:, 2:3], -1.0, None, ALU.mult)
                            v.scalar_tensor_tensor(
                                ndelta[:, tt * 128:(tt + 1) * 128], dlb[tt][:, 0:128],
                                rc[:, 3:4], v_aug[:, tt * 129:tt * 129 + 128],
                                ALU.mult, ALU.add).then_inc(sV.h, 1)
                        v.wait_ge(sP.h, pe_kt)
                        v.tensor_copy(sk_tm[:, :], p_tr[:, :]).then_inc(sV.h, 1)
                        v.wait_ge(sP.h, pe_mu)
                        v.tensor_add(mem_f[:, h * 128:(h + 1) * 128],
                                     mem_f[:, h * 128:(h + 1) * 128], p_mu[:, 0:128])
                        v.tensor_add(z_f[:, h:h + 1], z_f[:, h:h + 1], zadd[:, :])
                        v.tensor_copy(memz[:, h * 129:h * 129 + 128],
                                      mem_f[:, h * 128:(h + 1) * 128])
                        v.tensor_copy(memz[:, h * 129 + 128:h * 129 + 129],
                                      z_f[:, h:h + 1]).then_inc(sV.h, 1)
                    block.vector(vec_body)

                    def out_s(s, h=h, seg=seg, v_at=v_atile[3]):
                        s.wait_ge(sV.h, v_at)
                        sO.inc(s.dma_start(out=att0[h * DK:(h + 1) * DK, seg * SEG:(seg + 1) * SEG],
                                           in_=att_tile[:, :]), 16)
                    block.sync(out_s)

                    prev = dict(a_sigk=a_kb, v_sk=v_sk, v_vaug=v_vaug,
                                v_comb3=v_comb[3], v_dl3=v_dl[3], v_kt=v_kt,
                                v_memz=v_memz, v_atile=list(v_atile), pe_v=pe_v,
                                aexp=list(aexp), v_mu=v_memz, o_n=sO.n)


    return {"pe": (sP.h, sP.n), "act": (sA.h, sA.n), "dve": (sV.h, sV.n),
            "out": (sO.h, sO.n)}



def _consts_p0():
    ident = np.eye(128, dtype=BF)
    t_idx = np.arange(SEG)[:, None]
    s_idx = np.arange(SEG)[None, :]
    binmask = (s_idx >= t_idx).astype(BF)
    negc = np.full((128, 1), -CSHIFT, np.float32)
    return ident, binmask, negc


def _p0_in_maps(x0T_np, Wq, Wk, Wv, betas):
    ident, binmask, negc = _consts_p0()
    x0s = _prearr_x0(np.ascontiguousarray(x0T_np.T))
    maps = []
    for c in range(N_CORES):
        cols = slice(c * HPC * DK, (c + 1) * HPC * DK)
        maps.append({
            "x0s": x0s,
            "wq": _prearr_dmat(np.ascontiguousarray(Wq[:, cols])),
            "wk": _prearr_dmat(np.ascontiguousarray(Wk[:, cols])),
            "wv": _prearr_dmat(np.ascontiguousarray(Wv[:, cols])),
            "betas": np.ascontiguousarray(
                betas.reshape(16, DK)[2 * c:2 * c + 2, :].reshape(1, HPC * DK)),
            "ident": ident, "binmask": binmask, "negc": negc,
        })
    return maps





FFN_I = 8192
GSZ = 512
NG = FFN_I // GSZ
GC = GSZ // 128
NT = TPC // 128
LN_EPS = 1e-5


def build_p1(sparsify, final_tm):
    def io_builder(nc):
        io = dict(
            attc_mode="direct",
            attc=nc.dram_tensor("attc", [D, TPC], BF16, kind="ExternalInput").ap(),
            wo=nc.dram_tensor("wo", [128, DC * D], F32, kind="ExternalInput").ap(),
            wg=nc.dram_tensor("wg", [128, NG * DC * GSZ], F32, kind="ExternalInput").ap(),
            wu=nc.dram_tensor("wu", [128, NG * DC * GSZ], F32, kind="ExternalInput").ap(),
            wd=nc.dram_tensor("wd", [128, NG * GC * D], F32, kind="ExternalInput").ap(),
            ident=nc.dram_tensor("ident", [128, 128], BF16, kind="ExternalInput").ap(),
            identf=nc.dram_tensor("identf", [128, 128], F32, kind="ExternalInput").ap(),
            eps=nc.dram_tensor("epsrow", [1, 1], F32, kind="ExternalInput").ap(),
            x1scr=nc.dram_tensor("x1scr", [D, TPC], F32).ap(),
            xsum_t=None, cc_wait=None,
        )
        xres = nc.dram_tensor("xres", [D, TPC], F32, kind="ExternalInput").ap()
        io["xres_chunk"] = lambda s, dc: xres[dc * 128:(dc + 1) * 128, :]
        if final_tm:
            io["out_mode"] = "yout"
            io["out"] = nc.dram_tensor("yout", [TPC, D], F32, kind="ExternalOutput").ap()
        else:
            io["out_mode"] = "x2T"
            io["out"] = nc.dram_tensor("x2T", [D, TPC], F32, kind="ExternalOutput").ap()
        return io
    return _standalone(lambda E, io, entry: emit_p1(E, io, entry, sparsify, final_tm),
                       io_builder)


def emit_p1(E, io, entry, sparsify, final_tm, tag="p1"):
    nc, block, ctx = E.nc, E.block, E.ctx
    wo, wg, wu, wd = io["wo"], io["wg"], io["wu"], io["wd"]
    ident_in, identf_in, eps_in = io["ident"], io["identf"], io["eps"]
    out = io["out"]
    x1scr = io["x1scr"]
    out_mode = io["out_mode"]
    resident_res = io["xres_chunk"] is None
    with contextlib.ExitStack() as pctx:
        sb = lambda name, shape, dt: pctx.enter_context(nc.sbuf_tensor(tag + name, shape, dt))

        if io["xsum_t"] is None:
            xsum = sb("xsum", [128, DC * TPC], F32)
        else:
            xsum = io["xsum_t"]
        xstr = [sb(f"xstr{i}", [128, TPC], F32) for i in range(2)]
        ident = sb("ident_sb", [128, 128], BF16)
        identf = sb("identf_sb", [128, 128], F32)
        ones_col = sb("ones_col", [128, 1], F32)
        ones_row = sb("ones_row", [1, 128], F32)
        eps_sb = sb("eps_sb", [1, 1], F32)
        strow = sb("strow", [1, 3 * TPC], F32)
        silu_s = sb("silu_s", [128, 512], F32)

        pA = pctx.enter_context(nc.psum_tensor(tag + "pA", [128, 1024], F32))
        pB = pctx.enter_context(nc.psum_tensor(tag + "pB", [128, 1024], F32))
        pC = pctx.enter_context(nc.psum_tensor(tag + "pC", [128, 512], F32))
        pD = pctx.enter_context(nc.psum_tensor(tag + "pD", [128, 512], F32))
        pE = pctx.enter_context(nc.psum_tensor(tag + "pE", [128, 512], F32))
        pF = pctx.enter_context(nc.psum_tensor(tag + "pF", [128, 512], F32))

        sLo = E.sem(tag + "sLo")
        sLgu = E.sem(tag + "sLgu")
        sLd = E.sem(tag + "sLd")
        sWi = E.sem(tag + "sWi")
        sAt = E.sem(tag + "sAt")
        sXr = [E.sem(tag + "sXr0"), E.sem(tag + "sXr1")]
        sP = E.sem(tag + "sP")
        sA = E.sem(tag + "sA")
        sV = E.sem(tag + "sV")
        sScr = E.sem(tag + "sScr")
        sO = E.sem(tag + "sO")

        if True:
            VEC, SCA = E.VEC, E.SCA

            def emit_ln(src, v_ready):
                pe_st = []
                for dc in range(DC):
                    def sq_a(a, dc=dc):
                        if dc == 0:
                            a.wait_ge(sV.h, v_ready)
                        if dc >= 2:
                            a.wait_ge(sP.h, pe_st[dc - 2])
                        sA.inc(a.activation(out=xstr[dc % 2][:, :],
                                            in_=src[:, dc * TPC:(dc + 1) * TPC],
                                            func=AF.Square), 1)
                    SCA(sq_a)
                    a_n = sA.n

                    def st_t(t, dc=dc, a_n=a_n):
                        if dc == 0:
                            t.wait_ge(sV.h, v_ready)
                        for tb, bank in ((0, pC), (1, pD)):
                            t.matmul(bank[0:1, 0:512], ones_col[:, :],
                                     src[:, dc * TPC + tb * 512:dc * TPC + tb * 512 + 512],
                                     start=(dc == 0), stop=(dc == DC - 1))
                        t.wait_ge(sA.h, a_n)
                        mmi = None
                        for tb, bank in ((0, pE), (1, pF)):
                            mmi = t.matmul(bank[0:1, 0:512], ones_col[:, :],
                                           xstr[dc % 2][:, tb * 512:tb * 512 + 512],
                                           start=(dc == 0), stop=(dc == DC - 1))
                        sP.inc(mmi, 1)
                    block.tensor(st_t)
                    pe_st.append(sP.n)
                pe_stats = sP.n

                def stat_v(v, pe_stats=pe_stats):
                    m = strow[0:1, 0:TPC]
                    w = strow[0:1, TPC:2 * TPC]
                    tmp = strow[0:1, 2 * TPC:3 * TPC]
                    v.wait_ge(sP.h, pe_stats)
                    v.tensor_scalar(m[0:1, 0:512], pC[0:1, 0:512], 1.0 / D, None, ALU.mult)
                    v.tensor_scalar(m[0:1, 512:1024], pD[0:1, 0:512], 1.0 / D, None, ALU.mult)
                    v.tensor_scalar(w[0:1, 0:512], pE[0:1, 0:512], 1.0 / D, None, ALU.mult)
                    v.tensor_scalar(w[0:1, 512:1024], pF[0:1, 0:512], 1.0 / D, None, ALU.mult)
                    v.tensor_mul(tmp, m, m)
                    sV.inc(v.tensor_sub(w, w, tmp), 1)
                    vv = sV.n

                    return vv
                v_var = VEC(stat_v) or sV.n

                def sd_a(a, v_var=v_var):
                    a.wait_ge(sV.h, v_var)
                    sA.inc(a.activation(out=strow[0:1, 2 * TPC:3 * TPC],
                                        in_=strow[0:1, TPC:2 * TPC],
                                        func=AF.Sqrt, bias=eps_sb[0:1, 0:1]), 1)
                SCA(sd_a)
                a_sd = sA.n

                def rstd_v(v, a_sd=a_sd):
                    v.wait_ge(sA.h, a_sd)
                    sd = strow[0:1, 2 * TPC:3 * TPC]
                    w = strow[0:1, TPC:2 * TPC]
                    m2 = strow[0:1, 0:TPC]
                    v.reciprocal(w, sd)
                    v.tensor_mul(sd, sd, w)
                    v.tensor_scalar(sd, sd, -1.0, 2.0, ALU.mult, ALU.add)
                    sV.inc(v.tensor_mul(w, w, sd), 1)
                VEC(rstd_v)
                v_rstd = sV.n

                def bc_t(t, v_rstd=v_rstd):
                    t.wait_ge(sV.h, v_rstd)
                    mmi = None
                    for tb in range(2):
                        t.matmul(pA[:, tb * 512:(tb + 1) * 512], ones_row[:, :],
                                 strow[0:1, tb * 512:(tb + 1) * 512], start=True, stop=True)
                        mmi = t.matmul(pB[:, tb * 512:(tb + 1) * 512], ones_row[:, :],
                                       strow[0:1, TPC + tb * 512:TPC + (tb + 1) * 512],
                                       start=True, stop=True)
                    sP.inc(mmi, 1)
                block.tensor(bc_t)
                return sP.n

            _scA = nc.named_scope(tag + "_compose"); _scA.__enter__()
            with nc.sbuf_tensor(tag + "attc_sb", [128, DC * TPC], BF16) as attc_sb, \
                 contextlib.ExitStack() as actx:

                v_scr = 0
                if io["attc_mode"] == "direct":
                    attc = io["attc"]
                    def attc_s(s):
                        for se, va in entry:
                            s.wait_ge(se, va)
                        sAt.inc(s.dma_start(
                            out=attc_sb[:, :].rearrange("p (k t) -> p k t", t=TPC),
                            in_=attc.rearrange("(k p) t -> p k t", p=128)), 16)
                    block.sync(attc_s)
                elif io["attc_mode"] == "gather":
                    attg = io["attc_g"]
                    ccw = io["cc_wait"]
                    def attc_s(s):
                        for se, va in entry:
                            s.wait_ge(se, va)
                        s.wait_ge(ccw[0], ccw[1])
                        pid = s.partition_id()
                        sAt.inc(s.dma_start(
                            out=attc_sb[:, :].rearrange("p (k t) -> p k t", t=TPC),
                            in_=attg.rearrange("(k p) t -> p k t", p=128)[:, :, bass.ds(pid * TPC, TPC)]), 16)
                    block.sync(attc_s)
                else:
                    attg = io["attc_g"]
                    ccw = io["cc_wait"]
                    with nc.sbuf_tensor(tag + "att_raw", [128, DC * TPC], BF16) as att_raw:
                        def attc_s(s):
                            for se, va in entry:
                                s.wait_ge(se, va)
                            s.wait_ge(ccw[0], ccw[1])
                            pid = s.partition_id()
                            sAt.inc(s.dma_start(
                                out=att_raw[:, :].rearrange("p (k t) -> p k t", t=TPC),
                                in_=attg.rearrange("(k p) t -> p k t", p=128)[:, :, bass.ds(pid * TPC, TPC)]), 16)
                        block.sync(attc_s)

                        def scr_v(v, n_at=None):
                            for se, va in entry:
                                v.wait_ge(se, va)
                            v.wait_ge(sAt.h, sAt.n)
                            dst = attc_sb[:, :].rearrange(
                                "p (j segl h w) -> p j segl h w", j=16, segl=2, h=16, w=32)
                            srcv = att_raw[:, :].rearrange(
                                "p (h segl w j) -> p h segl w j", h=16, segl=2, w=32, j=16
                            ).transpose([0, 4, 2, 1, 3])
                            sV.inc(v.tensor_copy(dst, srcv), 1)
                        VEC(scr_v)
                        v_scr = sV.n

                wo_sb = actx.enter_context(nc.sbuf_tensor(tag + "wo_sb", [128, DC * D], BF16))

                def init_g(g):
                    for se, va in entry:
                        g.wait_ge(se, va)
                    if v_scr:
                        g.wait_ge(sV.h, v_scr)
                    sLo.inc(g.dma_start(out=wo_sb[:, :], in_=wo), 16)
                block.gpsimd(init_g)

                xres_w = {}
                def init_s(s):
                    sWi.inc(s.dma_start(out=ident[:, :], in_=ident_in), 16)
                    sWi.inc(s.dma_start(out=identf[:, :], in_=identf_in), 16)
                    sWi.inc(s.dma_start(out=eps_sb[:, :], in_=eps_in), 16)
                    if not resident_res:
                        for i in range(2):
                            xres_w[i] = sXr[i].inc(s.dma_start(
                                out=xstr[i][:, :], in_=io["xres_chunk"](s, i)), 16)
                block.sync(init_s)

                def init_v(v):
                    for se, va in entry:
                        v.wait_ge(se, va)
                    v.memset(ones_col[:, :], 1.0)
                    sV.inc(v.memset(ones_row[:, :], 1.0), 1)
                VEC(init_v)

                vcomb = []
                for fc in range(DC):
                    for tb in range(2):
                        bank = pC if tb == 0 else pD
                        def comb_t(t, fc=fc, tb=tb, bank=bank):
                            if fc == 0 and tb == 0:
                                t.wait_ge(sLo.h, sLo.n)
                                t.wait_ge(sWi.h, sWi.n)
                                t.wait_ge(sAt.h, sAt.n)
                                if v_scr:
                                    t.wait_ge(sV.h, v_scr)
                            if len(vcomb) >= 2:
                                t.wait_ge(sV.h, vcomb[-2])
                            mmi = None
                            for ac in range(DC):
                                mmi = t.matmul(bank[:, 0:512],
                                               wo_sb[:, ac * D + fc * 128:ac * D + fc * 128 + 128],
                                               attc_sb[:, ac * TPC + tb * 512:ac * TPC + tb * 512 + 512],
                                               start=(ac == 0), stop=(ac == DC - 1))
                            sP.inc(mmi, 1)
                        block.tensor(comb_t)
                        pe_c = sP.n

                        def comb_v(v, fc=fc, tb=tb, bank=bank, pe_c=pe_c,
                                   w_n=(0 if resident_res else xres_w[fc])):
                            v.wait_ge(sP.h, pe_c)
                            dst = xsum[:, fc * TPC + tb * 512:fc * TPC + tb * 512 + 512]
                            if resident_res:
                                sV.inc(v.tensor_add(dst, bank[:, 0:512], dst), 1)
                            else:
                                v.wait_ge(sXr[fc % 2].h, w_n)
                                sV.inc(v.tensor_add(dst, bank[:, 0:512],
                                                    xstr[fc % 2][:, tb * 512:tb * 512 + 512]), 1)
                        VEC(comb_v)
                        vcomb.append(sV.n)
                    if fc + 2 < DC and not resident_res:
                        def pre_s(s, fc=fc, guard=vcomb[-1]):
                            s.wait_ge(sV.h, guard)
                            xres_w[fc + 2] = sXr[fc % 2].inc(
                                s.dma_start(out=xstr[fc % 2][:, :],
                                            in_=io["xres_chunk"](s, fc + 2)), 16)
                        block.sync(pre_s)
                v_combe = sV.n
                pe_combe = sP.n
            _scA.__exit__(None, None, None)

            with nc.sbuf_tensor(tag + "x1b", [128, DC * TPC], BF16) as x1b:
                _scL = nc.named_scope(tag + "_ln1"); _scL.__enter__()
                pe_bc1 = emit_ln(xsum, v_combe)
                scr_w = 0
                for dc in range(DC):
                    def ap_v(v, dc=dc, pe_bc1=pe_bc1):
                        if dc == 0:
                            v.wait_ge(sP.h, pe_bc1)
                        c = xsum[:, dc * TPC:(dc + 1) * TPC]
                        v.tensor_sub(c, c, pA[:, :])
                        sV.inc(v.tensor_mul(c, c, pB[:, :]), 1)
                    VEC(ap_v)
                    v_n = sV.n

                    def ap_a(a, dc=dc, v_n=v_n):
                        a.wait_ge(sV.h, v_n)
                        sA.inc(a.activation(out=x1b[:, dc * TPC:(dc + 1) * TPC],
                                            in_=xsum[:, dc * TPC:(dc + 1) * TPC],
                                            func=AF.Copy), 1)
                    SCA(ap_a)

                    def ap_s(s, dc=dc, v_n=v_n):
                        s.wait_ge(sV.h, v_n)
                        sScr.inc(s.dma_start(out=x1scr[dc * 128:(dc + 1) * 128, :],
                                             in_=xsum[:, dc * TPC:(dc + 1) * TPC]), 16)
                    block.sync(ap_s)
                a_x1b = sA.n
                v_ap1 = sV.n
                scr_w = sScr.n
                _scL.__exit__(None, None, None)

                _scF = nc.named_scope(tag + "_ffn"); _scF.__enter__()
                QSZ = NT * GSZ // 4
                with nc.sbuf_tensor(tag + "wg_sb", [128, DC * GSZ], BF16) as wg_sb, \
                     nc.sbuf_tensor(tag + "wu_sb", [128, DC * GSZ], BF16) as wu_sb, \
                     nc.sbuf_tensor(tag + "wd_sb", [128, GC * D], BF16) as wd_sb, \
                     nc.sbuf_tensor(tag + "h_tm0", [128, NT * GSZ], BF16) as h_tm0, \
                     nc.sbuf_tensor(tag + "h_tm1", [128, NT * GSZ], BF16) as h_tm1, \
                     nc.sbuf_tensor(tag + "h_fm0", [128, GC * TPC], BF16) as h_fm0, \
                     nc.sbuf_tensor(tag + "h_fm1", [128, GC * TPC], BF16) as h_fm1, \
                     nc.sbuf_tensor(tag + "habs", [128, QSZ], BF16) as habs, \
                     nc.sbuf_tensor(tag + "spm", [128, 5 * (QSZ // 4)], BF16) as spm:
                    h_tmb = [h_tm0, h_tm1]
                    h_fmb = [h_fm0, h_fm1]

                    wg_v = wg.rearrange("p (g q) -> p g q", q=DC * GSZ)
                    wu_v = wu.rearrange("p (g q) -> p g q", q=DC * GSZ)
                    wd_v = wd.rearrange("p (g q) -> p g q", q=GC * D)

                    wld_gu = {}
                    wld_d = {}

                    def load_gu(g_eng, g, guard):
                        g_eng.wait_ge(sP.h, guard)
                        sLgu.inc(g_eng.dma_start(out=wg_sb[:, :], in_=wg_v[:, g, :]), 16)
                        sLgu.inc(g_eng.dma_start(out=wu_sb[:, :], in_=wu_v[:, g, :]), 16)
                        wld_gu[g] = sLgu.n

                    def load_d(g_eng, g, guard):
                        if guard:
                            g_eng.wait_ge(sP.h, guard)
                        sLd.inc(g_eng.dma_start(out=wd_sb[:, :], in_=wd_v[:, g, :]), 16)
                        wld_d[g] = sLd.n

                    block.gpsimd(lambda e: (load_gu(e, 0, pe_combe), load_d(e, 0, 0)))

                    pe_g = {}; pe_u = {}; pe_gu_end = {}; pe_tr = {}; pe_dn = {}
                    pe_group_end = {}
                    a_sil = {}; a_abs = {}
                    v_h = {}; v_sp = {}; v_trv = {}; v_dnv = {}

                    NSP = 4 if sparsify else 0

                    def plan_slot(g):
                        n = sP.n
                        for tt in range(NT):
                            n += 1; pe_g[(g, tt)] = n
                            n += 1; pe_u[(g, tt)] = n
                        pe_gu_end[g] = n
                        if g >= 1:
                            for tt in range(NT):
                                n += 1; pe_tr[(g - 1, tt)] = n
                            for idx in range(2 * DC):
                                n += 1; pe_dn[(g - 1, idx)] = n
                            pe_group_end[g - 1] = n
                        sP.n = n
                        n = sA.n
                        for tt in range(NT):
                            n += 1; a_sil[(g, tt)] = n
                        for q in range(NSP):
                            n += 1; a_abs[(g, q)] = n
                        sA.n = n
                        n = sV.n
                        for tt in range(NT):
                            n += 1; v_h[(g, tt)] = n
                        if g >= 1:
                            for tt in range(NT):
                                n += 1; v_trv[(g - 1, tt)] = n
                            for q in range(DC):
                                n += 1; v_dnv[(g - 1, q)] = n
                        for q in range(NSP):
                            n += 1; v_sp[(g, q)] = n
                        sV.n = n

                    def ten_slot(t, g):
                        htm = h_tmb[g % 2] if g < NG else None
                        if g < NG:
                            for tt in range(NT):
                                if tt == 0:
                                    t.wait_ge(sLgu.h, wld_gu[g])
                                    if g == 0:
                                        t.wait_ge(sA.h, a_x1b)
                                    else:
                                        t.wait_ge(sA.h, a_sil[(g - 1, NT - 1)])
                                        t.wait_ge(sV.h, v_h[(g - 1, NT - 1)])
                                else:
                                    t.wait_ge(sA.h, a_sil[(g, tt - 1)])
                                    t.wait_ge(sV.h, v_h[(g, tt - 1)])
                                mmi = None
                                for dc in range(DC):
                                    mmi = t.matmul(pC[:, :],
                                                   x1b[:, dc * TPC + tt * 128:dc * TPC + tt * 128 + 128],
                                                   wg_sb[:, dc * GSZ:(dc + 1) * GSZ],
                                                   start=(dc == 0), stop=(dc == DC - 1))
                                mmi.then_inc(sP.h, 1)
                                for dc in range(DC):
                                    mmi = t.matmul(pD[:, :],
                                                   x1b[:, dc * TPC + tt * 128:dc * TPC + tt * 128 + 128],
                                                   wu_sb[:, dc * GSZ:(dc + 1) * GSZ],
                                                   start=(dc == 0), stop=(dc == DC - 1))
                                mmi.then_inc(sP.h, 1)
                        if g >= 1:
                            gp = g - 1
                            hsrc = h_tmb[gp % 2]
                            hfm = h_fmb[gp % 2]
                            for tt in range(NT):
                                if tt == 0:
                                    if sparsify:
                                        t.wait_ge(sV.h, v_sp[(gp, NSP - 1)])
                                    else:
                                        t.wait_ge(sV.h, v_h[(gp, NT - 1)])
                                    if gp >= 2:
                                        t.wait_ge(sV.h, v_trv[(gp - 1, NT - 1)])
                                else:
                                    t.wait_ge(sV.h, v_trv[(gp, tt - 1)])
                                mmi = None
                                for ic in range(GC):
                                    mmi = t.matmul(pE[:, ic * 128:(ic + 1) * 128],
                                                   hsrc[:, tt * GSZ + ic * 128:tt * GSZ + ic * 128 + 128],
                                                   ident[:, :], start=True, stop=True)
                                mmi.then_inc(sP.h, 1)
                            for idx in range(2 * DC):
                                fc, tb = idx // 2, idx % 2
                                q = idx // 2
                                bank = pA if q % 2 == 0 else pB
                                if idx == 0:
                                    t.wait_ge(sV.h, v_trv[(gp, NT - 1)])
                                    t.wait_ge(sLd.h, wld_d[gp])
                                    if gp == 0:
                                        t.wait_ge(sScr.h, scr_w)
                                if q <= 1 and tb == 0 and gp == 0:
                                    t.wait_ge(sV.h, v_ap1)
                                if q >= 2 and tb == 0:
                                    t.wait_ge(sV.h, v_dnv[(gp, q - 2)])
                                mmi = None
                                for ic in range(GC):
                                    mmi = t.matmul(bank[:, tb * 512:tb * 512 + 512],
                                                   wd_sb[:, ic * D + fc * 128:ic * D + fc * 128 + 128],
                                                   hfm[:, ic * TPC + tb * 512:ic * TPC + tb * 512 + 512],
                                                   start=(ic == 0), stop=(ic == GC - 1))
                                mmi.then_inc(sP.h, 1)

                    def sca_slot(a, g):
                        if g >= NG:
                            return
                        for tt in range(NT):
                            a.wait_ge(sP.h, pe_g[(g, tt)])
                            a.activation(out=silu_s[:, :], in_=pC[:, :],
                                         func=AF.Silu).then_inc(sA.h, 1)
                            DRAIN_ACT and a.drain()
                        htm = h_tmb[g % 2]
                        for q in range(NSP):
                            a.wait_ge(sV.h, v_h[(g, NT - 1)])
                            if q >= 1:
                                a.wait_ge(sV.h, v_sp[(g, q - 1)])
                            a.activation(out=habs[:, :], in_=htm[:, q * QSZ:(q + 1) * QSZ],
                                         func=AF.Abs).then_inc(sA.h, 1)
                            DRAIN_ACT and a.drain()

                    def vec_slot(v, g):
                        if g < NG:
                            htm = h_tmb[g % 2]
                            for tt in range(NT):
                                v.wait_ge(sA.h, a_sil[(g, tt)])
                                v.wait_ge(sP.h, pe_u[(g, tt)])
                                v.tensor_mul(htm[:, tt * GSZ:(tt + 1) * GSZ],
                                             silu_s[:, :], pD[:, :]).then_inc(sV.h, 1)
                                DRAIN_DVE and v.drain()
                        if g >= 1:
                            gp = g - 1
                            hfm = h_fmb[gp % 2]
                            for tt in range(NT):
                                v.wait_ge(sP.h, pe_tr[(gp, tt)])
                                dst = hfm[:, :].rearrange("p (ic t) -> p ic t", t=TPC)[:, :, tt * 128:(tt + 1) * 128]
                                srcp = pE[:, :].rearrange("p (ic r) -> p ic r", r=128)
                                v.tensor_copy(dst, srcp).then_inc(sV.h, 1)
                                DRAIN_DVE and v.drain()
                            for q in range(DC):
                                bank = pA if q % 2 == 0 else pB
                                v.wait_ge(sP.h, pe_dn[(gp, 2 * q + 1)])
                                dst = xsum[:, q * TPC:(q + 1) * TPC]
                                if gp == 0:
                                    v.tensor_copy(dst, bank[:, :]).then_inc(sV.h, 1)
                                else:
                                    v.tensor_add(dst, dst, bank[:, :]).then_inc(sV.h, 1)
                                DRAIN_DVE and v.drain()
                        if g < NG and sparsify:
                            htm = h_tmb[g % 2]
                            QQ = QSZ // 4
                            for q in range(NSP):
                                v.wait_ge(sA.h, a_abs[(g, q)])
                                hv = htm[:, q * QSZ:(q + 1) * QSZ].rearrange("p (b four) -> p b four", four=4)
                                av = habs[:, :].rearrange("p (b four) -> p b four", four=4)
                                s0 = spm[:, 0 * QQ:1 * QQ]; s1 = spm[:, 1 * QQ:2 * QQ]
                                s2 = spm[:, 2 * QQ:3 * QQ]; s3 = spm[:, 3 * QQ:4 * QQ]
                                tmp = spm[:, 4 * QQ:5 * QQ]
                                aa = [av[:, :, j] for j in range(4)]
                                hh = [hv[:, :, j] for j in range(4)]
                                D_ = Drained(v)
                                D_.tensor_tensor(tmp, aa[0], aa[1], ALU.is_ge)
                                D_.tensor_copy(s0, tmp)
                                D_.tensor_scalar(s1, tmp, -1.0, None, ALU.mult)
                                D_.tensor_tensor(tmp, aa[0], aa[2], ALU.is_ge)
                                D_.tensor_add(s0, s0, tmp)
                                D_.tensor_scalar(s2, tmp, -1.0, None, ALU.mult)
                                D_.tensor_tensor(tmp, aa[0], aa[3], ALU.is_ge)
                                D_.tensor_add(s0, s0, tmp)
                                D_.tensor_scalar(s3, tmp, -1.0, None, ALU.mult)
                                D_.tensor_tensor(tmp, aa[1], aa[2], ALU.is_ge)
                                D_.tensor_add(s1, s1, tmp)
                                D_.tensor_sub(s2, s2, tmp)
                                D_.tensor_tensor(tmp, aa[1], aa[3], ALU.is_ge)
                                D_.tensor_add(s1, s1, tmp)
                                D_.tensor_sub(s3, s3, tmp)
                                D_.tensor_tensor(tmp, aa[2], aa[3], ALU.is_ge)
                                D_.tensor_add(s2, s2, tmp)
                                D_.tensor_sub(s3, s3, tmp)
                                D_.tensor_single_scalar(s0, s0, 1.5, ALU.is_ge)
                                D_.tensor_single_scalar(s1, s1, 0.5, ALU.is_ge)
                                D_.tensor_single_scalar(s2, s2, -0.5, ALU.is_ge)
                                D_.tensor_single_scalar(s3, s3, -1.5, ALU.is_ge)
                                D_.tensor_tensor(hh[0], hh[0], s0, ALU.mult)
                                D_.tensor_tensor(hh[1], hh[1], s1, ALU.mult)
                                D_.tensor_tensor(hh[2], hh[2], s2, ALU.mult)
                                v.tensor_tensor(hh[3], hh[3], s3, ALU.mult).then_inc(sV.h, 1)
                                DRAIN_DVE and v.drain()

                    for g in range(NG + 1):
                        plan_slot(g) if g < NG else None
                        if g == NG:
                            n = sP.n
                            for tt in range(NT):
                                n += 1; pe_tr[(NG - 1, tt)] = n
                            for idx in range(2 * DC):
                                n += 1; pe_dn[(NG - 1, idx)] = n
                            pe_group_end[NG - 1] = n
                            sP.n = n
                            n = sV.n
                            for tt in range(NT):
                                n += 1; v_trv[(NG - 1, tt)] = n
                            for q in range(DC):
                                n += 1; v_dnv[(NG - 1, q)] = n
                            sV.n = n
                        block.tensor(lambda t, g=g: ten_slot(t, g))
                        block.scalar(lambda a, g=g: sca_slot(Drained(a), g))
                        block.vector(lambda v, g=g: vec_slot(Drained(v), g))
                        if g + 1 < NG:
                            block.gpsimd(lambda e, g=g: load_gu(e, g + 1, pe_gu_end[g]))
                        if g < NG and g >= 1:
                            block.gpsimd(lambda e, g=g: load_d(
                                e, g, pe_group_end[g - 1]))
                    v_ffn = v_dnv[(NG - 1, DC - 1)]
                    _scF.__exit__(None, None, None)


                _scR = nc.named_scope(tag + "_ln2"); _scR.__enter__()
                vres = []
                for dc in range(DC):
                    def res_s(s, dc=dc, guard=(vres[dc - 2] if dc >= 2 else 0)):
                        if dc == 0:
                            s.wait_ge(sV.h, v_ffn)
                            s.wait_ge(sScr.h, scr_w)
                        if dc >= 2:
                            s.wait_ge(sV.h, guard)
                        sXr[dc % 2].inc(s.dma_start(out=xstr[dc % 2][:, :],
                                        in_=x1scr[dc * 128:(dc + 1) * 128, :]), 16)
                    block.sync(res_s)
                    n_ws = sXr[dc % 2].n

                    def res_v(v, dc=dc, n_ws=n_ws):
                        v.wait_ge(sXr[dc % 2].h, n_ws)
                        c = xsum[:, dc * TPC:(dc + 1) * TPC]
                        sV.inc(v.tensor_add(c, c, xstr[dc % 2][:, :]), 1)
                    VEC(res_v)
                    vres.append(sV.n)
                v_res = sV.n

                pe_bc2 = emit_ln(xsum, v_res)

                for dc in range(DC):
                    def ap2_v(v, dc=dc, pe_bc2=pe_bc2):
                        if dc == 0:
                            v.wait_ge(sP.h, pe_bc2)
                        c = xsum[:, dc * TPC:(dc + 1) * TPC]
                        v.tensor_sub(c, c, pA[:, :])
                        sV.inc(v.tensor_mul(c, c, pB[:, :]), 1)
                    VEC(ap2_v)
                    if out_mode == "x2T":
                        def o_s(s, dc=dc, v_n=sV.n):
                            s.wait_ge(sV.h, v_n)
                            sO.inc(s.dma_start(out=out[dc * 128:(dc + 1) * 128, :],
                                               in_=xsum[:, dc * TPC:(dc + 1) * TPC]), 16)
                        block.sync(o_s)
                    elif out_mode == "x2_cc":
                        def o_g(g, dc=dc, v_n=sV.n):
                            g.wait_ge(sV.h, v_n)
                            sO.inc(g.dma_start(out=out[dc * 128:(dc + 1) * 128, :],
                                               in_=xsum[:, dc * TPC:(dc + 1) * TPC]), 16)
                        block.gpsimd(o_g)
                v_x2 = sV.n

                if final_tm:
                    with nc.sbuf_tensor(tag + "ytile", [128, D], F32) as ytile:
                        v_yt = []
                        for tt in range(NT):
                            def ftr_t(t, tt=tt, v_x2=v_x2):
                                if tt == 0:
                                    t.wait_ge(sV.h, v_x2)
                                else:
                                    t.wait_ge(sV.h, v_yt[-1])
                                mmi = None
                                for dc in range(DC):
                                    bank = pA if dc < 8 else pB
                                    mmi = t.matmul(bank[:, (dc % 8) * 128:(dc % 8 + 1) * 128],
                                                   xsum[:, dc * TPC + tt * 128:dc * TPC + tt * 128 + 128],
                                                   identf[:, :], start=True, stop=True)
                                sP.inc(mmi, 1)
                            block.tensor(ftr_t)
                            pe_ftr = sP.n

                            def ftr_v(v, pe_ftr=pe_ftr, guard=sO.n):
                                v.wait_ge(sP.h, pe_ftr)
                                v.wait_ge(sO.h, guard)
                                v.tensor_copy(ytile[:, 0:1024], pA[:, :])
                                sV.inc(v.tensor_copy(ytile[:, 1024:2048], pB[:, :]), 1)
                            VEC(ftr_v)
                            v_yt.append(sV.n)

                            def ftr_s(s, tt=tt, v_n=sV.n):
                                s.wait_ge(sV.h, v_n)
                                sO.inc(s.dma_start(out=out[tt * 128:(tt + 1) * 128, :],
                                                   in_=ytile[:, :]), 16)
                            block.sync(ftr_s)
                _scR.__exit__(None, None, None)

    return {"pe": (sP.h, sP.n), "act": (sA.h, sA.n), "dve": (sV.h, sV.n),
            "out": (sO.h, sO.n)}



def build_p2(nblk=16):
    def io_builder(nc):
        x2T = nc.dram_tensor("x2T", [D, T], F32, kind="ExternalInput").ap()
        return dict(
            xblk_in=lambda grp, blk: x2T[512 * grp:512 * (grp + 1),
                                         blk * 512:(blk + 1) * 512].rearrange("(k p) t -> p k t", p=128),
            wq=nc.dram_tensor("wq", [128, DC * HPC * DK], F32, kind="ExternalInput").ap(),
            wk=nc.dram_tensor("wk", [128, DC * DK], F32, kind="ExternalInput").ap(),
            wv=nc.dram_tensor("wv", [128, DC * DK], F32, kind="ExternalInput").ap(),
            ropec=nc.dram_tensor("ropec", [128, T], F32, kind="ExternalInput").ap(),
            ropes=nc.dram_tensor("ropes", [128, T], F32, kind="ExternalInput").ap(),
            ident=nc.dram_tensor("ident", [128, 128], BF16, kind="ExternalInput").ap(),
            dmask=nc.dram_tensor("dmask", [128, 4 * 512], BF16, kind="ExternalInput").ap(),
            negc=nc.dram_tensor("negc", [128, 1], F32, kind="ExternalInput").ap(),
            att1=nc.dram_tensor("att1", [HPC * DK, T], BF16, kind="ExternalOutput").ap(),
            cc_wait=None,
        )
    return _standalone(lambda E, io, entry: emit_p2(E, io, entry, nblk=nblk), io_builder)


def emit_p2(E, io, entry, nblk=16, tag="p2"):
    nc, block, ctx = E.nc, E.block, E.ctx
    wq, wk, wv = io["wq"], io["wk"], io["wv"]
    cos_in, sin_in = io["ropec"], io["ropes"]
    ident_in, dmask_in, negc_in = io["ident"], io["dmask"], io["negc"]
    att1 = io["att1"]

    NQT = nblk * 4
    NQG = nblk

    with contextlib.ExitStack() as pctx:
        sb = lambda name, shape, dt: pctx.enter_context(nc.sbuf_tensor(tag + name, shape, dt))

        wq_sb = sb("wq_sb", [128, DC * HPC * DK], BF16)
        wk_sb = sb("wk_sb", [128, DC * DK], BF16)
        wv_sb = sb("wv_sb", [128, DC * DK], BF16)
        xblk = [sb(f"xblk{i}", [128, DC * 512], BF16) for i in range(2)]
        ropeb = [sb(f"ropeb{i}", [128, 1024], F32) for i in range(2)]
        ident = sb("ident_sb", [128, 128], BF16)
        dmask = sb("dmask_sb", [128, 4 * 512], BF16)
        negc = sb("negc_sb", [128, 1], F32)
        qsb = [sb(f"q{h}_sb", [128, T], BF16) for h in range(HPC)]
        ksb = sb("k_sb", [128, T], BF16)
        v_aug = sb("v_aug", [128, NQT * 129], BF16)
        vtmp = sb("vtmp", [128, 512], BF16)
        tr1 = sb("tr1", [128, 512], F32)
        tr2 = sb("tr2", [128, 512], F32)
        tr3 = sb("tr3", [128, 512], F32)
        ptile = [sb(f"ptile{i}", [128, 1024], BF16) for i in range(2)]
        ybf = sb("ybf", [128, 128], BF16)
        att_tile = sb("att_tile", [128, 512], BF16)
        rc = sb("rc", [128, 2], F32)

        pS = [pctx.enter_context(nc.psum_tensor(tag + f"pS{i}", [128, 1024], F32)) for i in range(2)]
        pY = [pctx.enter_context(nc.psum_tensor(tag + f"pY{i}", [128, 512], F32)) for i in range(4)]
        pQ, pV, pT = pY[0], pY[1], pY[2]

        sLw = E.sem(tag + "sLw")
        sXB = [E.sem(tag + "sXB0"), E.sem(tag + "sXB1")]
        sRC = [E.sem(tag + "sRC0"), E.sem(tag + "sRC1")]
        sWi = E.sem(tag + "sWi")
        sP = E.sem(tag + "sP")
        sA = E.sem(tag + "sA")
        sV = E.sem(tag + "sV")
        sO = E.sem(tag + "sO")

        def load_xblk(g, blk, buf):
            if io["cc_wait"] is not None and blk == 0:
                g.wait_ge(io["cc_wait"][0], io["cc_wait"][1])
            for grp in range(4):
                sXB[buf].inc(g.dma_start(
                    out=xblk[buf][:, :].rearrange("p (k t) -> p k t", t=512)[:, 4 * grp:4 * grp + 4, :],
                    in_=io["xblk_in"](grp, blk)), 16)

        if True:
            VEC, SCA = E.VEC, E.SCA

            XQ = block.sync if io.get("xblk_sync") else block.gpsimd

            def init_g(g):
                for se, va in entry:
                    g.wait_ge(se, va)
                sLw.inc(g.dma_start(out=wq_sb[:, :], in_=wq), 16)
                sLw.inc(g.dma_start(out=wk_sb[:, :], in_=wk), 16)
                sLw.inc(g.dma_start(out=wv_sb[:, :], in_=wv), 16)
            block.gpsimd(init_g)
            XQ(lambda e: load_xblk(e, 0, 0))

            def load_rope(s, blk):
                buf = blk % 2
                sRC[buf].inc(s.dma_start(out=ropeb[buf][:, 0:512],
                                         in_=cos_in[:, blk * 512:(blk + 1) * 512]), 16)
                sRC[buf].inc(s.dma_start(out=ropeb[buf][:, 512:1024],
                                         in_=sin_in[:, blk * 512:(blk + 1) * 512]), 16)

            def init_s(s):
                for se, va in entry:
                    s.wait_ge(se, va)
                load_rope(s, 0)
                load_rope(s, 1)
                sWi.inc(s.dma_start(out=ident[:, :], in_=ident_in), 16)
                sWi.inc(s.dma_start(out=dmask[:, :], in_=dmask_in), 16)
                sWi.inc(s.dma_start(out=negc[:, :], in_=negc_in), 16)
            block.sync(init_s)
            n_wi = sWi.n

            def init_v(v):
                for se, va in entry:
                    v.wait_ge(se, va)
                sV.inc(v.memset(v_aug[:, :], 1.0), 1)
            VEC(init_v)
            v_init = sV.n

            _sc1 = nc.named_scope(tag + "_proj"); _sc1.__enter__()
            last_pev = 0
            a_evac = 0
            v_rope = 0
            for blk in range(nblk):
                buf = blk % 2
                xv = xblk[buf][:, :]
                if blk + 1 < nblk:
                    def pre_g(g, blk=blk, guard=last_pev):
                        if blk >= 1:
                            g.wait_ge(sP.h, guard)
                        load_xblk(g, blk + 1, (blk + 1) % 2)
                    XQ(pre_g)

                for which in range(4):
                    def proj_t(t, blk=blk, which=which, xv=xv, g_a=a_evac, g_v=v_rope):
                        if which == 0:
                            t.wait_ge(sXB[blk % 2].h, 64 * (blk // 2 + 1))
                            if blk == 0:
                                t.wait_ge(sLw.h, sLw.n)
                                t.wait_ge(sWi.h, n_wi)
                                t.wait_ge(sV.h, v_init)
                        t.wait_ge(sA.h, g_a)
                        mmi = None
                        if which < 3:
                            wsb, col = ((wq_sb, which * DK) if which < 2 else (wk_sb, 0))
                            for dc in range(DC):
                                base = dc * (HPC * DK if which < 2 else DK) + col
                                mmi = t.matmul(pQ[:, :], wsb[:, base:base + DK],
                                               xv[:, dc * 512:(dc + 1) * 512],
                                               start=(dc == 0), stop=(dc == DC - 1))
                            sP.inc(mmi, 1)
                        else:
                            for dc in range(DC):
                                mmi = t.matmul(pV[:, :], wv_sb[:, dc * DK:dc * DK + DK],
                                               xv[:, dc * 512:(dc + 1) * 512],
                                               start=(dc == 0), stop=(dc == DC - 1))
                            sP.inc(mmi, 1)
                    block.tensor(proj_t)
                    pe_p = sP.n
                    if which == 3:
                        last_pev = pe_p

                    if which < 3:
                        def rope_a(a, pe_p=pe_p, g_v=v_rope):
                            a.wait_ge(sP.h, pe_p)
                            a.wait_ge(sV.h, g_v)
                            sA.inc(a.activation(out=tr1[:, :], in_=pQ[:, :], func=AF.Copy), 1)
                        SCA(rope_a)
                        a_evac = sA.n

                        def rope_v(v, blk=blk, which=which, a_n=sA.n):
                            v.wait_ge(sA.h, a_n)
                            if which == 0:
                                v.wait_ge(sRC[blk % 2].h, 32 * (blk // 2 + 1))
                            v.tensor_copy(tr2[0:64, :], tr1[64:128, :])
                            v.tensor_copy(tr2[64:128, :], tr1[0:64, :])
                            cs = ropeb[blk % 2][:, 0:512]
                            sn = ropeb[blk % 2][:, 512:1024]
                            v.tensor_mul(tr3[:, :], tr1[:, :], cs)
                            v.tensor_mul(tr2[:, :], tr2[:, :], sn)
                            dst = (qsb[which] if which < 2 else ksb)[:, blk * 512:(blk + 1) * 512]
                            sV.inc(v.tensor_add(dst, tr3[:, :], tr2[:, :]), 1)
                        VEC(rope_v)
                        v_rope = sV.n
                        if which == 2 and blk + 2 < nblk:
                            def rpre_s(s, blk=blk, guard=v_rope):
                                s.wait_ge(sV.h, guard)
                                load_rope(s, blk + 2)
                            block.sync(rpre_s)
                    if which == 3:
                        def v_a(a, pe_p=pe_p):
                            a.wait_ge(sP.h, pe_p)
                            sA.inc(a.activation(out=vtmp[:, :], in_=pV[:, :], func=AF.Copy), 1)
                        SCA(v_a)
                        a_evac = sA.n

                        def vt_t(t, a_n=sA.n, g_v=v_rope):
                            t.wait_ge(sA.h, a_n)
                            t.wait_ge(sV.h, g_v)
                            mmi = None
                            for j in range(4):
                                mmi = t.matmul(pT[:, j * 128:(j + 1) * 128],
                                               vtmp[:, j * 128:(j + 1) * 128], ident[:, :],
                                               start=True, stop=True)
                            sP.inc(mmi, 1)
                        block.tensor(vt_t)
                        pe_vt = sP.n

                        def vt_v(v, blk=blk, pe_vt=pe_vt):
                            v.wait_ge(sP.h, pe_vt)
                            dst = v_aug[:, :].rearrange("p (n c) -> p n c", c=129)[:, 4 * blk:4 * blk + 4, 0:128]
                            src = pT[:, :].rearrange("p (n c) -> p n c", c=128)
                            sV.inc(v.tensor_copy(dst, src), 1)
                        VEC(vt_v)
                        v_rope = sV.n
            v_stage1 = sV.n
            pe_stage1 = sP.n
            _sc1.__exit__(None, None, None)
            _sc2 = nc.named_scope(tag + "_attn"); _sc2.__enter__()

            ISQ = ISQRT_DK
            v_att = 0
            prev_av_last = 0
            for h in range(HPC):
                for qg in range(NQG):
                    nkb = 4 * qg + 4
                    npair = 2 * qg + 2

                    pe_ops = [("st", 0)]
                    for p in range(npair):
                        if p + 1 < npair:
                            pe_ops.append(("st", p + 1))
                        pe_ops.append(("av", p))
                    pe_st = {}
                    av_done = {}
                    n = sP.n
                    for kind, p in pe_ops:
                        n += 1
                        (pe_st if kind == "st" else av_done)[p] = n
                    pe_ytr = [n + 1 + j for j in range(4)]
                    sP.n = n + 4
                    aexp = {p: sA.n + p + 1 for p in range(npair)}
                    sA.n += npair
                    vmask = {npair - 2: sV.n + 1, npair - 1: sV.n + 2}
                    v_ye = [sV.n + 3 + j for j in range(4)]
                    v_att_new = sV.n + 7
                    sV.n += 7
                    o_prev = sO.n

                    def ten_body(t, h=h, qg=qg, nkb=nkb, npair=npair,
                                 pe_ops=list(pe_ops), aexp=dict(aexp),
                                 vmask=dict(vmask), v_ye=list(v_ye),
                                 v_att=v_att, first=((h, qg) == (0, 0))):
                        qslice = qsb[h][:, qg * 512:(qg + 1) * 512]
                        for kind, p in pe_ops:
                            if kind == "st":
                                if first and p < 2:
                                    t.wait_ge(sP.h, pe_stage1)
                                    t.wait_ge(sV.h, v_stage1)
                                if p >= 2:
                                    t.wait_ge(sA.h, aexp[p - 2])
                                elif v_att:
                                    t.wait_ge(sV.h, v_att)
                                mmi = None
                                for kb in (2 * p, 2 * p + 1):
                                    mmi = t.matmul(
                                        pS[p % 2][:, (kb % 2) * 512:(kb % 2) * 512 + 512],
                                        ksb[:, kb * 128:(kb + 1) * 128], qslice,
                                        start=True, stop=True)
                                mmi.then_inc(sP.h, 1)
                            else:
                                t.wait_ge(sA.h, aexp[p])
                                if p >= npair - 2:
                                    t.wait_ge(sV.h, vmask[p])
                                mmi = None
                                for kb in (2 * p, 2 * p + 1):
                                    for j in range(4):
                                        mmi = t.matmul(
                                            pY[j][:, 0:129],
                                            ptile[p % 2][:, (kb % 2) * 512 + j * 128:(kb % 2) * 512 + j * 128 + 128],
                                            v_aug[:, kb * 129:(kb + 1) * 129],
                                            start=(kb == 0), stop=(kb == nkb - 1))
                                mmi.then_inc(sP.h, 1)
                        for j in range(4):
                            t.wait_ge(sV.h, v_ye[j])
                            t.matmul(pS[1][:, j * 128:(j + 1) * 128], ybf[:, :],
                                     ident[:, :], start=True, stop=True).then_inc(sP.h, 1)
                    block.tensor(ten_body)

                    def sca_body(a, npair=npair, pe_st=dict(pe_st),
                                 av_done=dict(av_done), prev_av_last=prev_av_last):
                        for p in range(npair):
                            a.wait_ge(sP.h, pe_st[p])
                            guard = av_done[p - 2] if p >= 2 else prev_av_last
                            if guard:
                                a.wait_ge(sP.h, guard)
                            a.activation(out=ptile[p % 2][:, :], in_=pS[p % 2][:, :],
                                         func=AF.Exp, scale=ISQ,
                                         bias=negc[:, :]).then_inc(sA.h, 1)
                            DRAIN_ACT and a.drain()
                    block.scalar(lambda a: sca_body(Drained(a)))
                    sca_drains = npair

                    def vec_body(v, h=h, qg=qg, npair=npair, aexp=dict(aexp),
                                 av_last=av_done[npair - 1], pe_ytr=list(pe_ytr),
                                 o_prev=o_prev):
                        for p in (npair - 2, npair - 1):
                            v.wait_ge(sA.h, aexp[p])
                            inst = None
                            for kb in (2 * p, 2 * p + 1):
                                r = kb - 4 * qg
                                if 0 <= r < 4:
                                    inst = v.tensor_mul(
                                        ptile[p % 2][:, (kb % 2) * 512:(kb % 2) * 512 + 512],
                                        ptile[p % 2][:, (kb % 2) * 512:(kb % 2) * 512 + 512],
                                        dmask[:, r * 512:(r + 1) * 512])
                                    DRAIN_DVE and v.drain()
                            inst.then_inc(sV.h, 1)
                        for j in range(4):
                            v.wait_ge(sP.h, av_last)
                            if j > 0:
                                v.wait_ge(sP.h, pe_ytr[j - 1])
                            v.reciprocal(rc[:, 0:1], pY[j][:, 128:129])
                            DRAIN_DVE and v.drain()
                            v.tensor_scalar(ybf[:, :], pY[j][:, 0:128], rc[:, 0:1],
                                            None, ALU.mult).then_inc(sV.h, 1)
                            DRAIN_DVE and v.drain()
                        v.wait_ge(sP.h, pe_ytr[3])
                        v.wait_ge(sO.h, o_prev)
                        v.tensor_copy(att_tile[:, :], pS[1][:, 0:512]).then_inc(sV.h, 1)
                        DRAIN_DVE and v.drain()
                    block.vector(vec_body)

                    def out_s(s, h=h, qg=qg, v_n=v_att_new):
                        s.wait_ge(sV.h, v_n)
                        sO.inc(s.dma_start(
                            out=att1[h * DK:(h + 1) * DK, qg * 512:(qg + 1) * 512],
                            in_=att_tile[:, :]), 16)
                    block.sync(out_s)

                    v_att = v_att_new
                    prev_av_last = av_done[npair - 1]
            _sc2.__exit__(None, None, None)

    return {"pe": (sP.h, sP.n), "act": (sA.h, sA.n), "dve": (sV.h, sV.n),
            "out": (sO.h, sO.n)}



def build_fused():
    nc = bass.Bass("TRN2", target_bir_lowering=False, debug=False)
    I = lambda n, shp, dt=F32: nc.dram_tensor(n, shp, dt, kind="ExternalInput").ap()
    x0s = I("x0s", [128, NSEG * DC * SEG])
    wq0, wk0, wv0 = I("wq0", [128, DC * 256]), I("wk0", [128, DC * 256]), I("wv0", [128, DC * 256])
    betas = I("betas", [1, 256])
    wo0, wg0 = I("wo0", [128, DC * D]), I("wg0", [128, NG * DC * GSZ])
    wu0, wd0 = I("wu0", [128, NG * DC * GSZ]), I("wd0", [128, NG * GC * D])
    wq1, wk1, wv1 = I("wq1", [128, DC * 256]), I("wk1", [128, DC * DK]), I("wv1", [128, DC * DK])
    ropec, ropes = I("ropec", [128, T]), I("ropes", [128, T])
    wo1, wg1 = I("wo1", [128, DC * D]), I("wg1", [128, NG * DC * GSZ])
    wu1, wd1 = I("wu1", [128, NG * DC * GSZ]), I("wd1", [128, NG * GC * D])
    xres0 = I("xres", [D, TPC])
    ident = I("ident", [128, 128], BF16)
    identf = I("identf", [128, 128], F32)
    binmask = I("binmask", [SEG, SEG], BF16)
    dmask = I("dmask", [128, 4 * 512], BF16)
    negc = I("negc", [128, 1], F32)
    eps = I("epsrow", [1, 1], F32)
    yout = nc.dram_tensor("yout", [TPC, D], F32, kind="ExternalOutput").ap()

    cc_a0_in = nc.dram_tensor("cc_a0_in", [256, T], BF16).ap()
    cc_a0_out = nc.dram_tensor("cc_a0_out", [D, T], BF16, addr_space="Shared").ap()
    cc_x2_in = nc.dram_tensor("cc_x2_in", [D, TPC], BF16).ap()
    cc_x2_out = nc.dram_tensor("cc_x2_out", [N_CORES * D, TPC], BF16, addr_space="Shared").ap()
    cc_a1_in = nc.dram_tensor("cc_a1_in", [256, T], BF16).ap()
    cc_a1_out = nc.dram_tensor("cc_a1_out", [D, T], BF16, addr_space="Shared").ap()
    x1scr = nc.dram_tensor("x1scr", [D, TPC], F32).ap()
    x3scr = nc.dram_tensor("x3scr", [D, TPC], F32).ap()

    RG = [list(range(N_CORES))]
    ctx = contextlib.ExitStack()
    with ctx:
        xsum = ctx.enter_context(nc.sbuf_tensor("xsum_shared", [128, DC * TPC], F32))
        block = ctx.enter_context(nc.Block())
        E = Env(nc, block, ctx)
        sCC = E.sem("sCC")

        with nc.named_scope("P0"):
            ex0 = emit_p0(E, dict(x0s=x0s, wq=wq0, wk=wk0, wv=wv0, betas=betas,
                                  ident=ident, binmask=binmask, negc=negc,
                                  att0=cc_a0_in), [], tag="p0")

        def cc1_g(g):
            g.wait_ge(*ex0["out"])
            sCC.inc(g.collective_compute(
                "AllGather", ALU.bypass, replica_groups=RG,
                ins=[cc_a0_in.opt()], outs=[cc_a0_out.opt()]), 1)
        block.gpsimd(cc1_g)

        entry1 = [ex0["pe"], ex0["act"], ex0["dve"], ex0["out"]]
        io1 = dict(attc_mode="gather_scramble", attc_g=cc_a0_out,
                   cc_wait=(sCC.h, 1),
                   xres_chunk=lambda s, dc: xres0[dc * 128:(dc + 1) * 128, :],
                   wo=wo0, wg=wg0, wu=wu0, wd=wd0,
                   ident=ident, identf=identf, eps=eps, x1scr=x1scr,
                   xsum_t=xsum, out_mode="x2_cc", out=cc_x2_in)
        with nc.named_scope("P1"):
            ex1 = emit_p1(E, io1, entry1, sparsify=True, final_tm=False, tag="p1")

        def cc2_g(g):
            g.wait_ge(*ex1["out"])
            sCC.inc(g.collective_compute(
                "AllGather", ALU.bypass, replica_groups=RG,
                ins=[cc_x2_in.opt()], outs=[cc_x2_out.opt()]), 1)
        block.gpsimd(cc2_g)

        entry2 = [ex1["pe"], ex1["act"], ex1["dve"], ex1["out"]]
        x2g = cc_x2_out.rearrange("(r d) t -> r d t", d=D)
        io2 = dict(
            xblk_in=lambda grp, blk: x2g[blk // 2, 512 * grp:512 * (grp + 1),
                                         (blk % 2) * 512:(blk % 2) * 512 + 512
                                         ].rearrange("(k p) t -> p k t", p=128),
            wq=wq1, wk=wk1, wv=wv1, ropec=ropec, ropes=ropes,
            ident=ident, dmask=dmask, negc=negc, att1=cc_a1_in,
            cc_wait=(sCC.h, 2), xblk_sync=True)
        with nc.named_scope("P2"):
            ex2 = emit_p2(E, io2, entry2, tag="p2")

        def cc3_g(g):
            g.wait_ge(*ex2["out"])
            sCC.inc(g.collective_compute(
                "AllGather", ALU.bypass, replica_groups=RG,
                ins=[cc_a1_in.opt()], outs=[cc_a1_out.opt()]), 1)
        block.gpsimd(cc3_g)

        entry3 = [ex2["pe"], ex2["act"], ex2["dve"], ex2["out"]]
        io3 = dict(attc_mode="gather", attc_g=cc_a1_out, cc_wait=(sCC.h, 3),
                   xres_chunk=None,
                   wo=wo1, wg=wg1, wu=wu1, wd=wd1,
                   ident=ident, identf=identf, eps=eps, x1scr=x3scr,
                   xsum_t=xsum, out_mode="yout", out=yout)
        with nc.named_scope("P3"):
            ex3 = emit_p1(E, io3, entry3, sparsify=False, final_tm=True, tag="p3")

        osem, oval = ex3["out"]
        def fin_s(s):
            s.wait_ge(osem, oval)
        block.sync(fin_s)
    return nc




def _prearr_dmat(w):
    Dd, F = w.shape
    K = Dd // 128
    return np.ascontiguousarray(w.reshape(K, 128, F).transpose(1, 0, 2).reshape(128, K * F))


def _prearr_ffn(w):
    return np.ascontiguousarray(
        w.reshape(DC, 128, NG, GSZ).transpose(1, 2, 0, 3).reshape(128, NG * DC * GSZ))


def _prearr_wd(w):
    return np.ascontiguousarray(
        w.reshape(NG, GC, 128, D).transpose(2, 0, 1, 3).reshape(128, NG * GC * D))


def _prearr_x0(x0):
    return np.ascontiguousarray(
        x0.reshape(NSEG, SEG, DC, 128).transpose(3, 0, 2, 1).reshape(128, NSEG * DC * SEG))


def _scramble_fm(att_fm):
    A = att_fm.reshape(16, 128, NSEG, 32, 16)
    C = A.transpose(4, 1, 2, 0, 3)
    return np.ascontiguousarray(C.reshape(D, T))


def _p2_consts():
    inv = np.power(10000.0, -2.0 * np.arange(64) / 128.0)
    ang = np.arange(T)[:, None].astype(np.float64) * inv[None, :]
    cosT = np.ascontiguousarray(np.tile(np.cos(ang), (1, 2)).T.astype(np.float32))
    sinT = np.tile(np.sin(ang), (1, 2)).T.astype(np.float32)
    sinT[:64, :] *= -1.0
    sinT = np.ascontiguousarray(sinT)
    p = np.arange(128)[:, None]
    c = np.arange(512)[None, :]
    dmask = np.zeros((128, 4 * 512), BF)
    for r in range(4):
        dmask[:, r * 512:(r + 1) * 512] = (c >= r * 128 + p).astype(BF)
    negc = np.full((128, 1), -CSHIFT, np.float32)
    return cosT, sinT, dmask, negc


def kernel(L0_attn_x, L0_attn_Wq, L0_attn_Wk, L0_attn_Wv, L0_attn_Wo, L0_attn_betas,
           L0_ffn_w_gate, L0_ffn_w_up, L0_ffn_w_down,
           L1_attn_Wq, L1_attn_Wk, L1_attn_Wv, L1_attn_Wo,
           L1_ffn_w_gate, L1_ffn_w_up, L1_ffn_w_down):
    cores = list(range(N_CORES))
    x0T = np.ascontiguousarray(np.asarray(L0_attn_x)[0].T)
    ident = np.eye(128, dtype=BF)
    identf = np.eye(128, dtype=np.float32)
    eps = np.full((1, 1), LN_EPS, np.float32)
    _, binmask, negc = _consts_p0()
    cosT, sinT, dmask, _ = _p2_consts()
    wg1 = np.ascontiguousarray(L1_ffn_w_gate.T)
    wu1 = np.ascontiguousarray(L1_ffn_w_up.T)
    wd1 = np.ascontiguousarray(L1_ffn_w_down.T)
    betas16 = L0_attn_betas.reshape(16, DK)
    x0s = _prearr_x0(np.asarray(L0_attn_x)[0])
    wo0p, wo1p = _prearr_dmat(L0_attn_Wo), _prearr_dmat(L1_attn_Wo)
    wg0p, wu0p = _prearr_ffn(L0_ffn_w_gate), _prearr_ffn(L0_ffn_w_up)
    wd0p = _prearr_wd(L0_ffn_w_down)
    wg1p, wu1p, wd1p = _prearr_ffn(wg1), _prearr_ffn(wu1), _prearr_wd(wd1)
    maps = []
    for c in cores:
        sl = slice(c * TPC, (c + 1) * TPC)
        hs = slice(c * 256, (c + 1) * 256)
        kv = slice((c // 2) * 128, (c // 2 + 1) * 128)
        maps.append({
            "x0s": x0s,
            "xres": np.ascontiguousarray(x0T[:, sl]),
            "wq0": _prearr_dmat(L0_attn_Wq[:, hs]),
            "wk0": _prearr_dmat(L0_attn_Wk[:, hs]),
            "wv0": _prearr_dmat(L0_attn_Wv[:, hs]),
            "betas": np.ascontiguousarray(betas16[2 * c:2 * c + 2, :].reshape(1, 256)),
            "wo0": wo0p, "wg0": wg0p, "wu0": wu0p, "wd0": wd0p,
            "wq1": _prearr_dmat(L1_attn_Wq[:, hs]),
            "wk1": _prearr_dmat(L1_attn_Wk[:, kv]),
            "wv1": _prearr_dmat(L1_attn_Wv[:, kv]),
            "ropec": cosT, "ropes": sinT,
            "wo1": wo1p, "wg1": wg1p, "wu1": wu1p, "wd1": wd1p,
            "ident": ident, "identf": identf, "binmask": binmask,
            "dmask": dmask, "negc": negc, "epsrow": eps,
        })
    res = run_bass_kernel_spmd(build_fused(), maps, core_ids=cores)
    y = np.concatenate([r["yout"] for r in res.results], axis=0)
    return y.reshape(1, T, D)


def _kernel_unfused(L0_attn_x, L0_attn_Wq, L0_attn_Wk, L0_attn_Wv, L0_attn_Wo, L0_attn_betas,
           L0_ffn_w_gate, L0_ffn_w_up, L0_ffn_w_down,
           L1_attn_Wq, L1_attn_Wk, L1_attn_Wv, L1_attn_Wo,
           L1_ffn_w_gate, L1_ffn_w_up, L1_ffn_w_down):
    cores = list(range(N_CORES))
    x0T = np.ascontiguousarray(np.asarray(L0_attn_x)[0].T)
    ident = np.eye(128, dtype=BF)
    identf = np.eye(128, dtype=np.float32)
    eps = np.full((1, 1), LN_EPS, np.float32)

    res0 = run_bass_kernel_spmd(
        build_p0(), _p0_in_maps(x0T, L0_attn_Wq, L0_attn_Wk, L0_attn_Wv, L0_attn_betas),
        core_ids=cores)
    att0 = np.concatenate([r["att0"] for r in res0.results], axis=0)
    att0s = _scramble_fm(att0)

    maps1 = []
    for c in cores:
        sl = slice(c * TPC, (c + 1) * TPC)
        maps1.append({
            "attc": np.ascontiguousarray(att0s[:, sl]),
            "xres": np.ascontiguousarray(x0T[:, sl]),
            "wo": L0_attn_Wo, "wg": L0_ffn_w_gate, "wu": L0_ffn_w_up,
            "wd": L0_ffn_w_down,
            "ident": ident, "identf": identf, "epsrow": eps,
        })
    res1 = run_bass_kernel_spmd(build_p1(True, False), maps1, core_ids=cores)
    x2T = np.concatenate([r["x2T"] for r in res1.results], axis=1)

    cosT, sinT, dmask, negc = _p2_consts()
    maps2 = []
    for c in cores:
        kv = c // 2
        maps2.append({
            "x2T": x2T,
            "wq": np.ascontiguousarray(L1_attn_Wq[:, c * 256:(c + 1) * 256]),
            "wk": np.ascontiguousarray(L1_attn_Wk[:, kv * 128:(kv + 1) * 128]),
            "wv": np.ascontiguousarray(L1_attn_Wv[:, kv * 128:(kv + 1) * 128]),
            "ropec": cosT, "ropes": sinT,
            "ident": ident, "dmask": dmask, "negc": negc,
        })
    res2 = run_bass_kernel_spmd(build_p2(), maps2, core_ids=cores)
    att1 = np.concatenate([r["att1"] for r in res2.results], axis=0)

    wg1 = np.ascontiguousarray(L1_ffn_w_gate.T)
    wu1 = np.ascontiguousarray(L1_ffn_w_up.T)
    wd1 = np.ascontiguousarray(L1_ffn_w_down.T)
    maps3 = []
    for c in cores:
        sl = slice(c * TPC, (c + 1) * TPC)
        maps3.append({
            "attc": np.ascontiguousarray(att1[:, sl]),
            "xres": np.ascontiguousarray(x2T[:, sl]),
            "wo": L1_attn_Wo, "wg": wg1, "wu": wu1, "wd": wd1,
            "ident": ident, "identf": identf, "epsrow": eps,
        })
    res3 = run_bass_kernel_spmd(build_p1(False, True), maps3, core_ids=cores)
    y = np.concatenate([r["yout"] for r in res3.results], axis=0)
    return y.reshape(1, T, D)


# version 15 (final)
# speedup vs baseline: 1.1789x; 1.0412x over previous
import contextlib
import numpy as np
import ml_dtypes
import concourse.bass as bass
import concourse.mybir as mybir
from concourse.bass_utils import run_bass_kernel_spmd

F32 = mybir.dt.float32
BF16 = mybir.dt.bfloat16
AF = mybir.ActivationFunctionType
ALU = mybir.AluOpType
AX = mybir.AxisListType
BF = ml_dtypes.bfloat16

N_CORES = 8
D = 2048
T = 8192
DC = D // 128
SEG = 512
NSEG = T // SEG
DK = 128
HPC = 2
CSHIFT = 20.0
ISQRT_DK = float(1.0 / np.sqrt(np.float32(DK)))
TPC = T // N_CORES


DRAIN_DVE = True
DRAIN_ACT = False


class Drained:
    _PASSTHRU = {"wait_ge", "drain", "nop", "wait_op"}

    def __init__(self, eng):
        self._e = eng

    def __getattr__(self, name):
        f = getattr(self._e, name)
        en = DRAIN_ACT if type(self._e).__name__.endswith("ScalarEngine") else DRAIN_DVE
        if name in Drained._PASSTHRU or not callable(f) or not en:
            return f
        def wrapped(*a, **k):
            inst = f(*a, **k)
            self._e.drain()
            return inst
        return wrapped

    def drain_if(self):
        pass


class Sem:

    def __init__(self, ctx, nc, name):
        self.h = ctx.enter_context(nc.semaphore(name))
        self.n = 0

    def inc(self, inst, k=1):
        inst.then_inc(self.h, k)
        self.n += k
        return self.n

    def wait(self, eng, n=None):
        eng.wait_ge(self.h, self.n if n is None else n)



class Env:

    def __init__(self, nc, block, ctx):
        self.nc, self.block, self.ctx = nc, block, ctx
        self.VEC = lambda f: block.vector(lambda v: f(Drained(v)))
        self.SCA = lambda f: block.scalar(lambda a: f(Drained(a)))

    def sem(self, name):
        return Sem(self.ctx, self.nc, name)


def _standalone(emit, io_builder):
    nc = bass.Bass("TRN2", target_bir_lowering=False, debug=False)
    ctx = contextlib.ExitStack()
    with ctx:
        io = io_builder(nc)
        block = ctx.enter_context(nc.Block())
        E = Env(nc, block, ctx)
        exits = emit(E, io, [])
        osem, oval = exits["out"]
        def fin_s(s):
            s.wait_ge(osem, oval)
        block.sync(fin_s)
    return nc


def build_p0(nseg=NSEG):
    def io_builder(nc):
        return dict(
            x0s=nc.dram_tensor("x0s", [128, NSEG * DC * SEG], F32, kind="ExternalInput").ap(),
            wq=nc.dram_tensor("wq", [128, DC * HPC * DK], F32, kind="ExternalInput").ap(),
            wk=nc.dram_tensor("wk", [128, DC * HPC * DK], F32, kind="ExternalInput").ap(),
            wv=nc.dram_tensor("wv", [128, DC * HPC * DK], F32, kind="ExternalInput").ap(),
            betas=nc.dram_tensor("betas", [1, HPC * DK], F32, kind="ExternalInput").ap(),
            ident=nc.dram_tensor("ident", [128, 128], BF16, kind="ExternalInput").ap(),
            binmask=nc.dram_tensor("binmask", [SEG, SEG], BF16, kind="ExternalInput").ap(),
            negc=nc.dram_tensor("negc", [128, 1], F32, kind="ExternalInput").ap(),
            att0=nc.dram_tensor("att0", [HPC * DK, T], BF16, kind="ExternalOutput").ap(),
        )
    return _standalone(lambda E, io, entry: emit_p0(E, io, entry, nseg=nseg), io_builder)


def emit_p0(E, io, entry, nseg=NSEG, tag="p0"):
    nc, block, ctx = E.nc, E.block, E.ctx
    x0s, wq, wk, wv = io["x0s"], io["wq"], io["wk"], io["wv"]
    betas, ident_in, binmask_in, negc_in = io["betas"], io["ident"], io["binmask"], io["negc"]
    att0 = io["att0"]
    with contextlib.ExitStack() as pctx:
        sb = lambda name, shape, dt: pctx.enter_context(nc.sbuf_tensor(tag + name, shape, dt))
        ps = lambda name: pctx.enter_context(nc.psum_tensor(tag + name, [128, 512], F32))

        wq_sb = sb("wq_sb", [128, DC * HPC * DK], BF16)
        wk_sb = sb("wk_sb", [128, DC * HPC * DK], BF16)
        wv_sb = sb("wv_sb", [128, DC * HPC * DK], BF16)
        xseg = [sb(f"xseg{i}", [128, DC * SEG], BF16) for i in range(2)]
        ident = sb("ident_sb", [128, 128], BF16)
        binmask = sb("binmask_sb", [128, 4 * SEG], BF16)
        negc = sb("negc_sb", [128, 1], F32)
        ones_f = sb("ones_f", [1, 128], F32)
        beta_f = sb("beta_f", [1, HPC * DK], F32)
        bcast = sb("bcast", [128, HPC * DK], F32)
        mem_f = sb("mem_f", [128, HPC * DK], F32)
        z_f = sb("z_f", [128, HPC], F32)
        memz = sb("memz", [128, HPC * 129], BF16)
        q_sb = sb("q_sb", [128, SEG], BF16)
        k_sb = sb("k_sb", [128, SEG], BF16)
        sq_sb = sb("sq_sb", [128, SEG], BF16)
        sk_sb = sb("sk_sb", [128, SEG], BF16)
        sk_tm = sb("sk_tm", [128, SEG], BF16)
        v_aug = sb("v_aug", [128, 4 * 129], BF16)
        pst = sb("pst", [128, 4 * SEG], BF16)
        ndelta = sb("ndelta", [128, SEG], BF16)
        tq1 = sb("tq1", [128, SEG], F32)
        tq2 = sb("tq2", [128, SEG], F32)
        tk1 = sb("tk1", [128, SEG], F32)
        tk2 = sb("tk2", [128, SEG], F32)
        att_d = sb("att_d", [128, 128], F32)
        att_m = sb("att_m", [128, 128], F32)
        cmb = sb("cmb", [128, 128], F32)
        att_bf = sb("att_bf", [128, 512], BF16)
        att_tile = sb("att_tile", [128, SEG], BF16)
        rc = sb("rc", [128, 4], F32)
        zadd = sb("zadd", [128, 1], F32)

        p_q = ps("p_q")
        p_k = ps("p_k")
        p_v = ps("p_v")
        p_st = ps("p_st")
        p_ad = ps("p_ad")
        p_am = ps("p_am")
        p_tr = ps("p_tr")
        p_mu = ps("p_mu")

        sW = E.sem(tag + "sW")
        sW2 = E.sem(tag + "sW2")
        sX = [E.sem(tag + "sX0"), E.sem(tag + "sX1")]
        sP = E.sem(tag + "sP")
        sA = E.sem(tag + "sA")
        sV = E.sem(tag + "sV")
        sO = E.sem(tag + "sO")

        x0s_v = x0s.rearrange("p (s q) -> p s q", q=DC * SEG)

        def load_xseg(g, seg, buf):
            sX[buf].inc(g.dma_start(out=xseg[buf][:, :], in_=x0s_v[:, seg, :]), 16)

        if True:
            VEC, SCA = E.VEC, E.SCA
            def init_g(g):
                for se, va in entry:
                    g.wait_ge(se, va)
                for w, tsb in ((wq, wq_sb), (wk, wk_sb), (wv, wv_sb)):
                    sW.inc(g.dma_start(out=tsb[:, :], in_=w), 16)
                sW.inc(g.dma_start(out=beta_f[:, :], in_=betas), 16)
                load_xseg(g, 0, 0)
            block.gpsimd(init_g)

            def init_s(s):
                for se, va in entry:
                    s.wait_ge(se, va)
                sW2.inc(s.dma_start(out=ident[:, :], in_=ident_in), 16)
                sW2.inc(s.dma_start(
                    out=binmask[:, :].rearrange("p (k t) -> p k t", t=SEG),
                    in_=binmask_in.rearrange("(k p) t -> p k t", p=128)), 16)
                sW2.inc(s.dma_start(out=negc[:, :], in_=negc_in), 16)
            block.sync(init_s)

            def init_v(v):
                for se, va in entry:
                    v.wait_ge(se, va)
                v.memset(ones_f[:, :], 1.0)
                v.memset(mem_f[:, :], 0.0)
                v.memset(z_f[:, :], 1.0 / DK)
                v.memset(memz[:, :], 0.0)
                v.memset(v_aug[:, :], 1.0)
                inst = None
                for h in range(HPC):
                    inst = v.tensor_copy(memz[:, h * 129 + 128:h * 129 + 129], z_f[:, h:h + 1])
                sV.inc(inst, 1)
            VEC(init_v)
            v_init = sV.n

            def init_a(a):
                for se, va in entry:
                    a.wait_ge(se, va)
                a.wait_ge(sW.h, sW.n)
                a.wait_ge(sW2.h, sW2.n)
                sA.inc(a.activation(out=beta_f[:, :], in_=beta_f[:, :], func=AF.Sigmoid), 1)
            SCA(init_a)
            a_sig = sA.n

            def init_t(t):
                t.wait_ge(sA.h, a_sig)
                t.wait_ge(sV.h, v_init)
                for h in range(HPC):
                    mmi = t.matmul(p_mu[:, h * 128:(h + 1) * 128], ones_f[:, :],
                             beta_f[:, h * 128:(h + 1) * 128], start=True, stop=True)
                sP.inc(mmi, 1)
            block.tensor(init_t)
            p_bc = sP.n

            def init_v2(v):
                v.wait_ge(sP.h, p_bc)
                sV.inc(v.tensor_copy(bcast[:, :], p_mu[:, 0:256]), 1)
            VEC(init_v2)

            prev = dict(a_sigk=0, v_sk=0, v_vaug=0, v_comb3=0, v_dl3=0,
                        v_kt=0, v_memz=sV.n, v_atile=[0, 0, 0, 0], pe_v=0,
                        aexp=[0, 0, 0, 0], v_mu=0, o_n=0)

            for seg in range(nseg):
                buf = seg % 2
                xv = xseg[buf][:, :]

                if seg + 1 < nseg:
                    def pre_g(g, seg=seg, guard=prev["pe_v"]):
                        if seg >= 1:
                            g.wait_ge(sP.h, guard)
                        load_xseg(g, seg + 1, (seg + 1) % 2)
                    block.gpsimd(pre_g)

                for h in range(HPC):
                    n = sP.n
                    pe_q = n + 1; pe_k = n + 2; pe_v = n + 3
                    pe_st = [n + 4 + i for i in range(4)]
                    pe_ad = [0] * 4; pe_am = [0] * 4
                    m = n + 7
                    for ss in range(4):
                        m += 1; pe_ad[ss] = m
                        m += 1; pe_am[ss] = m
                    pe_tr = [m + 1 + i for i in range(4)]
                    pe_dl = [m + 5 + i for i in range(4)]
                    pe_kt = m + 9
                    pe_mu = m + 10
                    sP.n = pe_mu

                    a_qb = sA.n + 1
                    a_qs = sA.n + 2
                    a_kb = sA.n + 3
                    a_ks = sA.n + 4
                    aexp = [sA.n + 5 + i for i in range(4)]
                    sA.n += 8

                    v_sq = sV.n + 1
                    v_sk = sV.n + 2
                    v_vaug = sV.n + 3
                    v_pst = [sV.n + 4 + i for i in range(4)]
                    v_comb = [sV.n + 8 + i for i in range(4)]
                    v_atile = [sV.n + 12 + i for i in range(4)]
                    v_dl = [sV.n + 16 + i for i in range(4)]
                    v_kt = sV.n + 20
                    v_memz = sV.n + 21
                    sV.n += 21

                    stb = [p_st, p_mu, p_st, p_mu]
                    adb = [p_ad, p_q, p_ad, p_q]
                    amb = [p_am, p_k, p_am, p_k]
                    trb = [p_tr, p_st, p_tr, p_st]
                    dlb = [p_am, p_ad, p_am, p_ad]

                    def ten_body(t, seg=seg, h=h, xv=xv, p=dict(prev),
                                 aexp=list(aexp), v_pst=list(v_pst),
                                 v_comb=list(v_comb), v_atile=list(v_atile),
                                 v_sq=v_sq, v_sk=v_sk, v_vaug=v_vaug,
                                 v_dl=list(v_dl), v_kt=v_kt):
                        wc = lambda dc: dc * (HPC * DK) + h * DK
                        t.wait_ge(sX[seg % 2].h, 16 * (seg // 2 + 1))
                        if p["a_sigk"]:
                            t.wait_ge(sA.h, p["a_sigk"])
                        if p["v_comb3"]:
                            t.wait_ge(sV.h, p["v_comb3"])
                        if p["v_vaug"]:
                            t.wait_ge(sV.h, p["v_vaug"])
                        mmi = None
                        for dc in range(DC):
                            mmi = t.matmul(p_q[:, :], wq_sb[:, wc(dc):wc(dc) + DK],
                                           xv[:, dc * SEG:(dc + 1) * SEG],
                                           start=(dc == 0), stop=(dc == DC - 1))
                        mmi.then_inc(sP.h, 1)
                        for dc in range(DC):
                            mmi = t.matmul(p_k[:, :], wk_sb[:, wc(dc):wc(dc) + DK],
                                           xv[:, dc * SEG:(dc + 1) * SEG],
                                           start=(dc == 0), stop=(dc == DC - 1))
                        mmi.then_inc(sP.h, 1)
                        for tt in range(4):
                            for dc in range(DC):
                                mmi = t.matmul(p_v[:, tt * 128:(tt + 1) * 128],
                                               xv[:, dc * SEG + tt * 128:dc * SEG + tt * 128 + 128],
                                               wv_sb[:, wc(dc):wc(dc) + DK],
                                               start=(dc == 0), stop=(dc == DC - 1))
                        mmi.then_inc(sP.h, 1)
                        for tt in range(4):
                            if tt == 0:
                                t.wait_ge(sA.h, a_ks)
                                if p["aexp"][2]:
                                    t.wait_ge(sA.h, p["aexp"][2])
                                if p["v_atile"][3]:
                                    t.wait_ge(sV.h, p["v_atile"][3])
                            if tt == 1:
                                if p["aexp"][3]:
                                    t.wait_ge(sA.h, p["aexp"][3])
                                if p["v_memz"]:
                                    t.wait_ge(sV.h, p["v_memz"])
                            if tt >= 2:
                                t.wait_ge(sA.h, aexp[tt - 2])
                            t.matmul(stb[tt][:, :], k_sb[:, tt * 128:(tt + 1) * 128],
                                     q_sb[:, :], start=True,
                                     stop=True).then_inc(sP.h, 1)
                        for ss in range(4):
                            if ss == 0:
                                t.wait_ge(sV.h, v_pst[3])
                                t.wait_ge(sV.h, v_vaug)
                                t.wait_ge(sV.h, v_sq)
                                if p["v_memz"]:
                                    t.wait_ge(sV.h, p["v_memz"])
                            if ss == 1:
                                t.wait_ge(sA.h, a_qb)
                                t.wait_ge(sA.h, a_kb)
                            if ss >= 2:
                                t.wait_ge(sV.h, v_comb[ss - 2])
                            mmi = None
                            for tt in range(4):
                                mmi = t.matmul(adb[ss][:, 0:129],
                                               pst[:, tt * SEG + ss * 128:tt * SEG + ss * 128 + 128],
                                               v_aug[:, tt * 129:(tt + 1) * 129],
                                               start=(tt == 0), stop=(tt == 3))
                            mmi.then_inc(sP.h, 1)
                            t.matmul(amb[ss][:, 0:129], sq_sb[:, ss * 128:(ss + 1) * 128],
                                     memz[:, h * 129:(h + 1) * 129], start=True,
                                     stop=True).then_inc(sP.h, 1)
                        for ss in range(4):
                            t.wait_ge(sV.h, v_comb[ss])
                            if ss == 0 and p["v_kt"]:
                                t.wait_ge(sV.h, p["v_kt"])
                            if ss == 1:
                                t.wait_ge(sA.h, aexp[2])
                            if ss >= 2:
                                t.wait_ge(sV.h, v_atile[ss - 2])
                            t.matmul(trb[ss][:, 0:128], att_bf[:, ss * 128:(ss + 1) * 128],
                                     ident[:, :], start=True, stop=True).then_inc(sP.h, 1)
                        for tt in range(4):
                            if tt == 0:
                                t.wait_ge(sV.h, v_sk)
                                t.wait_ge(sV.h, v_comb[3])
                            if tt >= 2:
                                t.wait_ge(sV.h, v_dl[tt - 2])
                            t.matmul(dlb[tt][:, 0:129], sk_sb[:, tt * 128:(tt + 1) * 128],
                                     memz[:, h * 129:(h + 1) * 129], start=True,
                                     stop=True).then_inc(sP.h, 1)
                        t.wait_ge(sV.h, v_atile[2])
                        mmi = None
                        for tt in range(4):
                            mmi = t.matmul(p_tr[:, tt * 128:(tt + 1) * 128],
                                           sk_sb[:, tt * 128:(tt + 1) * 128], ident[:, :],
                                           start=True, stop=True)
                        mmi.then_inc(sP.h, 1)
                        t.wait_ge(sV.h, v_kt)
                        t.wait_ge(sV.h, v_dl[3])
                        t.wait_ge(sA.h, aexp[3])
                        mmi = None
                        for tt in range(4):
                            mmi = t.matmul(p_mu[:, 0:128], sk_tm[:, tt * 128:(tt + 1) * 128],
                                           ndelta[:, tt * 128:(tt + 1) * 128],
                                           start=(tt == 0), stop=(tt == 3))
                        mmi.then_inc(sP.h, 1)
                    block.tensor(ten_body)

                    def sca_body(a, p=dict(prev), pe_q=pe_q, pe_k=pe_k,
                                 pe_st=list(pe_st)):
                        a.wait_ge(sP.h, pe_q)
                        if p["v_sk"]:
                            a.wait_ge(sV.h, p["v_sk"])
                        a.activation(out=q_sb[:, :], in_=p_q[:, :], func=AF.Copy)
                        a.activation(out=tq1[:, :], in_=p_q[:, :], func=AF.Relu)
                        a.activation(out=tq2[:, :], in_=p_q[:, :], func=AF.Relu,
                                     scale=-1.0).then_inc(sA.h, 1)
                        a.activation(out=tq2[:, :], in_=tq2[:, :], func=AF.Exp,
                                     scale=-1.0).then_inc(sA.h, 1)
                        a.wait_ge(sP.h, pe_k)
                        a.activation(out=k_sb[:, :], in_=p_k[:, :], func=AF.Copy)
                        a.activation(out=tk1[:, :], in_=p_k[:, :], func=AF.Relu)
                        a.activation(out=tk2[:, :], in_=p_k[:, :], func=AF.Relu,
                                     scale=-1.0).then_inc(sA.h, 1)
                        a.activation(out=tk2[:, :], in_=tk2[:, :], func=AF.Exp,
                                     scale=-1.0).then_inc(sA.h, 1)
                        for tt in range(4):
                            a.wait_ge(sP.h, pe_st[tt])
                            a.activation(out=pst[:, tt * SEG:(tt + 1) * SEG],
                                         in_=stb[tt][:, :], func=AF.Exp,
                                         scale=ISQRT_DK,
                                         bias=negc[:, :]).then_inc(sA.h, 1)
                    block.scalar(lambda a: sca_body(Drained(a)))

                    def vec_body(vr, h=h, p=dict(prev), pe_v=pe_v,
                                 pe_ad=list(pe_ad), pe_am=list(pe_am),
                                 pe_tr=list(pe_tr), pe_dl=list(pe_dl),
                                 pe_kt=pe_kt, pe_mu=pe_mu,
                                 a_qs=a_qs, a_ks=a_ks, aexp=list(aexp)):
                        v = Drained(vr)
                        v.wait_ge(sA.h, a_qs)
                        v.tensor_add(sq_sb[:, :], tq1[:, :], tq2[:, :]).then_inc(sV.h, 1)
                        v.wait_ge(sA.h, a_ks)
                        v.tensor_add(sk_sb[:, :], tk1[:, :], tk2[:, :])
                        v.reduce_sum(zadd[:, :], sk_sb[:, :], axis=AX.X).then_inc(sV.h, 1)
                        v.wait_ge(sP.h, pe_v)
                        inst = None
                        for tt in range(4):
                            inst = v.tensor_copy(v_aug[:, tt * 129:tt * 129 + 128],
                                                 p_v[:, tt * 128:(tt + 1) * 128])
                        inst.then_inc(sV.h, 1)
                        for tt in range(4):
                            v.wait_ge(sA.h, aexp[tt])
                            v.tensor_mul(pst[:, tt * SEG:(tt + 1) * SEG],
                                         pst[:, tt * SEG:(tt + 1) * SEG],
                                         binmask[:, tt * SEG:(tt + 1) * SEG]).then_inc(sV.h, 1)
                        for ss in range(4):
                            v.wait_ge(sP.h, pe_ad[ss])
                            v.reciprocal(rc[:, 0:1], adb[ss][:, 128:129])
                            v.tensor_scalar(att_d[:, :], adb[ss][:, 0:128], rc[:, 0:1],
                                            None, ALU.mult)
                            v.wait_ge(sP.h, pe_am[ss])
                            v.reciprocal(rc[:, 1:2], amb[ss][:, 128:129])
                            v.tensor_scalar(att_m[:, :], amb[ss][:, 0:128], rc[:, 1:2],
                                            None, ALU.mult)
                            v.tensor_sub(cmb[:, :], att_m[:, :], att_d[:, :])
                            v.tensor_mul(cmb[:, :], cmb[:, :], bcast[:, h * 128:(h + 1) * 128])
                            v.tensor_add(att_bf[:, ss * 128:(ss + 1) * 128], att_d[:, :],
                                         cmb[:, :]).then_inc(sV.h, 1)
                        for ss in range(4):
                            v.wait_ge(sP.h, pe_tr[ss])
                            if ss == 0 and p["o_n"]:
                                v.wait_ge(sO.h, p["o_n"])
                            v.tensor_copy(att_tile[:, ss * 128:(ss + 1) * 128],
                                          trb[ss][:, 0:128]).then_inc(sV.h, 1)
                        for tt in range(4):
                            v.wait_ge(sP.h, pe_dl[tt])
                            v.reciprocal(rc[:, 2:3], dlb[tt][:, 128:129])
                            v.tensor_scalar(rc[:, 3:4], rc[:, 2:3], -1.0, None, ALU.mult)
                            v.scalar_tensor_tensor(
                                ndelta[:, tt * 128:(tt + 1) * 128], dlb[tt][:, 0:128],
                                rc[:, 3:4], v_aug[:, tt * 129:tt * 129 + 128],
                                ALU.mult, ALU.add).then_inc(sV.h, 1)
                        v.wait_ge(sP.h, pe_kt)
                        v.tensor_copy(sk_tm[:, :], p_tr[:, :]).then_inc(sV.h, 1)
                        v.wait_ge(sP.h, pe_mu)
                        v.tensor_add(mem_f[:, h * 128:(h + 1) * 128],
                                     mem_f[:, h * 128:(h + 1) * 128], p_mu[:, 0:128])
                        v.tensor_add(z_f[:, h:h + 1], z_f[:, h:h + 1], zadd[:, :])
                        v.tensor_copy(memz[:, h * 129:h * 129 + 128],
                                      mem_f[:, h * 128:(h + 1) * 128])
                        v.tensor_copy(memz[:, h * 129 + 128:h * 129 + 129],
                                      z_f[:, h:h + 1]).then_inc(sV.h, 1)
                    block.vector(vec_body)

                    def out_s(s, h=h, seg=seg, v_at=v_atile[3]):
                        s.wait_ge(sV.h, v_at)
                        sO.inc(s.dma_start(out=att0[h * DK:(h + 1) * DK, seg * SEG:(seg + 1) * SEG],
                                           in_=att_tile[:, :]), 16)
                    block.sync(out_s)

                    prev = dict(a_sigk=a_kb, v_sk=v_sk, v_vaug=v_vaug,
                                v_comb3=v_comb[3], v_dl3=v_dl[3], v_kt=v_kt,
                                v_memz=v_memz, v_atile=list(v_atile), pe_v=pe_v,
                                aexp=list(aexp), v_mu=v_memz, o_n=sO.n)


    return {"pe": (sP.h, sP.n), "act": (sA.h, sA.n), "dve": (sV.h, sV.n),
            "out": (sO.h, sO.n)}



def _consts_p0():
    ident = np.eye(128, dtype=BF)
    t_idx = np.arange(SEG)[:, None]
    s_idx = np.arange(SEG)[None, :]
    binmask = (s_idx >= t_idx).astype(BF)
    negc = np.full((128, 1), -CSHIFT, np.float32)
    return ident, binmask, negc


def _p0_in_maps(x0T_np, Wq, Wk, Wv, betas):
    ident, binmask, negc = _consts_p0()
    x0s = _prearr_x0(np.ascontiguousarray(x0T_np.T))
    maps = []
    for c in range(N_CORES):
        cols = slice(c * HPC * DK, (c + 1) * HPC * DK)
        maps.append({
            "x0s": x0s,
            "wq": _prearr_dmat(np.ascontiguousarray(Wq[:, cols])),
            "wk": _prearr_dmat(np.ascontiguousarray(Wk[:, cols])),
            "wv": _prearr_dmat(np.ascontiguousarray(Wv[:, cols])),
            "betas": np.ascontiguousarray(
                betas.reshape(16, DK)[2 * c:2 * c + 2, :].reshape(1, HPC * DK)),
            "ident": ident, "binmask": binmask, "negc": negc,
        })
    return maps





FFN_I = 8192
GSZ = 512
NG = FFN_I // GSZ
GC = GSZ // 128
NT = TPC // 128
LN_EPS = 1e-5


def build_p1(sparsify, final_tm):
    def io_builder(nc):
        io = dict(
            attc_mode="direct",
            attc=nc.dram_tensor("attc", [D, TPC], BF16, kind="ExternalInput").ap(),
            wo=nc.dram_tensor("wo", [128, DC * D], F32, kind="ExternalInput").ap(),
            wg=nc.dram_tensor("wg", [128, NG * DC * GSZ], F32, kind="ExternalInput").ap(),
            wu=nc.dram_tensor("wu", [128, NG * DC * GSZ], F32, kind="ExternalInput").ap(),
            wd=nc.dram_tensor("wd", [128, NG * GC * D], F32, kind="ExternalInput").ap(),
            ident=nc.dram_tensor("ident", [128, 128], BF16, kind="ExternalInput").ap(),
            identf=nc.dram_tensor("identf", [128, 128], F32, kind="ExternalInput").ap(),
            eps=nc.dram_tensor("epsrow", [1, 1], F32, kind="ExternalInput").ap(),
            x1scr=nc.dram_tensor("x1scr", [D, TPC], F32).ap(),
            xsum_t=None, cc_wait=None,
        )
        xres = nc.dram_tensor("xres", [D, TPC], F32, kind="ExternalInput").ap()
        io["xres_chunk"] = lambda s, dc: xres[dc * 128:(dc + 1) * 128, :]
        if final_tm:
            io["out_mode"] = "yout"
            io["out"] = nc.dram_tensor("yout", [TPC, D], F32, kind="ExternalOutput").ap()
        else:
            io["out_mode"] = "x2T"
            io["out"] = nc.dram_tensor("x2T", [D, TPC], F32, kind="ExternalOutput").ap()
        return io
    return _standalone(lambda E, io, entry: emit_p1(E, io, entry, sparsify, final_tm),
                       io_builder)


def emit_p1(E, io, entry, sparsify, final_tm, tag="p1"):
    nc, block, ctx = E.nc, E.block, E.ctx
    wo, wg, wu, wd = io["wo"], io["wg"], io["wu"], io["wd"]
    ident_in, identf_in, eps_in = io["ident"], io["identf"], io["eps"]
    out = io["out"]
    x1scr = io["x1scr"]
    out_mode = io["out_mode"]
    resident_res = io["xres_chunk"] is None
    with contextlib.ExitStack() as pctx:
        sb = lambda name, shape, dt: pctx.enter_context(nc.sbuf_tensor(tag + name, shape, dt))

        if io["xsum_t"] is None:
            xsum = sb("xsum", [128, DC * TPC], F32)
        else:
            xsum = io["xsum_t"]
        xstr = [sb(f"xstr{i}", [128, TPC], F32) for i in range(2)]
        ident = sb("ident_sb", [128, 128], BF16)
        identf = sb("identf_sb", [128, 128], F32)
        ones_col = sb("ones_col", [128, 1], F32)
        ones_row = sb("ones_row", [1, 128], F32)
        eps_sb = sb("eps_sb", [1, 1], F32)
        strow = sb("strow", [1, 3 * TPC], F32)
        silu_s = sb("silu_s", [128, 512], F32)

        pA = pctx.enter_context(nc.psum_tensor(tag + "pA", [128, 1024], F32))
        pB = pctx.enter_context(nc.psum_tensor(tag + "pB", [128, 1024], F32))
        pC = pctx.enter_context(nc.psum_tensor(tag + "pC", [128, 512], F32))
        pD = pctx.enter_context(nc.psum_tensor(tag + "pD", [128, 512], F32))
        pE = pctx.enter_context(nc.psum_tensor(tag + "pE", [128, 512], F32))
        pF = pctx.enter_context(nc.psum_tensor(tag + "pF", [128, 512], F32))

        sLo = E.sem(tag + "sLo")
        sLgu = E.sem(tag + "sLgu")
        sLd = E.sem(tag + "sLd")
        sWi = E.sem(tag + "sWi")
        sAt = E.sem(tag + "sAt")
        sXr = [E.sem(tag + "sXr0"), E.sem(tag + "sXr1")]
        sP = E.sem(tag + "sP")
        sA = E.sem(tag + "sA")
        sV = E.sem(tag + "sV")
        sScr = E.sem(tag + "sScr")
        sO = E.sem(tag + "sO")

        if True:
            VEC, SCA = E.VEC, E.SCA

            def emit_ln(src, v_ready):
                pe_st = []
                for dc in range(DC):
                    def sq_a(a, dc=dc):
                        if dc == 0:
                            a.wait_ge(sV.h, v_ready)
                        if dc >= 2:
                            a.wait_ge(sP.h, pe_st[dc - 2])
                        sA.inc(a.activation(out=xstr[dc % 2][:, :],
                                            in_=src[:, dc * TPC:(dc + 1) * TPC],
                                            func=AF.Square), 1)
                    SCA(sq_a)
                    a_n = sA.n

                    def st_t(t, dc=dc, a_n=a_n):
                        if dc == 0:
                            t.wait_ge(sV.h, v_ready)
                        for tb, bank in ((0, pC), (1, pD)):
                            t.matmul(bank[0:1, 0:512], ones_col[:, :],
                                     src[:, dc * TPC + tb * 512:dc * TPC + tb * 512 + 512],
                                     start=(dc == 0), stop=(dc == DC - 1))
                        t.wait_ge(sA.h, a_n)
                        mmi = None
                        for tb, bank in ((0, pE), (1, pF)):
                            mmi = t.matmul(bank[0:1, 0:512], ones_col[:, :],
                                           xstr[dc % 2][:, tb * 512:tb * 512 + 512],
                                           start=(dc == 0), stop=(dc == DC - 1))
                        sP.inc(mmi, 1)
                    block.tensor(st_t)
                    pe_st.append(sP.n)
                pe_stats = sP.n

                def stat_v(v, pe_stats=pe_stats):
                    m = strow[0:1, 0:TPC]
                    w = strow[0:1, TPC:2 * TPC]
                    tmp = strow[0:1, 2 * TPC:3 * TPC]
                    v.wait_ge(sP.h, pe_stats)
                    v.tensor_scalar(m[0:1, 0:512], pC[0:1, 0:512], 1.0 / D, None, ALU.mult)
                    v.tensor_scalar(m[0:1, 512:1024], pD[0:1, 0:512], 1.0 / D, None, ALU.mult)
                    v.tensor_scalar(w[0:1, 0:512], pE[0:1, 0:512], 1.0 / D, None, ALU.mult)
                    v.tensor_scalar(w[0:1, 512:1024], pF[0:1, 0:512], 1.0 / D, None, ALU.mult)
                    v.tensor_mul(tmp, m, m)
                    sV.inc(v.tensor_sub(w, w, tmp), 1)
                    vv = sV.n

                    return vv
                v_var = VEC(stat_v) or sV.n

                def sd_a(a, v_var=v_var):
                    a.wait_ge(sV.h, v_var)
                    sA.inc(a.activation(out=strow[0:1, 2 * TPC:3 * TPC],
                                        in_=strow[0:1, TPC:2 * TPC],
                                        func=AF.Sqrt, bias=eps_sb[0:1, 0:1]), 1)
                SCA(sd_a)
                a_sd = sA.n

                def rstd_v(v, a_sd=a_sd):
                    v.wait_ge(sA.h, a_sd)
                    sd = strow[0:1, 2 * TPC:3 * TPC]
                    w = strow[0:1, TPC:2 * TPC]
                    m2 = strow[0:1, 0:TPC]
                    v.reciprocal(w, sd)
                    v.tensor_mul(sd, sd, w)
                    v.tensor_scalar(sd, sd, -1.0, 2.0, ALU.mult, ALU.add)
                    sV.inc(v.tensor_mul(w, w, sd), 1)
                VEC(rstd_v)
                v_rstd = sV.n

                def bc_t(t, v_rstd=v_rstd):
                    t.wait_ge(sV.h, v_rstd)
                    mmi = None
                    for tb in range(2):
                        t.matmul(pA[:, tb * 512:(tb + 1) * 512], ones_row[:, :],
                                 strow[0:1, tb * 512:(tb + 1) * 512], start=True, stop=True)
                        mmi = t.matmul(pB[:, tb * 512:(tb + 1) * 512], ones_row[:, :],
                                       strow[0:1, TPC + tb * 512:TPC + (tb + 1) * 512],
                                       start=True, stop=True)
                    sP.inc(mmi, 1)
                block.tensor(bc_t)
                return sP.n

            _scA = nc.named_scope(tag + "_compose"); _scA.__enter__()
            with nc.sbuf_tensor(tag + "attc_sb", [128, DC * TPC], BF16) as attc_sb, \
                 contextlib.ExitStack() as actx:

                v_scr = 0
                if io["attc_mode"] == "direct":
                    attc = io["attc"]
                    def attc_s(s):
                        for se, va in entry:
                            s.wait_ge(se, va)
                        sAt.inc(s.dma_start(
                            out=attc_sb[:, :].rearrange("p (k t) -> p k t", t=TPC),
                            in_=attc.rearrange("(k p) t -> p k t", p=128)), 16)
                    block.sync(attc_s)
                elif io["attc_mode"] == "gather":
                    attg = io["attc_g"]
                    ccw = io["cc_wait"]
                    def attc_s(s):
                        for se, va in entry:
                            s.wait_ge(se, va)
                        s.wait_ge(ccw[0], ccw[1])
                        pid = s.partition_id()
                        sAt.inc(s.dma_start(
                            out=attc_sb[:, :].rearrange("p (k t) -> p k t", t=TPC),
                            in_=attg.rearrange("(k p) t -> p k t", p=128)[:, :, bass.ds(pid * TPC, TPC)]), 16)
                    block.sync(attc_s)
                else:
                    attg = io["attc_g"]
                    ccw = io["cc_wait"]
                    with nc.sbuf_tensor(tag + "att_raw", [128, DC * TPC], BF16) as att_raw:
                        def attc_s(s):
                            for se, va in entry:
                                s.wait_ge(se, va)
                            s.wait_ge(ccw[0], ccw[1])
                            pid = s.partition_id()
                            sAt.inc(s.dma_start(
                                out=att_raw[:, :].rearrange("p (k t) -> p k t", t=TPC),
                                in_=attg.rearrange("(k p) t -> p k t", p=128)[:, :, bass.ds(pid * TPC, TPC)]), 16)
                        block.sync(attc_s)

                        def scr_v(v, n_at=None):
                            for se, va in entry:
                                v.wait_ge(se, va)
                            v.wait_ge(sAt.h, sAt.n)
                            dst = attc_sb[:, :].rearrange(
                                "p (j segl h w) -> p j segl h w", j=16, segl=2, h=16, w=32)
                            srcv = att_raw[:, :].rearrange(
                                "p (h segl w j) -> p h segl w j", h=16, segl=2, w=32, j=16
                            ).transpose([0, 4, 2, 1, 3])
                            sV.inc(v.tensor_copy(dst, srcv), 1)
                        VEC(scr_v)
                        v_scr = sV.n

                wo_sb = actx.enter_context(nc.sbuf_tensor(tag + "wo_sb", [128, DC * D], BF16))

                def init_g(g):
                    for se, va in entry:
                        g.wait_ge(se, va)
                    if v_scr:
                        g.wait_ge(sV.h, v_scr)
                    sLo.inc(g.dma_start(out=wo_sb[:, :], in_=wo), 16)
                block.gpsimd(init_g)

                xres_w = {}
                def init_s(s):
                    sWi.inc(s.dma_start(out=ident[:, :], in_=ident_in), 16)
                    sWi.inc(s.dma_start(out=identf[:, :], in_=identf_in), 16)
                    sWi.inc(s.dma_start(out=eps_sb[:, :], in_=eps_in), 16)
                    if not resident_res:
                        for i in range(2):
                            xres_w[i] = sXr[i].inc(s.dma_start(
                                out=xstr[i][:, :], in_=io["xres_chunk"](s, i)), 16)
                block.sync(init_s)

                def init_v(v):
                    for se, va in entry:
                        v.wait_ge(se, va)
                    v.memset(ones_col[:, :], 1.0)
                    sV.inc(v.memset(ones_row[:, :], 1.0), 1)
                VEC(init_v)

                vcomb = []
                for fc in range(DC):
                    for tb in range(2):
                        bank = pC if tb == 0 else pD
                        def comb_t(t, fc=fc, tb=tb, bank=bank):
                            if fc == 0 and tb == 0:
                                t.wait_ge(sLo.h, sLo.n)
                                t.wait_ge(sWi.h, sWi.n)
                                t.wait_ge(sAt.h, sAt.n)
                                if v_scr:
                                    t.wait_ge(sV.h, v_scr)
                            if len(vcomb) >= 2:
                                t.wait_ge(sV.h, vcomb[-2])
                            mmi = None
                            for ac in range(DC):
                                mmi = t.matmul(bank[:, 0:512],
                                               wo_sb[:, ac * D + fc * 128:ac * D + fc * 128 + 128],
                                               attc_sb[:, ac * TPC + tb * 512:ac * TPC + tb * 512 + 512],
                                               start=(ac == 0), stop=(ac == DC - 1))
                            sP.inc(mmi, 1)
                        block.tensor(comb_t)
                        pe_c = sP.n

                        def comb_v(v, fc=fc, tb=tb, bank=bank, pe_c=pe_c,
                                   w_n=(0 if resident_res else xres_w[fc])):
                            v.wait_ge(sP.h, pe_c)
                            dst = xsum[:, fc * TPC + tb * 512:fc * TPC + tb * 512 + 512]
                            if resident_res:
                                sV.inc(v.tensor_add(dst, bank[:, 0:512], dst), 1)
                            else:
                                v.wait_ge(sXr[fc % 2].h, w_n)
                                sV.inc(v.tensor_add(dst, bank[:, 0:512],
                                                    xstr[fc % 2][:, tb * 512:tb * 512 + 512]), 1)
                        VEC(comb_v)
                        vcomb.append(sV.n)
                    if fc + 2 < DC and not resident_res:
                        def pre_s(s, fc=fc, guard=vcomb[-1]):
                            s.wait_ge(sV.h, guard)
                            xres_w[fc + 2] = sXr[fc % 2].inc(
                                s.dma_start(out=xstr[fc % 2][:, :],
                                            in_=io["xres_chunk"](s, fc + 2)), 16)
                        block.sync(pre_s)
                v_combe = sV.n
                pe_combe = sP.n
            _scA.__exit__(None, None, None)

            with nc.sbuf_tensor(tag + "x1b", [128, DC * TPC], BF16) as x1b:
                _scL = nc.named_scope(tag + "_ln1"); _scL.__enter__()
                pe_bc1 = emit_ln(xsum, v_combe)
                scr_w = 0
                for dc in range(DC):
                    def ap_v(v, dc=dc, pe_bc1=pe_bc1):
                        if dc == 0:
                            v.wait_ge(sP.h, pe_bc1)
                        c = xsum[:, dc * TPC:(dc + 1) * TPC]
                        v.tensor_sub(c, c, pA[:, :])
                        sV.inc(v.tensor_mul(c, c, pB[:, :]), 1)
                    VEC(ap_v)
                    v_n = sV.n

                    def ap_a(a, dc=dc, v_n=v_n):
                        a.wait_ge(sV.h, v_n)
                        sA.inc(a.activation(out=x1b[:, dc * TPC:(dc + 1) * TPC],
                                            in_=xsum[:, dc * TPC:(dc + 1) * TPC],
                                            func=AF.Copy), 1)
                    SCA(ap_a)

                    def ap_s(s, dc=dc, v_n=v_n):
                        s.wait_ge(sV.h, v_n)
                        sScr.inc(s.dma_start(out=x1scr[dc * 128:(dc + 1) * 128, :],
                                             in_=xsum[:, dc * TPC:(dc + 1) * TPC]), 16)
                    block.sync(ap_s)
                a_x1b = sA.n
                v_ap1 = sV.n
                scr_w = sScr.n
                _scL.__exit__(None, None, None)

                _scF = nc.named_scope(tag + "_ffn"); _scF.__enter__()
                QSZ = NT * GSZ // 4
                with nc.sbuf_tensor(tag + "wg_sb", [128, DC * GSZ], BF16) as wg_sb, \
                     nc.sbuf_tensor(tag + "wu_sb", [128, DC * GSZ], BF16) as wu_sb, \
                     nc.sbuf_tensor(tag + "wd_sb", [128, GC * D], BF16) as wd_sb, \
                     nc.sbuf_tensor(tag + "h_tm0", [128, NT * GSZ], BF16) as h_tm0, \
                     nc.sbuf_tensor(tag + "h_tm1", [128, NT * GSZ], BF16) as h_tm1, \
                     nc.sbuf_tensor(tag + "h_fm0", [128, GC * TPC], BF16) as h_fm0, \
                     nc.sbuf_tensor(tag + "h_fm1", [128, GC * TPC], BF16) as h_fm1, \
                     nc.sbuf_tensor(tag + "habs", [128, QSZ], BF16) as habs, \
                     nc.sbuf_tensor(tag + "spm", [128, QSZ], BF16) as spm:
                    h_tmb = [h_tm0, h_tm1]
                    h_fmb = [h_fm0, h_fm1]

                    wg_v = wg.rearrange("p (g q) -> p g q", q=DC * GSZ)
                    wu_v = wu.rearrange("p (g q) -> p g q", q=DC * GSZ)
                    wd_v = wd.rearrange("p (g q) -> p g q", q=GC * D)

                    wld_gu = {}
                    wld_d = {}

                    def load_gu(g_eng, g, guard):
                        g_eng.wait_ge(sP.h, guard)
                        sLgu.inc(g_eng.dma_start(out=wg_sb[:, :], in_=wg_v[:, g, :]), 16)
                        sLgu.inc(g_eng.dma_start(out=wu_sb[:, :], in_=wu_v[:, g, :]), 16)
                        wld_gu[g] = sLgu.n

                    def load_d(g_eng, g, guard):
                        if guard:
                            g_eng.wait_ge(sP.h, guard)
                        sLd.inc(g_eng.dma_start(out=wd_sb[:, :], in_=wd_v[:, g, :]), 16)
                        wld_d[g] = sLd.n

                    block.gpsimd(lambda e: (load_gu(e, 0, pe_combe), load_d(e, 0, 0)))

                    pe_g = {}; pe_u = {}; pe_gu_end = {}; pe_tr = {}; pe_dn = {}
                    pe_group_end = {}
                    a_sil = {}; a_abs = {}
                    v_h = {}; v_sp = {}; v_trv = {}; v_dnv = {}

                    NSP = 4 if sparsify else 0

                    def plan_slot(g):
                        n = sP.n
                        for tt in range(NT):
                            n += 1; pe_g[(g, tt)] = n
                            n += 1; pe_u[(g, tt)] = n
                        pe_gu_end[g] = n
                        if g >= 1:
                            for tt in range(NT):
                                n += 1; pe_tr[(g - 1, tt)] = n
                            for idx in range(2 * DC):
                                n += 1; pe_dn[(g - 1, idx)] = n
                            pe_group_end[g - 1] = n
                        sP.n = n
                        n = sA.n
                        for tt in range(NT):
                            n += 1; a_sil[(g, tt)] = n
                            if NSP and tt % 2 == 1:
                                n += 1; a_abs[(g, tt // 2)] = n
                        sA.n = n
                        n = sV.n
                        for tt in range(NT):
                            n += 1; v_h[(g, tt)] = n
                            if NSP and tt % 2 == 1 and tt < NT - 1:
                                n += 1; v_sp[(g, tt // 2)] = n
                        if g >= 1:
                            for tt in range(NT):
                                n += 1; v_trv[(g - 1, tt)] = n
                            for q in range(DC):
                                n += 1; v_dnv[(g - 1, q)] = n
                        if NSP:
                            n += 1; v_sp[(g, 3)] = n
                        sV.n = n

                    def ten_slot(t, g):
                        htm = h_tmb[g % 2] if g < NG else None
                        if g < NG:
                            for tt in range(NT):
                                if tt == 0:
                                    t.wait_ge(sLgu.h, wld_gu[g])
                                    if g == 0:
                                        t.wait_ge(sA.h, a_x1b)
                                    else:
                                        t.wait_ge(sA.h, a_sil[(g - 1, NT - 1)])
                                        t.wait_ge(sV.h, v_h[(g - 1, NT - 1)])
                                else:
                                    t.wait_ge(sA.h, a_sil[(g, tt - 1)])
                                    t.wait_ge(sV.h, v_h[(g, tt - 1)])
                                mmi = None
                                for dc in range(DC):
                                    mmi = t.matmul(pC[:, :],
                                                   x1b[:, dc * TPC + tt * 128:dc * TPC + tt * 128 + 128],
                                                   wg_sb[:, dc * GSZ:(dc + 1) * GSZ],
                                                   start=(dc == 0), stop=(dc == DC - 1))
                                mmi.then_inc(sP.h, 1)
                                for dc in range(DC):
                                    mmi = t.matmul(pD[:, :],
                                                   x1b[:, dc * TPC + tt * 128:dc * TPC + tt * 128 + 128],
                                                   wu_sb[:, dc * GSZ:(dc + 1) * GSZ],
                                                   start=(dc == 0), stop=(dc == DC - 1))
                                mmi.then_inc(sP.h, 1)
                        if g >= 1:
                            gp = g - 1
                            hsrc = h_tmb[gp % 2]
                            hfm = h_fmb[gp % 2]
                            for tt in range(NT):
                                if tt == 0:
                                    if sparsify:
                                        t.wait_ge(sV.h, v_sp[(gp, NSP - 1)])
                                    else:
                                        t.wait_ge(sV.h, v_h[(gp, NT - 1)])
                                    if gp >= 2:
                                        t.wait_ge(sV.h, v_trv[(gp - 1, NT - 1)])
                                else:
                                    t.wait_ge(sV.h, v_trv[(gp, tt - 1)])
                                mmi = None
                                for ic in range(GC):
                                    mmi = t.matmul(pE[:, ic * 128:(ic + 1) * 128],
                                                   hsrc[:, tt * GSZ + ic * 128:tt * GSZ + ic * 128 + 128],
                                                   ident[:, :], start=True, stop=True)
                                mmi.then_inc(sP.h, 1)
                            for idx in range(2 * DC):
                                fc, tb = idx // 2, idx % 2
                                q = idx // 2
                                bank = pA if q % 2 == 0 else pB
                                if idx == 0:
                                    t.wait_ge(sV.h, v_trv[(gp, NT - 1)])
                                    t.wait_ge(sLd.h, wld_d[gp])
                                    if gp == 0:
                                        t.wait_ge(sScr.h, scr_w)
                                if q <= 1 and tb == 0 and gp == 0:
                                    t.wait_ge(sV.h, v_ap1)
                                if q >= 2 and tb == 0:
                                    t.wait_ge(sV.h, v_dnv[(gp, q - 2)])
                                mmi = None
                                for ic in range(GC):
                                    mmi = t.matmul(bank[:, tb * 512:tb * 512 + 512],
                                                   wd_sb[:, ic * D + fc * 128:ic * D + fc * 128 + 128],
                                                   hfm[:, ic * TPC + tb * 512:ic * TPC + tb * 512 + 512],
                                                   start=(ic == 0), stop=(ic == GC - 1))
                                mmi.then_inc(sP.h, 1)

                    def sca_slot(a, g):
                        if g >= NG:
                            return
                        htm = h_tmb[g % 2]
                        for tt in range(NT):
                            a.wait_ge(sP.h, pe_g[(g, tt)])
                            a.activation(out=silu_s[:, :], in_=pC[:, :],
                                         func=AF.Silu).then_inc(sA.h, 1)
                            DRAIN_ACT and a.drain()
                            if NSP and tt % 2 == 1:
                                q = tt // 2
                                a.wait_ge(sV.h, v_h[(g, tt)])
                                if q >= 1:
                                    a.wait_ge(sV.h, v_sp[(g, q - 1)])
                                a.activation(out=habs[:, :], in_=htm[:, q * QSZ:(q + 1) * QSZ],
                                             func=AF.Abs).then_inc(sA.h, 1)
                                DRAIN_ACT and a.drain()

                    def sp_chain(eng, g, q, ab, spb, fin):
                        htm = h_tmb[g % 2]
                        QQ = QSZ // 4
                        hv = htm[:, q * QSZ:(q + 1) * QSZ].rearrange("p (b four) -> p b four", four=4)
                        av = ab[:, :].rearrange("p (b four) -> p b four", four=4)
                        s0 = spb[:, 0 * QQ:1 * QQ]; s1 = spb[:, 1 * QQ:2 * QQ]
                        s2 = spb[:, 2 * QQ:3 * QQ]
                        tmp = spb[:, 3 * QQ:4 * QQ]
                        aa = [av[:, :, j] for j in range(4)]
                        hh = [hv[:, :, j] for j in range(4)]
                        D_ = Drained(eng)
                        D_.tensor_tensor(tmp, aa[0], aa[1], ALU.is_ge)
                        D_.tensor_copy(s0, tmp)
                        D_.tensor_scalar(s1, tmp, -1.0, None, ALU.mult)
                        D_.tensor_tensor(tmp, aa[0], aa[2], ALU.is_ge)
                        D_.tensor_add(s0, s0, tmp)
                        D_.tensor_scalar(s2, tmp, -1.0, None, ALU.mult)
                        D_.tensor_tensor(tmp, aa[0], aa[3], ALU.is_ge)
                        D_.tensor_add(s0, s0, tmp)
                        D_.tensor_tensor(tmp, aa[1], aa[2], ALU.is_ge)
                        D_.tensor_add(s1, s1, tmp)
                        D_.tensor_sub(s2, s2, tmp)
                        D_.tensor_tensor(tmp, aa[1], aa[3], ALU.is_ge)
                        D_.tensor_add(s1, s1, tmp)
                        D_.tensor_tensor(tmp, aa[2], aa[3], ALU.is_ge)
                        D_.tensor_add(s2, s2, tmp)
                        D_.tensor_single_scalar(s0, s0, 1.5, ALU.is_ge)
                        D_.tensor_single_scalar(s1, s1, 0.5, ALU.is_ge)
                        D_.tensor_single_scalar(s2, s2, -0.5, ALU.is_ge)
                        D_.tensor_add(tmp, s0, s1)
                        D_.tensor_add(tmp, tmp, s2)
                        D_.tensor_scalar(tmp, tmp, -1.0, 2.0, ALU.mult, ALU.add)
                        D_.tensor_tensor(hh[0], hh[0], s0, ALU.mult)
                        D_.tensor_tensor(hh[1], hh[1], s1, ALU.mult)
                        D_.tensor_tensor(hh[2], hh[2], s2, ALU.mult)
                        eng.tensor_tensor(hh[3], hh[3], tmp, ALU.mult).then_inc(fin, 1)
                        DRAIN_DVE and eng.drain()

                    def vec_slot(v, g):
                        if g < NG:
                            htm = h_tmb[g % 2]
                            for tt in range(NT):
                                v.wait_ge(sA.h, a_sil[(g, tt)])
                                v.wait_ge(sP.h, pe_u[(g, tt)])
                                v.tensor_mul(htm[:, tt * GSZ:(tt + 1) * GSZ],
                                             silu_s[:, :], pD[:, :]).then_inc(sV.h, 1)
                                DRAIN_DVE and v.drain()
                                if sparsify and tt % 2 == 1 and tt < NT - 1:
                                    q = tt // 2
                                    v.wait_ge(sA.h, a_abs[(g, q)])
                                    sp_chain(v, g, q, habs, spm, sV.h)
                        if g >= 1:
                            gp = g - 1
                            hfm = h_fmb[gp % 2]
                            for tt in range(NT):
                                v.wait_ge(sP.h, pe_tr[(gp, tt)])
                                dst = hfm[:, :].rearrange("p (ic t) -> p ic t", t=TPC)[:, :, tt * 128:(tt + 1) * 128]
                                srcp = pE[:, :].rearrange("p (ic r) -> p ic r", r=128)
                                v.tensor_copy(dst, srcp).then_inc(sV.h, 1)
                                DRAIN_DVE and v.drain()
                            for q in range(DC):
                                bank = pA if q % 2 == 0 else pB
                                v.wait_ge(sP.h, pe_dn[(gp, 2 * q + 1)])
                                dst = xsum[:, q * TPC:(q + 1) * TPC]
                                if gp == 0:
                                    v.tensor_copy(dst, bank[:, :]).then_inc(sV.h, 1)
                                else:
                                    v.tensor_add(dst, dst, bank[:, :]).then_inc(sV.h, 1)
                                DRAIN_DVE and v.drain()
                        if g < NG and sparsify:
                            v.wait_ge(sA.h, a_abs[(g, 3)])
                            sp_chain(v, g, 3, habs, spm, sV.h)

                    for g in range(NG + 1):
                        plan_slot(g) if g < NG else None
                        if g == NG:
                            n = sP.n
                            for tt in range(NT):
                                n += 1; pe_tr[(NG - 1, tt)] = n
                            for idx in range(2 * DC):
                                n += 1; pe_dn[(NG - 1, idx)] = n
                            pe_group_end[NG - 1] = n
                            sP.n = n
                            n = sV.n
                            for tt in range(NT):
                                n += 1; v_trv[(NG - 1, tt)] = n
                            for q in range(DC):
                                n += 1; v_dnv[(NG - 1, q)] = n
                            sV.n = n
                        block.tensor(lambda t, g=g: ten_slot(t, g))
                        block.scalar(lambda a, g=g: sca_slot(Drained(a), g))
                        block.vector(lambda v, g=g: vec_slot(Drained(v), g))
                        if g + 1 < NG:
                            block.gpsimd(lambda e, g=g: load_gu(e, g + 1, pe_gu_end[g]))
                        if g < NG and g >= 1:
                            block.gpsimd(lambda e, g=g: load_d(
                                e, g, pe_group_end[g - 1]))
                    v_ffn = v_dnv[(NG - 1, DC - 1)]
                    _scF.__exit__(None, None, None)


                _scR = nc.named_scope(tag + "_ln2"); _scR.__enter__()
                vres = []
                for dc in range(DC):
                    def res_s(s, dc=dc, guard=(vres[dc - 2] if dc >= 2 else 0)):
                        if dc == 0:
                            s.wait_ge(sV.h, v_ffn)
                            s.wait_ge(sScr.h, scr_w)
                        if dc >= 2:
                            s.wait_ge(sV.h, guard)
                        sXr[dc % 2].inc(s.dma_start(out=xstr[dc % 2][:, :],
                                        in_=x1scr[dc * 128:(dc + 1) * 128, :]), 16)
                    block.sync(res_s)
                    n_ws = sXr[dc % 2].n

                    def res_v(v, dc=dc, n_ws=n_ws):
                        v.wait_ge(sXr[dc % 2].h, n_ws)
                        c = xsum[:, dc * TPC:(dc + 1) * TPC]
                        sV.inc(v.tensor_add(c, c, xstr[dc % 2][:, :]), 1)
                    VEC(res_v)
                    vres.append(sV.n)
                v_res = sV.n

                pe_bc2 = emit_ln(xsum, v_res)

                for dc in range(DC):
                    def ap2_v(v, dc=dc, pe_bc2=pe_bc2):
                        if dc == 0:
                            v.wait_ge(sP.h, pe_bc2)
                        c = xsum[:, dc * TPC:(dc + 1) * TPC]
                        v.tensor_sub(c, c, pA[:, :])
                        sV.inc(v.tensor_mul(c, c, pB[:, :]), 1)
                    VEC(ap2_v)
                    if out_mode == "x2T":
                        def o_s(s, dc=dc, v_n=sV.n):
                            s.wait_ge(sV.h, v_n)
                            sO.inc(s.dma_start(out=out[dc * 128:(dc + 1) * 128, :],
                                               in_=xsum[:, dc * TPC:(dc + 1) * TPC]), 16)
                        block.sync(o_s)
                    elif out_mode == "x2_cc":
                        def o_g(g, dc=dc, v_n=sV.n):
                            g.wait_ge(sV.h, v_n)
                            sO.inc(g.dma_start(out=out[dc * 128:(dc + 1) * 128, :],
                                               in_=xsum[:, dc * TPC:(dc + 1) * TPC]), 16)
                        block.gpsimd(o_g)
                v_x2 = sV.n

                if final_tm:
                    with nc.sbuf_tensor(tag + "ytile", [128, D], F32) as ytile:
                        v_yt = []
                        for tt in range(NT):
                            def ftr_t(t, tt=tt, v_x2=v_x2):
                                if tt == 0:
                                    t.wait_ge(sV.h, v_x2)
                                else:
                                    t.wait_ge(sV.h, v_yt[-1])
                                mmi = None
                                for dc in range(DC):
                                    bank = pA if dc < 8 else pB
                                    mmi = t.matmul(bank[:, (dc % 8) * 128:(dc % 8 + 1) * 128],
                                                   xsum[:, dc * TPC + tt * 128:dc * TPC + tt * 128 + 128],
                                                   identf[:, :], start=True, stop=True)
                                sP.inc(mmi, 1)
                            block.tensor(ftr_t)
                            pe_ftr = sP.n

                            def ftr_v(v, pe_ftr=pe_ftr, guard=sO.n):
                                v.wait_ge(sP.h, pe_ftr)
                                v.wait_ge(sO.h, guard)
                                v.tensor_copy(ytile[:, 0:1024], pA[:, :])
                                sV.inc(v.tensor_copy(ytile[:, 1024:2048], pB[:, :]), 1)
                            VEC(ftr_v)
                            v_yt.append(sV.n)

                            def ftr_s(s, tt=tt, v_n=sV.n):
                                s.wait_ge(sV.h, v_n)
                                sO.inc(s.dma_start(out=out[tt * 128:(tt + 1) * 128, :],
                                                   in_=ytile[:, :]), 16)
                            block.sync(ftr_s)
                _scR.__exit__(None, None, None)

    return {"pe": (sP.h, sP.n), "act": (sA.h, sA.n), "dve": (sV.h, sV.n),
            "out": (sO.h, sO.n)}



def build_p2(nblk=16):
    def io_builder(nc):
        x2T = nc.dram_tensor("x2T", [D, T], F32, kind="ExternalInput").ap()
        return dict(
            xblk_in=lambda grp, blk: x2T[512 * grp:512 * (grp + 1),
                                         blk * 512:(blk + 1) * 512].rearrange("(k p) t -> p k t", p=128),
            wq=nc.dram_tensor("wq", [128, DC * HPC * DK], F32, kind="ExternalInput").ap(),
            wk=nc.dram_tensor("wk", [128, DC * DK], F32, kind="ExternalInput").ap(),
            wv=nc.dram_tensor("wv", [128, DC * DK], F32, kind="ExternalInput").ap(),
            ropec=nc.dram_tensor("ropec", [128, T], F32, kind="ExternalInput").ap(),
            ropes=nc.dram_tensor("ropes", [128, T], F32, kind="ExternalInput").ap(),
            ident=nc.dram_tensor("ident", [128, 128], BF16, kind="ExternalInput").ap(),
            dmask=nc.dram_tensor("dmask", [128, 4 * 512], BF16, kind="ExternalInput").ap(),
            negc=nc.dram_tensor("negc", [128, 1], F32, kind="ExternalInput").ap(),
            att1=nc.dram_tensor("att1", [HPC * DK, T], BF16, kind="ExternalOutput").ap(),
            cc_wait=None,
        )
    return _standalone(lambda E, io, entry: emit_p2(E, io, entry, nblk=nblk), io_builder)


def emit_p2(E, io, entry, nblk=16, tag="p2"):
    nc, block, ctx = E.nc, E.block, E.ctx
    wq, wk, wv = io["wq"], io["wk"], io["wv"]
    cos_in, sin_in = io["ropec"], io["ropes"]
    ident_in, dmask_in, negc_in = io["ident"], io["dmask"], io["negc"]
    att1 = io["att1"]

    NQT = nblk * 4
    NQG = nblk

    with contextlib.ExitStack() as pctx:
        sb = lambda name, shape, dt: pctx.enter_context(nc.sbuf_tensor(tag + name, shape, dt))

        wq_sb = sb("wq_sb", [128, DC * HPC * DK], BF16)
        wk_sb = sb("wk_sb", [128, DC * DK], BF16)
        wv_sb = sb("wv_sb", [128, DC * DK], BF16)
        xblk = [sb(f"xblk{i}", [128, DC * 512], BF16) for i in range(2)]
        ropeb = [sb(f"ropeb{i}", [128, 1024], F32) for i in range(2)]
        ident = sb("ident_sb", [128, 128], BF16)
        dmask = sb("dmask_sb", [128, 4 * 512], BF16)
        negc = sb("negc_sb", [128, 1], F32)
        qsb = [sb(f"q{h}_sb", [128, T], BF16) for h in range(HPC)]
        ksb = sb("k_sb", [128, T], BF16)
        v_aug = sb("v_aug", [128, NQT * 129], BF16)
        vtmp = sb("vtmp", [128, 512], BF16)
        tr1 = sb("tr1", [128, 512], F32)
        tr2 = sb("tr2", [128, 512], F32)
        tr3 = sb("tr3", [128, 512], F32)
        ptile = [sb(f"ptile{i}", [128, 1024], BF16) for i in range(2)]
        ybf = sb("ybf", [128, 128], BF16)
        att_tile = sb("att_tile", [128, 512], BF16)
        rc = sb("rc", [128, 2], F32)

        pS = [pctx.enter_context(nc.psum_tensor(tag + f"pS{i}", [128, 1024], F32)) for i in range(2)]
        pY = [pctx.enter_context(nc.psum_tensor(tag + f"pY{i}", [128, 512], F32)) for i in range(4)]
        pQ, pV, pT = pY[0], pY[1], pY[2]

        sLw = E.sem(tag + "sLw")
        sXB = [E.sem(tag + "sXB0"), E.sem(tag + "sXB1")]
        sRC = [E.sem(tag + "sRC0"), E.sem(tag + "sRC1")]
        sWi = E.sem(tag + "sWi")
        sP = E.sem(tag + "sP")
        sA = E.sem(tag + "sA")
        sV = E.sem(tag + "sV")
        sO = E.sem(tag + "sO")

        def load_xblk(g, blk, buf):
            if io["cc_wait"] is not None and blk == 0:
                g.wait_ge(io["cc_wait"][0], io["cc_wait"][1])
            for grp in range(4):
                sXB[buf].inc(g.dma_start(
                    out=xblk[buf][:, :].rearrange("p (k t) -> p k t", t=512)[:, 4 * grp:4 * grp + 4, :],
                    in_=io["xblk_in"](grp, blk)), 16)

        if True:
            VEC, SCA = E.VEC, E.SCA

            XQ = block.sync if io.get("xblk_sync") else block.gpsimd

            def init_g(g):
                for se, va in entry:
                    g.wait_ge(se, va)
                sLw.inc(g.dma_start(out=wq_sb[:, :], in_=wq), 16)
                sLw.inc(g.dma_start(out=wk_sb[:, :], in_=wk), 16)
                sLw.inc(g.dma_start(out=wv_sb[:, :], in_=wv), 16)
            block.gpsimd(init_g)
            XQ(lambda e: load_xblk(e, 0, 0))

            def load_rope(s, blk):
                buf = blk % 2
                sRC[buf].inc(s.dma_start(out=ropeb[buf][:, 0:512],
                                         in_=cos_in[:, blk * 512:(blk + 1) * 512]), 16)
                sRC[buf].inc(s.dma_start(out=ropeb[buf][:, 512:1024],
                                         in_=sin_in[:, blk * 512:(blk + 1) * 512]), 16)

            def init_s(s):
                for se, va in entry:
                    s.wait_ge(se, va)
                load_rope(s, 0)
                load_rope(s, 1)
                sWi.inc(s.dma_start(out=ident[:, :], in_=ident_in), 16)
                sWi.inc(s.dma_start(out=dmask[:, :], in_=dmask_in), 16)
                sWi.inc(s.dma_start(out=negc[:, :], in_=negc_in), 16)
            block.sync(init_s)
            n_wi = sWi.n

            def init_v(v):
                for se, va in entry:
                    v.wait_ge(se, va)
                sV.inc(v.memset(v_aug[:, :], 1.0), 1)
            VEC(init_v)
            v_init = sV.n

            _sc1 = nc.named_scope(tag + "_proj"); _sc1.__enter__()
            last_pev = 0
            a_evac = 0
            v_rope = 0
            for blk in range(nblk):
                buf = blk % 2
                xv = xblk[buf][:, :]
                if blk + 1 < nblk:
                    def pre_g(g, blk=blk, guard=last_pev):
                        if blk >= 1:
                            g.wait_ge(sP.h, guard)
                        load_xblk(g, blk + 1, (blk + 1) % 2)
                    XQ(pre_g)

                for which in range(4):
                    def proj_t(t, blk=blk, which=which, xv=xv, g_a=a_evac, g_v=v_rope):
                        if which == 0:
                            t.wait_ge(sXB[blk % 2].h, 64 * (blk // 2 + 1))
                            if blk == 0:
                                t.wait_ge(sLw.h, sLw.n)
                                t.wait_ge(sWi.h, n_wi)
                                t.wait_ge(sV.h, v_init)
                        t.wait_ge(sA.h, g_a)
                        mmi = None
                        if which < 3:
                            wsb, col = ((wq_sb, which * DK) if which < 2 else (wk_sb, 0))
                            for dc in range(DC):
                                base = dc * (HPC * DK if which < 2 else DK) + col
                                mmi = t.matmul(pQ[:, :], wsb[:, base:base + DK],
                                               xv[:, dc * 512:(dc + 1) * 512],
                                               start=(dc == 0), stop=(dc == DC - 1))
                            sP.inc(mmi, 1)
                        else:
                            for dc in range(DC):
                                mmi = t.matmul(pV[:, :], wv_sb[:, dc * DK:dc * DK + DK],
                                               xv[:, dc * 512:(dc + 1) * 512],
                                               start=(dc == 0), stop=(dc == DC - 1))
                            sP.inc(mmi, 1)
                    block.tensor(proj_t)
                    pe_p = sP.n
                    if which == 3:
                        last_pev = pe_p

                    if which < 3:
                        def rope_a(a, pe_p=pe_p, g_v=v_rope):
                            a.wait_ge(sP.h, pe_p)
                            a.wait_ge(sV.h, g_v)
                            sA.inc(a.activation(out=tr1[:, :], in_=pQ[:, :], func=AF.Copy), 1)
                        SCA(rope_a)
                        a_evac = sA.n

                        def rope_v(v, blk=blk, which=which, a_n=sA.n):
                            v.wait_ge(sA.h, a_n)
                            if which == 0:
                                v.wait_ge(sRC[blk % 2].h, 32 * (blk // 2 + 1))
                            v.tensor_copy(tr2[0:64, :], tr1[64:128, :])
                            v.tensor_copy(tr2[64:128, :], tr1[0:64, :])
                            cs = ropeb[blk % 2][:, 0:512]
                            sn = ropeb[blk % 2][:, 512:1024]
                            v.tensor_mul(tr3[:, :], tr1[:, :], cs)
                            v.tensor_mul(tr2[:, :], tr2[:, :], sn)
                            dst = (qsb[which] if which < 2 else ksb)[:, blk * 512:(blk + 1) * 512]
                            sV.inc(v.tensor_add(dst, tr3[:, :], tr2[:, :]), 1)
                        VEC(rope_v)
                        v_rope = sV.n
                        if which == 2 and blk + 2 < nblk:
                            def rpre_s(s, blk=blk, guard=v_rope):
                                s.wait_ge(sV.h, guard)
                                load_rope(s, blk + 2)
                            block.sync(rpre_s)
                    if which == 3:
                        def v_a(a, pe_p=pe_p):
                            a.wait_ge(sP.h, pe_p)
                            sA.inc(a.activation(out=vtmp[:, :], in_=pV[:, :], func=AF.Copy), 1)
                        SCA(v_a)
                        a_evac = sA.n

                        def vt_t(t, a_n=sA.n, g_v=v_rope):
                            t.wait_ge(sA.h, a_n)
                            t.wait_ge(sV.h, g_v)
                            mmi = None
                            for j in range(4):
                                mmi = t.matmul(pT[:, j * 128:(j + 1) * 128],
                                               vtmp[:, j * 128:(j + 1) * 128], ident[:, :],
                                               start=True, stop=True)
                            sP.inc(mmi, 1)
                        block.tensor(vt_t)
                        pe_vt = sP.n

                        def vt_v(v, blk=blk, pe_vt=pe_vt):
                            v.wait_ge(sP.h, pe_vt)
                            dst = v_aug[:, :].rearrange("p (n c) -> p n c", c=129)[:, 4 * blk:4 * blk + 4, 0:128]
                            src = pT[:, :].rearrange("p (n c) -> p n c", c=128)
                            sV.inc(v.tensor_copy(dst, src), 1)
                        VEC(vt_v)
                        v_rope = sV.n
            v_stage1 = sV.n
            pe_stage1 = sP.n
            _sc1.__exit__(None, None, None)
            _sc2 = nc.named_scope(tag + "_attn"); _sc2.__enter__()

            ISQ = ISQRT_DK
            v_att = 0
            prev_av_last = 0
            for h in range(HPC):
                for qg in range(NQG):
                    nkb = 4 * qg + 4
                    npair = 2 * qg + 2

                    pe_ops = [("st", 0)]
                    for p in range(npair):
                        if p + 1 < npair:
                            pe_ops.append(("st", p + 1))
                        pe_ops.append(("av", p))
                    pe_st = {}
                    av_done = {}
                    n = sP.n
                    for kind, p in pe_ops:
                        n += 1
                        (pe_st if kind == "st" else av_done)[p] = n
                    pe_ytr = [n + 1 + j for j in range(4)]
                    sP.n = n + 4
                    aexp = {p: sA.n + p + 1 for p in range(npair)}
                    sA.n += npair
                    vmask = {npair - 2: sV.n + 1, npair - 1: sV.n + 2}
                    v_ye = [sV.n + 3 + j for j in range(4)]
                    v_att_new = sV.n + 7
                    sV.n += 7
                    o_prev = sO.n

                    def ten_body(t, h=h, qg=qg, nkb=nkb, npair=npair,
                                 pe_ops=list(pe_ops), aexp=dict(aexp),
                                 vmask=dict(vmask), v_ye=list(v_ye),
                                 v_att=v_att, first=((h, qg) == (0, 0))):
                        qslice = qsb[h][:, qg * 512:(qg + 1) * 512]
                        for kind, p in pe_ops:
                            if kind == "st":
                                if first and p < 2:
                                    t.wait_ge(sP.h, pe_stage1)
                                    t.wait_ge(sV.h, v_stage1)
                                if p >= 2:
                                    t.wait_ge(sA.h, aexp[p - 2])
                                elif v_att:
                                    t.wait_ge(sV.h, v_att)
                                mmi = None
                                for kb in (2 * p, 2 * p + 1):
                                    mmi = t.matmul(
                                        pS[p % 2][:, (kb % 2) * 512:(kb % 2) * 512 + 512],
                                        ksb[:, kb * 128:(kb + 1) * 128], qslice,
                                        start=True, stop=True)
                                mmi.then_inc(sP.h, 1)
                            else:
                                t.wait_ge(sA.h, aexp[p])
                                if p >= npair - 2:
                                    t.wait_ge(sV.h, vmask[p])
                                mmi = None
                                for kb in (2 * p, 2 * p + 1):
                                    for j in range(4):
                                        mmi = t.matmul(
                                            pY[j][:, 0:129],
                                            ptile[p % 2][:, (kb % 2) * 512 + j * 128:(kb % 2) * 512 + j * 128 + 128],
                                            v_aug[:, kb * 129:(kb + 1) * 129],
                                            start=(kb == 0), stop=(kb == nkb - 1))
                                mmi.then_inc(sP.h, 1)
                        for j in range(4):
                            t.wait_ge(sV.h, v_ye[j])
                            t.matmul(pS[1][:, j * 128:(j + 1) * 128], ybf[:, :],
                                     ident[:, :], start=True, stop=True).then_inc(sP.h, 1)
                    block.tensor(ten_body)

                    def sca_body(a, npair=npair, pe_st=dict(pe_st),
                                 av_done=dict(av_done), prev_av_last=prev_av_last):
                        for p in range(npair):
                            a.wait_ge(sP.h, pe_st[p])
                            guard = av_done[p - 2] if p >= 2 else prev_av_last
                            if guard:
                                a.wait_ge(sP.h, guard)
                            a.activation(out=ptile[p % 2][:, :], in_=pS[p % 2][:, :],
                                         func=AF.Exp, scale=ISQ,
                                         bias=negc[:, :]).then_inc(sA.h, 1)
                            DRAIN_ACT and a.drain()
                    block.scalar(lambda a: sca_body(Drained(a)))
                    sca_drains = npair

                    def vec_body(v, h=h, qg=qg, npair=npair, aexp=dict(aexp),
                                 av_last=av_done[npair - 1], pe_ytr=list(pe_ytr),
                                 o_prev=o_prev):
                        for p in (npair - 2, npair - 1):
                            v.wait_ge(sA.h, aexp[p])
                            inst = None
                            for kb in (2 * p, 2 * p + 1):
                                r = kb - 4 * qg
                                if 0 <= r < 4:
                                    inst = v.tensor_mul(
                                        ptile[p % 2][:, (kb % 2) * 512:(kb % 2) * 512 + 512],
                                        ptile[p % 2][:, (kb % 2) * 512:(kb % 2) * 512 + 512],
                                        dmask[:, r * 512:(r + 1) * 512])
                                    DRAIN_DVE and v.drain()
                            inst.then_inc(sV.h, 1)
                        for j in range(4):
                            v.wait_ge(sP.h, av_last)
                            if j > 0:
                                v.wait_ge(sP.h, pe_ytr[j - 1])
                            v.reciprocal(rc[:, 0:1], pY[j][:, 128:129])
                            DRAIN_DVE and v.drain()
                            v.tensor_scalar(ybf[:, :], pY[j][:, 0:128], rc[:, 0:1],
                                            None, ALU.mult).then_inc(sV.h, 1)
                            DRAIN_DVE and v.drain()
                        v.wait_ge(sP.h, pe_ytr[3])
                        v.wait_ge(sO.h, o_prev)
                        v.tensor_copy(att_tile[:, :], pS[1][:, 0:512]).then_inc(sV.h, 1)
                        DRAIN_DVE and v.drain()
                    block.vector(vec_body)

                    def out_s(s, h=h, qg=qg, v_n=v_att_new):
                        s.wait_ge(sV.h, v_n)
                        sO.inc(s.dma_start(
                            out=att1[h * DK:(h + 1) * DK, qg * 512:(qg + 1) * 512],
                            in_=att_tile[:, :]), 16)
                    block.sync(out_s)

                    v_att = v_att_new
                    prev_av_last = av_done[npair - 1]
            _sc2.__exit__(None, None, None)

    return {"pe": (sP.h, sP.n), "act": (sA.h, sA.n), "dve": (sV.h, sV.n),
            "out": (sO.h, sO.n)}



def build_fused():
    nc = bass.Bass("TRN2", target_bir_lowering=False, debug=False)
    I = lambda n, shp, dt=F32: nc.dram_tensor(n, shp, dt, kind="ExternalInput").ap()
    x0s = I("x0s", [128, NSEG * DC * SEG])
    wq0, wk0, wv0 = I("wq0", [128, DC * 256]), I("wk0", [128, DC * 256]), I("wv0", [128, DC * 256])
    betas = I("betas", [1, 256])
    wo0, wg0 = I("wo0", [128, DC * D]), I("wg0", [128, NG * DC * GSZ])
    wu0, wd0 = I("wu0", [128, NG * DC * GSZ]), I("wd0", [128, NG * GC * D])
    wq1, wk1, wv1 = I("wq1", [128, DC * 256]), I("wk1", [128, DC * DK]), I("wv1", [128, DC * DK])
    ropec, ropes = I("ropec", [128, T]), I("ropes", [128, T])
    wo1, wg1 = I("wo1", [128, DC * D]), I("wg1", [128, NG * DC * GSZ])
    wu1, wd1 = I("wu1", [128, NG * DC * GSZ]), I("wd1", [128, NG * GC * D])
    xres0 = I("xres", [D, TPC])
    ident = I("ident", [128, 128], BF16)
    identf = I("identf", [128, 128], F32)
    binmask = I("binmask", [SEG, SEG], BF16)
    dmask = I("dmask", [128, 4 * 512], BF16)
    negc = I("negc", [128, 1], F32)
    eps = I("epsrow", [1, 1], F32)
    yout = nc.dram_tensor("yout", [TPC, D], F32, kind="ExternalOutput").ap()

    cc_a0_in = nc.dram_tensor("cc_a0_in", [256, T], BF16).ap()
    cc_a0_out = nc.dram_tensor("cc_a0_out", [D, T], BF16, addr_space="Shared").ap()
    cc_x2_in = nc.dram_tensor("cc_x2_in", [D, TPC], BF16).ap()
    cc_x2_out = nc.dram_tensor("cc_x2_out", [N_CORES * D, TPC], BF16, addr_space="Shared").ap()
    cc_a1_in = nc.dram_tensor("cc_a1_in", [256, T], BF16).ap()
    cc_a1_out = nc.dram_tensor("cc_a1_out", [D, T], BF16, addr_space="Shared").ap()
    x1scr = nc.dram_tensor("x1scr", [D, TPC], F32).ap()
    x3scr = nc.dram_tensor("x3scr", [D, TPC], F32).ap()

    RG = [list(range(N_CORES))]
    ctx = contextlib.ExitStack()
    with ctx:
        xsum = ctx.enter_context(nc.sbuf_tensor("xsum_shared", [128, DC * TPC], F32))
        block = ctx.enter_context(nc.Block())
        E = Env(nc, block, ctx)
        sCC = E.sem("sCC")

        with nc.named_scope("P0"):
            ex0 = emit_p0(E, dict(x0s=x0s, wq=wq0, wk=wk0, wv=wv0, betas=betas,
                                  ident=ident, binmask=binmask, negc=negc,
                                  att0=cc_a0_in), [], tag="p0")

        def cc1_g(g):
            g.wait_ge(*ex0["out"])
            sCC.inc(g.collective_compute(
                "AllGather", ALU.bypass, replica_groups=RG,
                ins=[cc_a0_in.opt()], outs=[cc_a0_out.opt()]), 1)
        block.gpsimd(cc1_g)

        entry1 = [ex0["pe"], ex0["act"], ex0["dve"], ex0["out"]]
        io1 = dict(attc_mode="gather_scramble", attc_g=cc_a0_out,
                   cc_wait=(sCC.h, 1),
                   xres_chunk=lambda s, dc: xres0[dc * 128:(dc + 1) * 128, :],
                   wo=wo0, wg=wg0, wu=wu0, wd=wd0,
                   ident=ident, identf=identf, eps=eps, x1scr=x1scr,
                   xsum_t=xsum, out_mode="x2_cc", out=cc_x2_in)
        with nc.named_scope("P1"):
            ex1 = emit_p1(E, io1, entry1, sparsify=True, final_tm=False, tag="p1")

        def cc2_g(g):
            g.wait_ge(*ex1["out"])
            sCC.inc(g.collective_compute(
                "AllGather", ALU.bypass, replica_groups=RG,
                ins=[cc_x2_in.opt()], outs=[cc_x2_out.opt()]), 1)
        block.gpsimd(cc2_g)

        entry2 = [ex1["pe"], ex1["act"], ex1["dve"], ex1["out"]]
        x2g = cc_x2_out.rearrange("(r d) t -> r d t", d=D)
        io2 = dict(
            xblk_in=lambda grp, blk: x2g[blk // 2, 512 * grp:512 * (grp + 1),
                                         (blk % 2) * 512:(blk % 2) * 512 + 512
                                         ].rearrange("(k p) t -> p k t", p=128),
            wq=wq1, wk=wk1, wv=wv1, ropec=ropec, ropes=ropes,
            ident=ident, dmask=dmask, negc=negc, att1=cc_a1_in,
            cc_wait=(sCC.h, 2), xblk_sync=True)
        with nc.named_scope("P2"):
            ex2 = emit_p2(E, io2, entry2, tag="p2")

        def cc3_g(g):
            g.wait_ge(*ex2["out"])
            sCC.inc(g.collective_compute(
                "AllGather", ALU.bypass, replica_groups=RG,
                ins=[cc_a1_in.opt()], outs=[cc_a1_out.opt()]), 1)
        block.gpsimd(cc3_g)

        entry3 = [ex2["pe"], ex2["act"], ex2["dve"], ex2["out"]]
        io3 = dict(attc_mode="gather", attc_g=cc_a1_out, cc_wait=(sCC.h, 3),
                   xres_chunk=None,
                   wo=wo1, wg=wg1, wu=wu1, wd=wd1,
                   ident=ident, identf=identf, eps=eps, x1scr=x3scr,
                   xsum_t=xsum, out_mode="yout", out=yout)
        with nc.named_scope("P3"):
            ex3 = emit_p1(E, io3, entry3, sparsify=False, final_tm=True, tag="p3")

        osem, oval = ex3["out"]
        def fin_s(s):
            s.wait_ge(osem, oval)
        block.sync(fin_s)
    return nc




def _prearr_dmat(w):
    Dd, F = w.shape
    K = Dd // 128
    return np.ascontiguousarray(w.reshape(K, 128, F).transpose(1, 0, 2).reshape(128, K * F))


def _prearr_ffn(w):
    return np.ascontiguousarray(
        w.reshape(DC, 128, NG, GSZ).transpose(1, 2, 0, 3).reshape(128, NG * DC * GSZ))


def _prearr_wd(w):
    return np.ascontiguousarray(
        w.reshape(NG, GC, 128, D).transpose(2, 0, 1, 3).reshape(128, NG * GC * D))


def _prearr_x0(x0):
    return np.ascontiguousarray(
        x0.reshape(NSEG, SEG, DC, 128).transpose(3, 0, 2, 1).reshape(128, NSEG * DC * SEG))


def _scramble_fm(att_fm):
    A = att_fm.reshape(16, 128, NSEG, 32, 16)
    C = A.transpose(4, 1, 2, 0, 3)
    return np.ascontiguousarray(C.reshape(D, T))


def _p2_consts():
    inv = np.power(10000.0, -2.0 * np.arange(64) / 128.0)
    ang = np.arange(T)[:, None].astype(np.float64) * inv[None, :]
    cosT = np.ascontiguousarray(np.tile(np.cos(ang), (1, 2)).T.astype(np.float32))
    sinT = np.tile(np.sin(ang), (1, 2)).T.astype(np.float32)
    sinT[:64, :] *= -1.0
    sinT = np.ascontiguousarray(sinT)
    p = np.arange(128)[:, None]
    c = np.arange(512)[None, :]
    dmask = np.zeros((128, 4 * 512), BF)
    for r in range(4):
        dmask[:, r * 512:(r + 1) * 512] = (c >= r * 128 + p).astype(BF)
    negc = np.full((128, 1), -CSHIFT, np.float32)
    return cosT, sinT, dmask, negc


def kernel(L0_attn_x, L0_attn_Wq, L0_attn_Wk, L0_attn_Wv, L0_attn_Wo, L0_attn_betas,
           L0_ffn_w_gate, L0_ffn_w_up, L0_ffn_w_down,
           L1_attn_Wq, L1_attn_Wk, L1_attn_Wv, L1_attn_Wo,
           L1_ffn_w_gate, L1_ffn_w_up, L1_ffn_w_down):
    cores = list(range(N_CORES))
    x0T = np.ascontiguousarray(np.asarray(L0_attn_x)[0].T)
    ident = np.eye(128, dtype=BF)
    identf = np.eye(128, dtype=np.float32)
    eps = np.full((1, 1), LN_EPS, np.float32)
    _, binmask, negc = _consts_p0()
    cosT, sinT, dmask, _ = _p2_consts()
    wg1 = np.ascontiguousarray(L1_ffn_w_gate.T)
    wu1 = np.ascontiguousarray(L1_ffn_w_up.T)
    wd1 = np.ascontiguousarray(L1_ffn_w_down.T)
    betas16 = L0_attn_betas.reshape(16, DK)
    x0s = _prearr_x0(np.asarray(L0_attn_x)[0])
    wo0p, wo1p = _prearr_dmat(L0_attn_Wo), _prearr_dmat(L1_attn_Wo)
    wg0p, wu0p = _prearr_ffn(L0_ffn_w_gate), _prearr_ffn(L0_ffn_w_up)
    wd0p = _prearr_wd(L0_ffn_w_down)
    wg1p, wu1p, wd1p = _prearr_ffn(wg1), _prearr_ffn(wu1), _prearr_wd(wd1)
    maps = []
    for c in cores:
        sl = slice(c * TPC, (c + 1) * TPC)
        hs = slice(c * 256, (c + 1) * 256)
        kv = slice((c // 2) * 128, (c // 2 + 1) * 128)
        maps.append({
            "x0s": x0s,
            "xres": np.ascontiguousarray(x0T[:, sl]),
            "wq0": _prearr_dmat(L0_attn_Wq[:, hs]),
            "wk0": _prearr_dmat(L0_attn_Wk[:, hs]),
            "wv0": _prearr_dmat(L0_attn_Wv[:, hs]),
            "betas": np.ascontiguousarray(betas16[2 * c:2 * c + 2, :].reshape(1, 256)),
            "wo0": wo0p, "wg0": wg0p, "wu0": wu0p, "wd0": wd0p,
            "wq1": _prearr_dmat(L1_attn_Wq[:, hs]),
            "wk1": _prearr_dmat(L1_attn_Wk[:, kv]),
            "wv1": _prearr_dmat(L1_attn_Wv[:, kv]),
            "ropec": cosT, "ropes": sinT,
            "wo1": wo1p, "wg1": wg1p, "wu1": wu1p, "wd1": wd1p,
            "ident": ident, "identf": identf, "binmask": binmask,
            "dmask": dmask, "negc": negc, "epsrow": eps,
        })
    res = run_bass_kernel_spmd(build_fused(), maps, core_ids=cores)
    y = np.concatenate([r["yout"] for r in res.results], axis=0)
    return y.reshape(1, T, D)


def _kernel_unfused(L0_attn_x, L0_attn_Wq, L0_attn_Wk, L0_attn_Wv, L0_attn_Wo, L0_attn_betas,
           L0_ffn_w_gate, L0_ffn_w_up, L0_ffn_w_down,
           L1_attn_Wq, L1_attn_Wk, L1_attn_Wv, L1_attn_Wo,
           L1_ffn_w_gate, L1_ffn_w_up, L1_ffn_w_down):
    cores = list(range(N_CORES))
    x0T = np.ascontiguousarray(np.asarray(L0_attn_x)[0].T)
    ident = np.eye(128, dtype=BF)
    identf = np.eye(128, dtype=np.float32)
    eps = np.full((1, 1), LN_EPS, np.float32)

    res0 = run_bass_kernel_spmd(
        build_p0(), _p0_in_maps(x0T, L0_attn_Wq, L0_attn_Wk, L0_attn_Wv, L0_attn_betas),
        core_ids=cores)
    att0 = np.concatenate([r["att0"] for r in res0.results], axis=0)
    att0s = _scramble_fm(att0)

    maps1 = []
    for c in cores:
        sl = slice(c * TPC, (c + 1) * TPC)
        maps1.append({
            "attc": np.ascontiguousarray(att0s[:, sl]),
            "xres": np.ascontiguousarray(x0T[:, sl]),
            "wo": L0_attn_Wo, "wg": L0_ffn_w_gate, "wu": L0_ffn_w_up,
            "wd": L0_ffn_w_down,
            "ident": ident, "identf": identf, "epsrow": eps,
        })
    res1 = run_bass_kernel_spmd(build_p1(True, False), maps1, core_ids=cores)
    x2T = np.concatenate([r["x2T"] for r in res1.results], axis=1)

    cosT, sinT, dmask, negc = _p2_consts()
    maps2 = []
    for c in cores:
        kv = c // 2
        maps2.append({
            "x2T": x2T,
            "wq": np.ascontiguousarray(L1_attn_Wq[:, c * 256:(c + 1) * 256]),
            "wk": np.ascontiguousarray(L1_attn_Wk[:, kv * 128:(kv + 1) * 128]),
            "wv": np.ascontiguousarray(L1_attn_Wv[:, kv * 128:(kv + 1) * 128]),
            "ropec": cosT, "ropes": sinT,
            "ident": ident, "dmask": dmask, "negc": negc,
        })
    res2 = run_bass_kernel_spmd(build_p2(), maps2, core_ids=cores)
    att1 = np.concatenate([r["att1"] for r in res2.results], axis=0)

    wg1 = np.ascontiguousarray(L1_ffn_w_gate.T)
    wu1 = np.ascontiguousarray(L1_ffn_w_up.T)
    wd1 = np.ascontiguousarray(L1_ffn_w_down.T)
    maps3 = []
    for c in cores:
        sl = slice(c * TPC, (c + 1) * TPC)
        maps3.append({
            "attc": np.ascontiguousarray(att1[:, sl]),
            "xres": np.ascontiguousarray(x2T[:, sl]),
            "wo": L1_attn_Wo, "wg": wg1, "wu": wu1, "wd": wd1,
            "ident": ident, "identf": identf, "epsrow": eps,
        })
    res3 = run_bass_kernel_spmd(build_p1(False, True), maps3, core_ids=cores)
    y = np.concatenate([r["yout"] for r in res3.results], axis=0)
    return y.reshape(1, T, D)
